# version 06
# baseline (speedup 1.0000x reference)
import numpy as np
import ml_dtypes
import traceback
from contextlib import ExitStack

_DBG_ORIGIN = True

import concourse.bass as bass
import concourse.mybir as mybir
from concourse.bass_utils import run_bass_kernel_spmd

F32 = mybir.dt.float32
BF16 = mybir.dt.bfloat16
FP8 = mybir.dt.float8e4
I32 = mybir.dt.int32
AF = mybir.ActivationFunctionType
ALU = mybir.AluOpType
AX = mybir.AxisListType.X

B, N, D = 4, 8192, 1024
C = 128
TW = 512
NT = N // TW
CPT = TW // C
NOWN = N // 2
NOT = NOWN // TW
IF = 4096
C32 = 12582912.0
C64 = 6755399441055744.0
EPS = 1e-5

ENGS = ("t", "v", "s", "g", "y")
QUEUES = ("yd", "sd", "gd")


class Prog:

    def __init__(self, nc):
        self.nc = nc
        self.items = {e: [] for e in ENGS}
        self.cnt = {k: 0 for k in ENGS}
        self.wm = {e: {} for e in ENGS}
        self.sem = {}
        self.waited = {e: set() for e in ENGS}

    def make_sems(self, st):
        self._semstack = st
        for k in ENGS:
            self.sem[k] = st.enter_context(self.nc.semaphore("sem_" + k))

    def _stream_sem(self, stream):
        if stream not in self.sem:
            self.sem[stream] = self._semstack.enter_context(
                self.nc.semaphore("sem_" + stream))
            self.cnt[stream] = 0
        return self.sem[stream]

    def _need(self, eng, waits):
        wm = self.wm[eng]
        best = {}
        for tok in waits:
            if tok is None:
                continue
            s, v = tok
            if v is None:
                continue
            if v > best.get(s, -1):
                best[s] = v
        need = []
        for s, v in best.items():
            if wm.get(s, 0) >= v:
                continue
            wm[s] = v
            need.append((s, v))
            if s in ENGS:
                self.waited[s].add(v)
        return need

    def _add(self, eng, need, val, fn):
        self.items[eng].append((need, val, fn))

    def do(self, eng, op, *args, waits=(), inc=True, **kw):
        need = self._need(eng, waits)
        self.cnt[eng] += 1
        val = self.cnt[eng]

        def fn(e, op=op, args=args, kw=kw):
            return getattr(e, op)(*args, **kw)
        self._add(eng, need, val, fn)
        return (eng, val)

    def raw(self, eng, fn, waits=(), inc=True):
        need = self._need(eng, waits)
        self.cnt[eng] += 1
        val = self.cnt[eng]
        self._add(eng, need, val, fn)
        return (eng, val)

    def dma(self, queue, stream, out, in_, waits=()):
        eng = {"yd": "y", "sd": "s", "gd": "g"}[queue]
        self._stream_sem(stream)
        need = self._need(eng, waits)
        self.cnt[stream] += 16
        val = self.cnt[stream]
        sems = self.sem

        def fn(e, out=out, in_=in_, stream=stream):
            h = e.dma_start(out=out, in_=in_)
            h.then_inc(sems[stream], 16)
            return None
        self._add(eng, need, None, fn)
        return (stream, val)

    def dma_reg(self, queue, stream, fn0, waits=()):
        eng = {"yd": "y", "sd": "s", "gd": "g"}[queue]
        self._stream_sem(stream)
        need = self._need(eng, waits)
        self.cnt[stream] += 16
        sems = self.sem

        def fn(e, fn0=fn0, stream=stream):
            fn0(e).then_inc(sems[stream], 16)
            return None
        self._add(eng, need, None, fn)
        return (stream, self.cnt[stream])

    def fence(self, eng, waits):
        need = self._need(eng, waits)
        if not need:
            return
        self._add(eng, need, None, None)

    def emit(self, block):
        rank = {}
        for e in ENGS:
            rank[e] = {v: i + 1 for i, v in enumerate(sorted(self.waited[e]))}
        sems = self.sem

        def mkbody(name, items):
            r_all = rank
            waited = self.waited[name]
            rk = rank[name]

            def body(e):
                for need, val, fn in items:
                    for s, v in need:
                        if s in ENGS:
                            e.wait_ge(sems[s], r_all[s][v])
                        else:
                            e.wait_ge(sems[s], v)
                    if fn is None:
                        continue
                    h = fn(e)
                    if h is not None and val is not None and val in waited:
                        h.then_inc(sems[name], 1)
            return body
        for name, attr in (("t", "tensor"), ("v", "vector"), ("s", "scalar"),
                           ("g", "gpsimd"), ("y", "sync")):
            getattr(block, attr)(mkbody(name, self.items[name]))


def W(*toks):
    return [t for t in toks if t is not None]


def build_program(variant=4):
    nc = bass.Bass("TRN2", target_bir_lowering=False)

    def din(name, shape, dt=F32):
        return nc.dram_tensor(name, shape, dt, kind="ExternalInput").ap()

    x_tm = din("x_tm", [N, D])
    x_own = din("x_own", [NOWN, D])
    xT = din("xT", [D, N])
    wq_d = din("wq", [D, 512])
    wk_d = din("wk", [D, 512])
    wv_d = din("wv", [D, 512])
    wb_d = din("wb", [D, 2])
    wo_d = din("wo", [D, D])
    wgT_d = din("wgT", [D, IF])
    wuT_d = din("wuT", [D, IF])
    wdT_d = din("wdT", [IF, D])
    wmod_d = din("wmod", [D, 6 * D])
    cvec_d = din("cvec", [128, 8])
    gnorm_d = din("gnorm", [128, 8])
    tokbase_d = din("tok_base", [1, 1], I32)
    out_d = nc.dram_tensor("out", [NOWN, D], F32, kind="ExternalOutput").ap()
    coll_in = nc.dram_tensor("coll_in", [512, N], BF16, kind="Internal",
                             addr_space="Local").ap()
    coll_out = nc.dram_tensor("coll_out", [2, 512, N], BF16, kind="Internal",
                              addr_space="Local").ap()
    wg8_d = nc.dram_tensor("wg8_d", [128, 8, IF], FP8, kind="Internal",
                           addr_space="Local").ap()
    wu8_d = nc.dram_tensor("wu8_d", [128, 8, IF], FP8, kind="Internal",
                           addr_space="Local").ap()

    p = Prog(nc)
    with ExitStack() as stA:
        p.make_sems(stA)

        def sbA(name, shape, dt=F32):
            return stA.enter_context(nc.sbuf_tensor(name, shape, dt))

        Wq = sbA("Wq", [128, 8, 512], BF16)
        Wk = sbA("Wk", [128, 8, 512], BF16)
        Wv = sbA("Wv", [128, 8, 512], BF16)
        Wbt = sbA("Wbt", [128, 8, 2], BF16)
        Wo = sbA("Wo", [128, 8, 1024], BF16)
        Wd8 = sbA("Wd8", [128, 32, 1024], FP8)
        qaug = sbA("qaug", [1, 1024], BF16)
        kaug = sbA("kaug", [1, 1024], BF16)
        vaug = sbA("vaug", [1, 1024], BF16)
        baug = sbA("baug", [1, 4], BF16)
        eyeb = sbA("eyeb", [128, 128], BF16)
        eyef = sbA("eyef", [128, 128], F32)
        mask_sl = sbA("mask_sl", [128, 128], BF16)
        mask_ule = sbA("mask_ule", [128, 128], BF16)
        ones_c = sbA("ones_c", [128, 1], BF16)
        ones_cf = sbA("ones_cf", [128, 1], F32)
        nones_c = sbA("nones_c", [128, 1], BF16)
        ones_rf = sbA("ones_rf", [1, 128], F32)
        gdisc = sbA("gdisc", [128, 1], F32)
        g2bc = sbA("g2bc", [128, D], F32)
        s2bc = sbA("s2bc", [128, D], F32)
        gbbc = sbA("gbbc", [128, D], F32)
        gabc = sbA("gabc", [128, D], F32)
        gncol = sbA("gncol", [128, 8], F32)
        sagcol = sbA("sagcol", [128, 8], BF16)
        g1col = sbA("g1col", [128, 8], F32)
        wsc = sbA("wsc", [128, 8], F32)
        nthr = sbA("nthr", [128, 4], F32)
        tb_s = sbA("tb_s", [1, 1], I32)
        cv = sbA("cv", [128, 8], F32)
        silc = sbA("silc", [128, 8], BF16)
        eps_c = sbA("eps_c", [128, 1], F32)
        gatescr = sbA("gatescr", [1, 1], F32)

        pb = [stA.enter_context(nc.psum_tensor(f"pb{i}", [128, 512], F32))
              for i in range(8)]

        def MM(bank, sl, lhsT, rhs, start, stop, waits=(), inc=True):
            return p.do("t", "matmul", pb[bank][:, sl], lhsT, rhs, waits=waits,
                        start=start, stop=stop, inc=inc)

        def MMr(bank, psl, sl, lhsT, rhs, start, stop, waits=(), inc=True):
            return p.do("t", "matmul", pb[bank][psl, sl], lhsT, rhs, waits=waits,
                        start=start, stop=stop, inc=inc)

        def TRb(bank, bsl, in_, waits=(), inc=True):
            return p.do("t", "transpose", pb[bank].bitcast(BF16)[:, bsl], in_,
                        eyeb[:], waits=waits, inc=inc)

        stI = ExitStack()
        ri = stI.enter_context(nc.sbuf_tensor("ri", [128, 128], I32))
        ci = stI.enter_context(nc.sbuf_tensor("ci", [128, 128], I32))
        tio = p.do("g", "iota", ri[:], [[0, 128]], channel_multiplier=1)
        tio = p.do("g", "iota", ci[:], [[1, 128]], channel_multiplier=0)
        EYEB = p.do("v", "tensor_tensor", eyeb[:], ri[:], ci[:], ALU.is_equal,
                    waits=[tio])
        EYEF = p.do("v", "tensor_tensor", eyef[:], ri[:], ci[:], ALU.is_equal)
        MSL = p.do("v", "tensor_tensor", mask_sl[:], ri[:], ci[:], ALU.is_gt)
        MULE = p.do("v", "tensor_tensor", mask_ule[:], ri[:], ci[:], ALU.is_le)
        ONESC = p.do("v", "memset", ones_c[:], 1.0)
        ONESCF = p.do("v", "memset", ones_cf[:], 1.0)
        NONESC = p.do("v", "memset", nones_c[:], -1.0)
        ONESRF = p.do("v", "memset", ones_rf[:], 1.0)
        tg = p.do("v", "memset", gdisc[:], 0.5)
        tg = p.do("v", "tensor_scalar", gdisc[:], gdisc[:], C32, -C32, ALU.add,
                  ALU.add, waits=[tg])
        GDISC = p.do("v", "tensor_scalar", gdisc[:], gdisc[:], -2.0, 1.0, ALU.mult,
                     ALU.add, waits=[tg])
        stI.close()
        EPSC = p.do("v", "memset", eps_c[:], EPS)
        d_tb = p.dma("yd", "tbq", tb_s[:], tokbase_d)
        d_cv = p.dma("yd", "cvq", cv[:], cvec_d)
        t_sc = p.do("s", "activation", cv[:], cv[:], AF.Silu, waits=[d_cv])
        t_scb = p.do("v", "tensor_copy", silc[:], cv[:], waits=[t_sc])
        gn = p.dma("yd", "gnq", gncol[:], gnorm_d)

        b7busy = [None]

        with ExitStack() as stP:
            def sbP(name, shape, dt=F32):
                return stP.enter_context(nc.sbuf_tensor(name, shape, dt))

            modrow = sbP("modrow", [1, 6 * D], F32)
            garow = sbP("garow", [1, D], F32)
            gbrow = sbP("gbrow", [1, D], F32)
            magrow = sbP("magrow", [1, D], F32)
            g1row = sbP("g1row", [1, D], F32)
            g2row = sbP("g2row", [1, D], F32)
            sagrow = sbP("sagrow", [1, D], F32)
            sagf = sbP("sagf", [128, 8], F32)
            wmsb = [sbP(f"wmsb{i}", [128, 8, 512], BF16) for i in range(2)]
            wstb = [sbP(f"wstb{i}", [128, 8, 512], BF16) for i in range(2)]
            wotf = [sbP(f"wotf{i}", [128, 1024], F32) for i in range(2)]
            wbstb = sbP("wbstb", [128, 8, 2], BF16)

            wm_free = [None, None]
            last_ev = None
            for nb in range(12):
                pr = nb % 2
                src = wmod_d.rearrange("(c p) m -> p c m", p=128)[:, :, nb * 512:(nb + 1) * 512]
                d = p.dma("gd", f"wmod{pr}", wmsb[pr][:], src, waits=W(wm_free[pr]))
                m = None
                for kc in range(8):
                    m = MMr(7, slice(0, 1), slice(0, 512), silc[:, kc:kc + 1],
                            wmsb[pr][:, kc, :], start=(kc == 0), stop=(kc == 7),
                            waits=W(t_scb, d, b7busy[0]) if kc == 0 else ())
                last_ev = p.do("s", "copy", modrow[0:1, nb * 512:(nb + 1) * 512],
                               pb[7][0:1, 0:512], waits=[m])
                b7busy[0] = last_ev
                wm_free[pr] = m
            MOD = last_ev

            def mrow(k):
                return modrow[0:1, k * D:(k + 1) * D]

            def clamp_gate(k, dst, after):
                t1 = p.do("v", "tensor_scalar", dst, mrow(k), 0.0, None, ALU.is_ge,
                          waits=W(after))
                t1 = p.do("v", "tensor_scalar", dst, dst, 2.0, -1.0, ALU.mult,
                          ALU.add, waits=[t1])
                t2 = p.do("v", "scalar_tensor_tensor", magrow[:], mrow(k), -1.0,
                          mrow(k), ALU.mult, ALU.max, waits=W(after))
                t2 = p.do("v", "tensor_scalar", magrow[:], magrow[:], 0.25, None,
                          ALU.max, waits=[t2])
                return p.do("v", "tensor_tensor", dst, dst, magrow[:], ALU.mult,
                            waits=[t1, t2])

            t_ga = clamp_gate(2, garow[:], MOD)
            t_gb = clamp_gate(5, gbrow[:], t_ga)
            t_g1r = p.do("v", "tensor_scalar", g1row[:], mrow(1), 1.0, None,
                         ALU.add, waits=[MOD])
            t_g2r = p.do("v", "tensor_scalar", g2row[:], mrow(4), 1.0, None,
                         ALU.add, waits=[MOD])

            def row_to_cols(row_ap, dst, after):
                m = None
                for j in range(8):
                    m = p.do("t", "transpose", pb[7][:, j:j + 1],
                             row_ap[0:1, j * 128:(j + 1) * 128], eyef[0:1, 0:1],
                             waits=W(after, EYEF, b7busy[0]) if j == 0 else ())
                ev = p.do("s", "copy", dst, pb[7][:, 0:8], waits=[m])
                b7busy[0] = ev
                return ev

            t_g1c = row_to_cols(g1row, g1col[:, :], t_g1r)
            t_r = p.do("v", "reciprocal", sagrow[:], g1row[:], waits=[t_g1r])
            t_r = p.do("v", "tensor_tensor", sagrow[:], sagrow[:], mrow(0),
                       ALU.mult, waits=[t_r])
            t_sagc = row_to_cols(sagrow, sagf[:, :], t_r)
            t_sagb = p.do("v", "tensor_copy", sagcol[:], sagf[:], waits=[t_sagc])

            def bcast_tile(row_ap, dst, after):
                ev = None
                for h in range(2):
                    m = p.do("t", "matmul", pb[7][:, 0:512], ones_rf[:],
                             row_ap[0:1, h * 512:(h + 1) * 512], start=True,
                             stop=True, waits=W(after, ONESRF, b7busy[0]))
                    ev = p.do("s", "copy", dst[:, h * 512:(h + 1) * 512],
                              pb[7][:, 0:512], waits=[m])
                    b7busy[0] = ev
                return ev

            t_g2bc = bcast_tile(g2row, g2bc, t_g2r)
            t_s2bc = bcast_tile(modrow[0:1, 3 * D:4 * D], s2bc, MOD)
            t_gbbc = bcast_tile(gbrow, gbbc, t_gb)
            t_gabc = bcast_tile(garow, gabc, t_ga)

            wfree = [None, None]

            def fold_qkv(widx, dram, Wt, augt):
                pr = widx % 2
                d = p.dma("gd", f"wst{pr}", wstb[pr][:],
                          dram.rearrange("(c p) m -> p c m", p=128),
                          waits=W(wfree[pr]))
                tt = None
                for kc in range(8):
                    tt = p.do("v", "tensor_scalar", Wt[:, kc, :], wstb[pr][:, kc, :],
                              g1col[:, kc:kc + 1], None, ALU.mult,
                              waits=W(d, t_g1c) if kc == 0 else ())
                wfree[pr] = tt
                m = None
                for kc in range(8):
                    m = MMr(7, slice(0, 1), slice(0, 512), ones_c[:], Wt[:, kc, :],
                            start=(kc == 0), stop=(kc == 7),
                            waits=W(tt, ONESC, b7busy[0]) if kc == 0 else ())
                e1 = p.do("v", "tensor_copy", augt[0:1, 0:512], pb[7][0:1, 0:512],
                          waits=[m])
                b7busy[0] = e1
                m = None
                for kc in range(8):
                    m = MMr(7, slice(0, 1), slice(0, 512), sagcol[:, kc:kc + 1],
                            Wt[:, kc, :], start=(kc == 0), stop=(kc == 7),
                            waits=W(t_sagb, b7busy[0]) if kc == 0 else ())
                e2 = p.do("v", "tensor_copy", augt[0:1, 512:1024],
                          pb[7][0:1, 0:512], waits=[m])
                b7busy[0] = e2
                return e2

            t_wq = fold_qkv(0, wq_d, Wq, qaug)
            t_wk = fold_qkv(1, wk_d, Wk, kaug)
            t_wv = fold_qkv(2, wv_d, Wv, vaug)

            d = p.dma("gd", "wst0", wbstb[:], wb_d.rearrange("(c p) m -> p c m", p=128), waits=W(wfree[0]))
            tt = None
            for kc in range(8):
                tt = p.do("v", "tensor_scalar", Wbt[:, kc, :], wbstb[:, kc, :],
                          g1col[:, kc:kc + 1], None, ALU.mult,
                          waits=W(d, t_g1c) if kc == 0 else ())
            m = None
            for kc in range(8):
                m = MMr(7, slice(0, 1), slice(0, 2), ones_c[:], Wbt[:, kc, :],
                        start=(kc == 0), stop=(kc == 7),
                        waits=W(tt, b7busy[0]) if kc == 0 else ())
            e1 = p.do("v", "tensor_copy", baug[0:1, 0:2], pb[7][0:1, 0:2], waits=[m])
            b7busy[0] = e1
            m = None
            for kc in range(8):
                m = MMr(7, slice(0, 1), slice(0, 2), sagcol[:, kc:kc + 1],
                        Wbt[:, kc, :], start=(kc == 0), stop=(kc == 7),
                        waits=W(t_sagb, b7busy[0]) if kc == 0 else ())
            t_wb = p.do("v", "tensor_copy", baug[0:1, 2:4], pb[7][0:1, 0:2], waits=[m])
            b7busy[0] = t_wb

            wofree = [None, None]
            t_wo = None
            for kc in range(8):
                pr = kc % 2
                d = p.dma("yd", f"wo{pr}", wotf[pr][:], wo_d[kc * 128:(kc + 1) * 128, :],
                          waits=W(wofree[pr]))
                u = p.do("v", "tensor_scalar", wotf[pr][:], wotf[pr][:],
                         gncol[:, kc:kc + 1], None, ALU.mult, waits=[d, gn])
                t_wo = p.do("v", "tensor_tensor", Wo[:, kc, :], wotf[pr][:],
                            gabc[:], ALU.mult, waits=[u, t_gabc])
                wofree[pr] = t_wo

            PRO_GATE = [t_wq, t_wk, t_wv, t_wb, t_wo, t_g2bc, t_s2bc, t_gbbc,
                        t_sagb, MOD, t_gabc, t_sagc]

        PROJ_RDY = [t_wq, t_wk, t_wv, t_wb]
        GATEP = p.do("v", "memset", gatescr[:], 0.0, waits=PRO_GATE)

        with ExitStack() as stB:
            def sbB(name, shape, dt=F32):
                return stB.enter_context(nc.sbuf_tensor(name, shape, dt))

            xTb = [sbB(f"xTb{i}", [128, 8, TW], BF16) for i in range(2)]
            xtmt = sbB("xtmt", [128, 2, D], F32)
            qT = [sbB(f"qT{i}", [128, 4, TW], BF16) for i in range(2)]
            kT = [sbB(f"kT{i}", [128, 4, TW], BF16) for i in range(2)]
            vt = [sbB(f"vt{i}", [128, 4, TW], BF16) for i in range(2)]
            sqt = sbB("sqt", [128, TW], BF16)
            osq = sqt
            augT_ = sbB("augT_", [1, 2 * TW], BF16)
            augT = [augT_, augT_]
            rows_ = sbB("rows_", [1, 10 * TW], F32)
            rowsb = sbB("rowsb", [1, 4 * TW], F32)
            bnsS = sbB("bnsS", [128, 12], F32)
            bnaS = sbB("bnaS", [128, 8], F32)
            ost_ = sbB("ost_", [128, 4, TW], BF16)
            ost = [ost_, ost_]
            Pt = sbB("Pt", [128, 2, 128], BF16)
            Zt = sbB("Zt", [128, 2, 128], BF16)
            Rt = sbB("Rt", [128, 2, 128], BF16)
            Mtt = sbB("Mtt", [128, 2, 128], BF16)
            Ktm = sbB("Ktm", [128, 2, 256], BF16)
            Kkap = sbB("Kkap", [128, 2, 256], BF16)
            Wpt = sbB("Wpt", [128, 4, 128], BF16)
            Ut = sbB("Ut", [128, 256], BF16)
            Sst = [sbB(f"Sst{h}", [128, 512], BF16) for h in range(2)]
            Sf32 = [sbB(f"Sf32{h}", [128, 512], F32) for h in range(2)]
            statc = sbB("statc", [128, 4, 4], F32)
            rowsc = sbB("rowsc", [1, 128], F32)
            rbc = sbB("rbc", [128, 128], BF16)
            ffst_ = sbB("ffst_", [128, IF // 2], BF16)
            ffst = [ffst_, ffst_]
            tpos = sbB("tpos", [128, IF], BF16)
            absdmp = tpos
            w8tmp = sbB("w8tmp", [128, IF], FP8)
            tneg = w8tmp
            absacc = sbB("absacc", [128, 35], F32)

            RM, RVAR, RSIG, RINV = 0, 1, 2, 3
            RB0, RNN0, REQ0 = 4, 6, 8

            def R(pr, r, n=1, cs=None):
                if cs is None:
                    return rows_[0:1, r * TW:(r + n) * TW]
                return rows_[0:1, r * TW + cs.start:r * TW + cs.stop]

            def Rb(r, n=1, cs=None):
                if cs is None:
                    return rowsb[0:1, r * TW:(r + n) * TW]
                return rowsb[0:1, r * TW + cs.start:r * TW + cs.stop]

            bg = []
            ffree = [None, None]
            abs_done = {}
            abs_last = {}
            dmp_last = [None]
            tern_last = [None]
            w8_free = [None]
            GATE_BG = [None]

            def mk_abs(mi, dram, nch, chw):
                def mkgo(j, hh):
                    def go():
                        pr = 0
                        c2 = chw // 2
                        dd = p.dma("gd", "ffst", ffst[pr][:, 0:c2],
                                   dram[j * 128:(j + 1) * 128,
                                        hh * c2:(hh + 1) * c2],
                                   waits=W(ffree[pr]))
                        col = j if hh == 0 else 32
                        a = p.do("s", "activation", absdmp[:, 0:c2],
                                 ffst[pr][:, 0:c2], AF.Abs,
                                 accum_out=absacc[:, col:col + 1],
                                 waits=W(dd, dmp_last[0],
                                         abs_done.get(mi - 1) if mi > 0 else None,
                                         abs_last.get(mi)))
                        dmp_last[0] = a
                        ffree[pr] = a
                        if hh == 1:
                            a = p.do("v", "tensor_tensor", absacc[:, j:j + 1],
                                     absacc[:, j:j + 1], absacc[:, 32:33],
                                     ALU.add, waits=[a, abs_last.get(mi)])
                        abs_last[mi] = a
                    return go
                for j in range(nch):
                    for hh in range(2):
                        bg.append(mkgo(j, hh))

                def fin():
                    r = p.do("v", "tensor_reduce", absacc[:, 33:34],
                             absacc[:, 0:nch], AX, ALU.add,
                             waits=[abs_last[mi]])
                    mm_ = p.do("t", "matmul", pb[7][0:1, 0:1], ones_cf[:],
                               absacc[:, 33:34], start=True, stop=True,
                               waits=W(r, ONESCF, b7busy[0]))
                    nelem = float(D * IF)
                    s1 = p.do("v", "tensor_scalar", absacc[0:1, 34:35],
                              pb[7][0:1, 0:1], 1.0 / nelem, 1e-5, ALU.mult,
                              ALU.max, waits=[mm_])
                    b7busy[0] = s1
                    mm2 = p.do("t", "matmul", pb[7][:, 4:5], ones_rf[:],
                               absacc[0:1, 34:35], start=True, stop=True,
                               waits=W(s1, b7busy[0]))
                    e1 = p.do("v", "tensor_copy", wsc[:, 3 + mi:4 + mi],
                              pb[7][:, 4:5], waits=[mm2])
                    b7busy[0] = e1
                    w1 = p.do("v", "tensor_scalar", wsc[:, mi:mi + 1],
                              wsc[:, 3 + mi:4 + mi], 0.5, None, ALU.mult,
                              waits=[e1])
                    w2 = p.do("v", "tensor_scalar", nthr[:, mi:mi + 1],
                              wsc[:, mi:mi + 1], -1.0, None, ALU.mult, waits=[w1])
                    abs_done[mi] = w2
                bg.append(fin)

            def mk_tern(mi, dram, dst_sb, dst_dram, nch, chw):
                def mkgo(j, hh):
                    def go():
                        pr = 0
                        c2 = chw // 2
                        dd = p.dma("gd", "ffst", ffst[pr][:, 0:c2],
                                   dram[j * 128:(j + 1) * 128,
                                        hh * c2:(hh + 1) * c2],
                                   waits=W(ffree[pr]))
                        a1 = p.do("v", "tensor_scalar", tpos[:, 0:c2],
                                  ffst[pr][:, 0:c2], wsc[:, mi:mi + 1], None,
                                  ALU.is_gt, waits=W(dd, abs_done[mi], tern_last[0]))
                        a2 = p.do("v", "tensor_scalar", w8tmp[:, 0:c2],
                                  ffst[pr][:, 0:c2], nthr[:, mi:mi + 1], None,
                                  ALU.is_lt, waits=W(dd, w8_free[0]))
                        if dst_sb is not None:
                            a3 = p.do("v", "tensor_tensor",
                                      dst_sb[:, j, hh * c2:(hh + 1) * c2],
                                      tpos[:, 0:c2], w8tmp[:, 0:c2],
                                      ALU.subtract, waits=[a1, a2])
                            w8_free[0] = a3
                        else:
                            a3 = p.do("v", "tensor_tensor",
                                      w8tmp[:, 0:c2], tpos[:, 0:c2],
                                      w8tmp[:, 0:c2], ALU.subtract,
                                      waits=[a1, a2])
                            a4 = p.dma("gd", "w8out",
                                       dst_dram[:, j, hh * c2:(hh + 1) * c2],
                                       w8tmp[:, 0:c2], waits=[a3])
                            w8_free[0] = a4
                        ffree[pr] = a3
                        tern_last[0] = a3
                    return go
                for j in range(nch):
                    for hh in range(2):
                        bg.append(mkgo(j, hh))

            mk_abs(0, wgT_d, 8, IF)
            mk_tern(0, wgT_d, None, wg8_d, 8, IF)
            mk_abs(1, wuT_d, 8, IF)
            mk_tern(1, wuT_d, None, wu8_d, 8, IF)
            mk_abs(2, wdT_d, 32, 1024)
            mk_tern(2, wdT_d, Wd8, None, 32, 1024)

            def fin_muxd():
                t1 = p.do("v", "tensor_tensor", wsc[:, 6:7], wsc[:, 4:5],
                          wsc[:, 5:6], ALU.mult, waits=[abs_done[1], abs_done[2]])
                t1 = p.do("v", "tensor_scalar", wsc[:, 6:7], wsc[:, 6:7],
                          1.0 / 127.0, None, ALU.mult, waits=[t1])
                abs_done[3] = t1
            bg.append(fin_muxd)
            bgi = [0]

            def run_bg(k):
                n = 0
                while n < k and bgi[0] < len(bg):
                    bg[bgi[0]]()
                    bgi[0] += 1
                    n += 1

            GATE0 = GATEP
            ffree[0] = GATE0
            ffree[1] = GATE0
            dmp_last[0] = GATE0
            tern_last[0] = GATE0
            w8_free[0] = GATE0
            xT_free = [GATE0, GATE0]
            xtm_free = [GATE0]
            q_free = [GATE0, GATE0]
            k_free = [GATE0, GATE0]
            v_free = [GATE0, GATE0]
            aug_free = [GATE0, GATE0]
            rows_free = [GATE0]
            ost_free = [GATE0]
            bn_last = [None]
            bna_free = [None]
            big_free = [None, None]
            ch_free = [(), ()]
            b6_free = [None]
            sq_last = [None]
            osq_free = [None]
            Pt_free = [GATE0, GATE0]
            Zt_free = [GATE0, GATE0]
            Rt_free = [GATE0, GATE0]
            Mt_free = [GATE0, GATE0]
            Ktm_free = [GATE0, GATE0]
            Kkap_free = [GATE0, GATE0]
            Wpt_free = [GATE0, GATE0]
            Ut_free = [GATE0]
            rowsc_free = [GATE0]
            rbc_free = [GATE0]
            statc_free = [GATE0]
            chain_tok = [None, None]
            state_guard = [None, None]

            for i in (range(NT) if variant >= 1 else []):
                pr = i % 2
                t0 = i * TW
                dx = p.dma("gd", f"xTb{pr}", xTb[pr][:],
                           xT.rearrange("(c p) t -> p c t", p=128)[:, :, t0:t0 + TW],
                           waits=W(xT_free[pr]))
                bt = None
                bn_prev = [xtm_free[0], xtm_free[0]]
                for c_ in range(CPT):
                    sl2 = c_ % 2
                    dxm = p.dma("yd", f"xtm{sl2}", xtmt[:, sl2, :],
                                x_tm[t0 + c_ * 128:t0 + (c_ + 1) * 128, :],
                                waits=W(bn_prev[sl2]))
                    b1 = p.do("v", "bn_stats", bnsS[:, 0:6], xtmt[:, sl2, 0:512],
                              waits=W(dxm, bn_last[0]))
                    b1 = p.do("v", "bn_stats", bnsS[:, 6:12], xtmt[:, sl2, 512:1024],
                              waits=[b1])
                    b2 = p.do("v", "bn_aggr", bnaS[:, 2 * c_:2 * c_ + 2], bnsS[:],
                              waits=W(b1, bna_free[0] if c_ == 0 else None))
                    bn_last[0] = b2
                    bn_prev[sl2] = b1
                    bt = b2
                xtm_free[0] = bt
                m = None
                for c_ in range(CPT):
                    m = p.do("t", "transpose", pb[7][0:1, c_ * 128:(c_ + 1) * 128],
                             bnaS[:, 2 * c_:2 * c_ + 1], eyef[:],
                             waits=W(bt, EYEF, b7busy[0]) if c_ == 0 else ())
                ev = p.do("s", "copy", R(pr, RM), pb[7][0:1, 0:512],
                          waits=W(m, rows_free[0]))
                b7busy[0] = ev
                rows_free[0] = None
                m = None
                for c_ in range(CPT):
                    m = p.do("t", "transpose", pb[7][0:1, c_ * 128:(c_ + 1) * 128],
                             bnaS[:, 2 * c_ + 1:2 * c_ + 2], eyef[:],
                             waits=W(b7busy[0]) if c_ == 0 else ())
                bna_free[0] = m
                ev = p.do("s", "copy", R(pr, RVAR), pb[7][0:1, 0:512], waits=[m])
                b7busy[0] = ev
                t_sig = p.do("s", "activation", R(pr, RSIG), R(pr, RVAR), AF.Sqrt,
                             bias=eps_c[0:1, 0:1], waits=W(ev, EPSC))
                t_inv = p.do("v", "reciprocal", R(pr, RINV), R(pr, RSIG),
                             waits=[t_sig])
                a1 = p.do("v", "tensor_scalar", augT[pr][0:1, 0:TW],
                          R(pr, RM), -1.0, None, ALU.mult,
                          waits=W(ev, aug_free[pr]))
                aug_free[pr] = None
                a2 = p.do("v", "tensor_copy", augT[pr][0:1, TW:2 * TW],
                          R(pr, RSIG), waits=[t_sig])
                AUG = [a1, a2]

                def proj_fm(Wt, waug, dstT, dst_free_tok, sign_ones, row0, sgn_tok):
                    last_tok = None
                    for oc in range(4):
                        bk = oc % 2
                        m = None
                        for kc in range(8):
                            m = MM(bk, slice(0, 512),
                                   Wt[:, kc, oc * 128:(oc + 1) * 128],
                                   xTb[pr][:, kc, :], start=(kc == 0), stop=False,
                                   waits=W(dx, *PROJ_RDY, big_free[bk], *AUG)
                                   if kc == 0 else ())
                        m = MM(bk, slice(0, 512),
                               waug[0:1, oc * 128:(oc + 1) * 128],
                               augT[pr][0:1, 0:TW], start=False, stop=False)
                        m = MM(bk, slice(0, 512),
                               waug[0:1, 512 + oc * 128:512 + (oc + 1) * 128],
                               augT[pr][0:1, TW:2 * TW], start=False, stop=True)
                        e = p.do("s", "copy", dstT[:, oc, :], pb[bk][:, 0:512],
                                 waits=W(m, dst_free_tok if oc == 0 else None))
                        sq = p.do("s", "activation", sqt[:], pb[bk][:, 0:512],
                                  AF.Square, waits=W(sq_last[0]))
                        mm2 = MMr(7, slice(0, 1), slice(0, 512), sign_ones, sqt[:],
                                  start=(oc % 2 == 0), stop=(oc % 2 == 1),
                                  waits=W(sq, b7busy[0], sgn_tok)
                                  if oc % 2 == 0 else [sq])
                        big_free[bk] = sq
                        sq_last[0] = mm2
                        if oc % 2 == 1:
                            rr = row0 + oc // 2
                            e2 = p.do("v", "tensor_copy", R(pr, rr),
                                      pb[7][0:1, 0:512], waits=[mm2])
                            b7busy[0] = e2
                            last_tok = e2
                    return last_tok

                qsq = proj_fm(Wq, qaug, qT[pr], q_free[pr], ones_c[:], REQ0, ONESC)
                q_free[pr] = None
                tq = p.do("v", "tensor_scalar", R(pr, REQ0, 2), R(pr, REQ0, 2),
                          EPS, None, ALU.mult, waits=[qsq])
                ksq = proj_fm(Wk, kaug, kT[pr], k_free[pr], nones_c[:], RNN0,
                              NONESC)
                k_free[pr] = None

                tb1 = None
                for hh in range(2):
                    m = None
                    for kc in range(8):
                        m = MMr(7, slice(0, 1), slice(0, 512),
                                Wbt[:, kc, hh:hh + 1],
                                xTb[pr][:, kc, :], start=(kc == 0), stop=False,
                                waits=W(b7busy[0]) if kc == 0 else ())
                    m = MMr(7, slice(0, 1), slice(0, 512), baug[0:1, hh:hh + 1],
                            augT[pr][0:1, 0:TW], start=False, stop=False)
                    m = MMr(7, slice(0, 1), slice(0, 512), baug[0:1, 2 + hh:3 + hh],
                            augT[pr][0:1, TW:2 * TW], start=False, stop=True)
                    tb1 = p.do("v", "tensor_tensor", R(pr, RB0 + hh),
                               pb[7][0:1, 0:512], R(pr, RINV), ALU.mult,
                               waits=[m, t_inv])
                    b7busy[0] = tb1
                t_bsig = p.do("s", "activation", R(pr, RB0, 2), R(pr, RB0, 2),
                              AF.Sigmoid, waits=[tb1])
                t_sn = p.do("s", "activation", R(pr, RNN0, 2), R(pr, RNN0, 2),
                            AF.Sqrt, scale=-1.0, waits=[ksq])
                t_dl = p.do("v", "reciprocal", R(pr, RNN0, 2), R(pr, RNN0, 2),
                            waits=[t_sn])
                t_bd = p.do("v", "tensor_tensor", R(pr, RB0, 2), R(pr, RB0, 2),
                            R(pr, RNN0, 2), ALU.mult, waits=[t_dl, t_bsig])
                t_ng = p.do("v", "scalar_tensor_tensor", Rb(0, 2), R(pr, RB0, 2),
                            -1.0, R(pr, RNN0, 2), ALU.mult, ALU.mult,
                            waits=[t_bd])
                t_gv0 = p.do("v", "tensor_tensor", Rb(2), R(pr, RB0),
                             R(pr, RINV), ALU.mult, waits=[t_bd])
                t_gv1 = p.do("v", "tensor_tensor", Rb(3), R(pr, RB0 + 1),
                             R(pr, RINV), ALU.mult, waits=[t_bd])

                m = None
                first = True
                for c_ in range(CPT):
                    for st_ in range(4):
                        m = p.do("t", "transpose",
                                 pb[7][:, c_ * 4 + st_:c_ * 4 + st_ + 1],
                                 Rb(st_, cs=slice(c_ * 128, (c_ + 1) * 128)),
                                 ones_cf[0:1, 0:1],
                                 waits=W(t_ng, t_gv0, t_gv1, b7busy[0],
                                         statc_free[0]) if first else ())
                        first = False
                STATC = p.do("s", "copy", statc[:].rearrange("p c m -> p (c m)"),
                             pb[7][:, 0:16], waits=[m])
                b7busy[0] = STATC
                statc_free[0] = None

                vlast = None
                for tc in range(CPT):
                    bk = tc % 2
                    m = None
                    for kc in range(8):
                        m = MM(bk, slice(0, 512),
                               xTb[pr][:, kc, tc * 128:(tc + 1) * 128],
                               Wv[:, kc, :], start=(kc == 0), stop=False,
                               waits=W(big_free[bk]) if kc == 0 else ())
                    m = MM(bk, slice(0, 512),
                           augT[pr][0:1, tc * 128:(tc + 1) * 128],
                           vaug[0:1, 0:512], start=False, stop=False)
                    m = MM(bk, slice(0, 512),
                           augT[pr][0:1, TW + tc * 128:TW + (tc + 1) * 128],
                           vaug[0:1, 512:1024], start=False, stop=True)
                    e0 = p.do("v", "tensor_scalar", vt[pr][:, tc, 0:256],
                              pb[bk][:, 0:256], statc[:, tc, 2:3], None, ALU.mult,
                              waits=W(m, STATC, v_free[pr] if tc == 0 else None))
                    e1 = p.do("v", "tensor_scalar", vt[pr][:, tc, 256:512],
                              pb[bk][:, 256:512], statc[:, tc, 3:4], None,
                              ALU.mult, waits=[m])
                    big_free[bk] = e1
                    vlast = e1
                v_free[pr] = None
                aug_free[pr] = m
                xT_free[pr] = m

                run_bg(4)

                for c_ in (range(CPT) if variant >= 2 else []):
                    sub = variant if variant < 10 else variant - 20
                    g = i * CPT + c_
                    cs = slice(c_ * 128, (c_ + 1) * 128)
                    pp = 2 + (g % 2)
                    ppo = 2 + ((g + 1) % 2)
                    solved = [None, None]
                    for h in range(2):
                        m = None
                        for jj in range(2):
                            m = MM(pp, slice(h * 128, (h + 1) * 128),
                                   kT[pr][:, 2 * h + jj, cs],
                                   kT[pr][:, 2 * h + jj, cs],
                                   start=(jj == 0), stop=(jj == 1),
                                   waits=W(*ch_free[pp - 2], ksq) if jj == 0 else ())
                        mA = m
                        m = None
                        for jj in range(2):
                            m = MM(pp, slice(256 + h * 128, 256 + (h + 1) * 128),
                                   kT[pr][:, 2 * h + jj, cs],
                                   qT[pr][:, 2 * h + jj, cs],
                                   start=(jj == 0), stop=(jj == 1),
                                   waits=W(qsq) if jj == 0 else ())
                        mM = m
                        eP = p.do("v", "scalar_tensor_tensor", Pt[:, h, :],
                                  pb[pp][:, h * 128:(h + 1) * 128],
                                  statc[:, c_, h:h + 1], mask_sl[:], ALU.mult,
                                  ALU.mult, waits=W(mA, STATC, MSL, Pt_free[h]))
                        Pt_free[h] = None
                        eM = p.do("v", "tensor_tensor", Mtt[:, h, :],
                                  pb[pp][:, 256 + h * 128:256 + (h + 1) * 128],
                                  mask_ule[:], ALU.mult,
                                  waits=W(mM, MULE, Mt_free[h]))
                        Mt_free[h] = None
                        ch_free[pp - 2] = (eM,)
                        if variant > 10 and sub < 2:
                            solved[h] = eM
                            continue
                        mz = TRb(ppo, slice(512 * h, 512 * h + 128), Pt[:, h, :],
                                 waits=W(eP, EYEB, *ch_free[ppo - 2]))
                        mk2 = TRb(ppo, slice(512 * h + 128, 512 * h + 256),
                                  kT[pr][:, 2 * h, cs], waits=())
                        mk2 = TRb(ppo, slice(512 * h + 256, 512 * h + 384),
                                  kT[pr][:, 2 * h + 1, cs], waits=())
                        eZ = p.do("s", "copy", Zt[:, h, :],
                                  pb[ppo].bitcast(BF16)[:, 512 * h:512 * h + 128],
                                  waits=W(mz, Zt_free[h]))
                        Zt_free[h] = None
                        eK = p.do("s", "copy", Ktm[:, h, :],
                                  pb[ppo].bitcast(BF16)[:, 512 * h + 128:512 * h + 384],
                                  waits=W(mk2, Ktm_free[h]))
                        Ktm_free[h] = None
                        ch_free[ppo - 2] = (eK,)
                        eKk = p.do("v", "tensor_scalar", Kkap[:, h, :],
                                   Ktm[:, h, :], statc[:, c_, h:h + 1], None,
                                   ALU.mult, waits=W(eK, Kkap_free[h]))
                        Kkap_free[h] = None
                        if variant > 10 and sub < 3:
                            solved[h] = eKk
                            continue
                        r1 = MM(6, slice(h * 128, (h + 1) * 128), eyeb[:], eyeb[:],
                                start=True, stop=False, waits=W(EYEB, b6_free[0]))
                        r2 = MM(6, slice(h * 128, (h + 1) * 128), Pt[:, h, :],
                                eyeb[:], start=False, stop=True, waits=[eP])
                        lr = p.do("s", "copy", Rt[:, h, :],
                                  pb[6][:, h * 128:(h + 1) * 128],
                                  waits=W(r2, Rt_free[h]))
                        Rt_free[h] = None
                        b6_free[0] = lr
                        lp, lz = eP, eZ
                        for lvl in range(1, 7):
                            bkl = 2 + ((g + lvl) % 2)
                            m1 = MM(bkl, slice(h * 128, (h + 1) * 128), Zt[:, h, :],
                                    Pt[:, h, :], start=True, stop=True,
                                    waits=W(lp, lz, *ch_free[bkl - 2])
                                    if lvl == 1 else ())
                            m2 = MM(bkl, slice(256 + h * 128, 256 + (h + 1) * 128),
                                    Pt[:, h, :], Zt[:, h, :], start=True, stop=True)
                            lz = p.do("s", "copy", Zt[:, h, :],
                                      pb[bkl][:, 256 + h * 128:256 + (h + 1) * 128],
                                      waits=[m2])
                            lp = p.do("v", "tensor_copy", Pt[:, h, :],
                                      pb[bkl][:, h * 128:(h + 1) * 128],
                                      waits=[lz])
                            ch_free[bkl - 2] = (lp, lz)
                            rm = MM(6, slice(h * 128, (h + 1) * 128), eyeb[:],
                                    Rt[:, h, :], start=True, stop=False,
                                    waits=[lp])
                            rm = MM(6, slice(h * 128, (h + 1) * 128), Pt[:, h, :],
                                    Rt[:, h, :], start=False, stop=True)
                            lr = p.do("s", "copy", Rt[:, h, :],
                                      pb[6][:, h * 128:(h + 1) * 128], waits=[rm])
                            b6_free[0] = lr
                        if variant > 10 and sub < 4:
                            solved[h] = lr
                            continue
                        m = MM(pp, slice(0, 128), Kkap[:, h, 0:128], Rt[:, h, :],
                               start=True, stop=True,
                               waits=W(eKk, lr, *ch_free[pp - 2]))
                        m2 = MM(pp, slice(128, 256), Kkap[:, h, 128:256],
                                Rt[:, h, :], start=True, stop=True)
                        eW = p.do("s", "copy",
                                  Wpt[:].rearrange("p c m -> p (c m)")[:, h * 256:(h + 1) * 256],
                                  pb[pp][:, 0:256], waits=W(m2, Wpt_free[h]))
                        Wpt_free[h] = None
                        ch_free[pp - 2] = (eW,)
                        Kkap_free[h] = eW
                        solved[h] = eW

                    for h in (range(2) if (variant < 10 or sub >= 4) else []):
                        mU = MM(6, slice(256, 512), Rt[:, h, :],
                                vt[pr][:, c_, h * 256:(h + 1) * 256],
                                start=True, stop=(g == 0),
                                waits=W(solved[0], solved[1], vlast, b6_free[0],
                                        chain_tok[h]))
                        if g > 0:
                            for jj in range(2):
                                mU = MM(6, slice(256, 512),
                                        Wpt[:, 2 * h + jj, :],
                                        Sst[h][:, jj * 256:(jj + 1) * 256],
                                        start=False, stop=(jj == 1))
                        eU = p.do("s", "copy", Ut[:], pb[6][:, 256:512],
                                  waits=W(mU, Ut_free[0]))
                        b6_free[0] = eU
                        mo = None
                        for dd in range(2):
                            if g > 0:
                                for jj in range(2):
                                    mo = MM(6, slice(256 + dd * 128, 256 + (dd + 1) * 128),
                                            Sst[h][:, jj * 256 + dd * 128:jj * 256 + (dd + 1) * 128],
                                            qT[pr][:, 2 * h + jj, cs],
                                            start=(jj == 0), stop=False,
                                            waits=W(eU, b6_free[0])
                                            if (dd == 0 and jj == 0) else ())
                                mo = MM(6, slice(256 + dd * 128, 256 + (dd + 1) * 128),
                                        Ut[:, dd * 128:(dd + 1) * 128], Mtt[:, h, :],
                                        start=False, stop=True)
                            else:
                                mo = MM(6, slice(256 + dd * 128, 256 + (dd + 1) * 128),
                                        Ut[:, dd * 128:(dd + 1) * 128], Mtt[:, h, :],
                                        start=True, stop=True,
                                        waits=W(eU, b6_free[0]) if dd == 0 else ())
                        sqo = p.do("s", "activation", osq[:, 0:256],
                                   pb[6][:, 256:512],
                                   AF.Square, waits=W(mo, osq_free[0], sq_last[0]))
                        sq_last[0] = sqo
                        mo2 = MMr(7, slice(0, 1), slice(0, 128), ones_c[:],
                                  osq[:, 0:128], start=True, stop=False,
                                  waits=W(sqo, b7busy[0]))
                        mo2 = MMr(7, slice(0, 1), slice(0, 128), ones_c[:],
                                  osq[:, 128:256], start=False, stop=True)
                        t1 = p.do("v", "tensor_scalar", rowsc[0:1, :],
                                  pb[7][0:1, 0:128], 1.0 / 256.0, None, ALU.mult,
                                  waits=W(mo2, rowsc_free[0]))
                        b7busy[0] = t1
                        t1 = p.do("v", "tensor_tensor", rowsc[0:1, :],
                                  rowsc[0:1, :], R(pr, REQ0 + h, cs=cs), ALU.add,
                                  waits=[t1, tq])
                        t1 = p.do("s", "activation", rowsc[0:1, :],
                                  rowsc[0:1, :], AF.Sqrt, waits=[t1])
                        t1 = p.do("v", "reciprocal", rowsc[0:1, :],
                                  rowsc[0:1, :], waits=[t1])
                        mb = p.do("t", "matmul", pb[7][:, 128:256], ones_rf[:],
                                  rowsc[0:1, :], start=True, stop=True,
                                  waits=W(t1, b7busy[0]))
                        eb = p.do("s", "copy", rbc[:], pb[7][:, 128:256],
                                  waits=W(mb, rbc_free[0]))
                        b7busy[0] = eb
                        oe = None
                        for dd in range(2):
                            oe = p.do("v", "tensor_tensor",
                                      ost[pr][:, 2 * h + dd, cs],
                                      pb[6][:, 256 + dd * 128:256 + (dd + 1) * 128],
                                      rbc[:], ALU.mult,
                                      waits=W(eb, ost_free[0]
                                              if (c_ == 0 and h == 0 and dd == 0)
                                              else None))
                        if c_ == 0 and h == 0:
                            ost_free[0] = None
                        rbc_free[0] = oe
                        rowsc_free[0] = oe
                        b6_free[0] = oe
                        osq_free[0] = mo2
                        sq_last[0] = mo2
                        ms = None
                        for jj in range(2):
                            ms = MM(4 + h, slice(jj * 256, (jj + 1) * 256),
                                    Ktm[:, h, jj * 128:(jj + 1) * 128], Ut[:],
                                    start=(jj == 0), stop=(jj == 1),
                                    waits=W(eU, state_guard[h]) if jj == 0 else ())
                        if g == 0:
                            es = p.do("v", "tensor_copy", Sf32[h][:], pb[4 + h][:],
                                      waits=[ms])
                        else:
                            es = p.do("v", "tensor_tensor", Sf32[h][:], Sf32[h][:],
                                      pb[4 + h][:], ALU.add, waits=[ms])
                        es = p.do("s", "copy", Sst[h][:], Sf32[h][:], waits=[es])
                        state_guard[h] = es
                        Ut_free[0] = ms
                        chain_tok[h] = es
                        Ktm_free[h] = ms
                        Rt_free[h] = mU
                        Mt_free[h] = mo
                        Wpt_free[h] = mU
                    for h in range(2):
                        Pt_free[h] = chain_tok[h]
                        Zt_free[h] = chain_tok[h]

                if variant >= 2 and (variant < 10 or (variant - 20) >= 4):
                    do_ = p.dma("gd", f"ocoll{pr}",
                                coll_in.rearrange("(c p) t -> p c t", p=128)[:, :, t0:t0 + TW],
                                ost[pr][:], waits=W(chain_tok[0], chain_tok[1]))
                    ost_free[0] = do_
                q_free[pr] = chain_tok[1]
                k_free[pr] = chain_tok[1]
                v_free[pr] = chain_tok[1]
                rows_free[0] = chain_tok[1]
                statc_free[0] = chain_tok[1]
                run_bg(4)

            run_bg(len(bg))
            FFN_RDY = W(tern_last[0], w8_free[0], abs_done.get(0),
                        abs_done.get(1), abs_done.get(2), abs_done.get(3))
            DUMMY_OUT = ost[0]
            PH23 = [("t", p.cnt["t"]), ("v", p.cnt["v"]), ("s", p.cnt["s"]),
                    ("g", p.cnt["g"])]

        GATE5 = p.do("v", "memset", gatescr[:], 0.0, waits=PH23 + FFN_RDY)
        if variant >= 3:
            COLL = p.do("g", "collective_compute", "AllGather", ALU.bypass,
                        replica_groups=[[0, 1], [2, 3], [4, 5], [6, 7]],
                        ins=[coll_in], outs=[coll_out],
                        waits=[("ocoll0", p.cnt.get("ocoll0", 0)),
                               ("ocoll1", p.cnt.get("ocoll1", 0))])
        else:
            COLL = GATE5

        with ExitStack() as stC:
            def sbC(name, shape, dt=F32):
                return stC.enter_context(nc.sbuf_tensor(name, shape, dt))

            sc8k = sbC("sc8k", [128, 4096], BF16)
            oat = sc8k.rearrange("p (c t) -> p c t", c=8)
            guiT = sc8k.rearrange("p (c t) -> p c t", c=32)
            xot = sbC("xot", [128, 4, D], F32)
            xmid = sbC("xmid", [128, 4, D], F32)
            h2t = sbC("h2t", [128, D], F32)
            h2i = sbC("h2i", [128, D], BF16)
            h2iT = sbC("h2iT", [128, 4, 8, 128], BF16)
            gub = [sbC(f"gub{t}", [128, IF], BF16) for t in range(4)]
            zt_ = sbC("zt_", [128, 512], F32)
            silg = zt_
            r32t = sbC("r32t", [128, 512], F32)
            r64t = sbC("r64t", [128, 512], F32)
            outt = sbC("outt", [128, 1024], F32)
            bns2 = sbC("bns2", [128, 12], F32)
            cols2 = sbC("cols2", [128, 4, 16], F32)
            wgst = [sbC(f"wgst{t}", [128, 8, 512], FP8) for t in range(2)]
            wust_ = sbC("wust_", [128, 8, 512], FP8)
            wust = [wust_, wust_]

            r0 = stC.enter_context(nc.sync.register("r0"))
            r1 = stC.enter_context(nc.sync.register("r1"))
            p.raw("y", lambda e: e.reg_load(r0, tb_s[:1, :1]),
                  waits=[d_tb], inc=False)

            G5 = PH23[-1]
            oat_free = [None]
            xot_free = [None]
            xmid_free = [None]
            big5 = [None, None]
            ch5 = [None, None]
            h2_free = [None]
            h2i_free = [None]
            h2iT_free = [None]
            gu_free = [[None] * 4]
            guiT_free = [None]
            z_free = [None]
            r32_free = [None]
            r64_free = [None]
            cols_free = [[None] * 4]
            bn2_last = [None]
            outt_free = [None]
            wg_free = [None, None]
            wu_free = [None, None]

            for j in (range(NOT) if variant >= 4 else []):
                fvar = 99 if variant == 4 else (variant - 40 if variant > 40 else 99)
                to = j * TW

                def load_oat(e, to=to):
                    e.reg_mov(r1, to)
                    e.reg_add(r1, r1, r0)
                    in_ap = bass.AP(coll_out.tensor, r1,
                                    [[N, 128], [512 * N, 2], [128 * N, 4], [1, TW]])
                    return e.dma_start(
                        out=oat.rearrange("p s2 t -> p s2 t") if False else sc8k[:].rearrange("p (s f t) -> p s f t", s=2, f=4),
                        in_=in_ap)
                dol = p.dma_reg("yd", "oat", load_oat,
                                waits=W(COLL, GATE5, oat_free[0]))
                dxo = p.dma("yd", "xot", xot[:],
                            x_own[to:to + TW, :].rearrange("(c p) m -> p c m", p=128),
                            waits=W(COLL, GATE5, xot_free[0]))

                xm_toks = []
                e = None
                m = None
                for tc in range(CPT):
                    for nn in range(2):
                        bk = nn
                        m = None
                        for kc in range(8):
                            m = MM(bk, slice(0, 512),
                                   oat[:, kc, tc * 128:(tc + 1) * 128],
                                   Wo[:, kc, nn * 512:(nn + 1) * 512],
                                   start=(kc == 0), stop=(kc == 7),
                                   waits=W(dol, dxo, t_wo, big5[bk], *PH23)
                                   if kc == 0 else ())
                        e = p.do("v", "tensor_tensor",
                                 xmid[:, tc, nn * 512:(nn + 1) * 512],
                                 pb[bk][:, 0:512],
                                 xot[:, tc, nn * 512:(nn + 1) * 512], ALU.add,
                                 waits=W(m, xmid_free[0]
                                         if (tc == 0 and nn == 0) else None))
                        big5[bk] = e
                    xm_toks.append(e)
                xmid_free[0] = None
                oat_free[0] = m
                guiT_free[0] = m
                xot_free[0] = e
                if fvar < 2:
                    continue

                h2it_toks = []
                for tc in range(CPT):
                    cl = cols2[:, tc, :]
                    b1 = p.do("v", "bn_stats", bns2[:, 0:6], xmid[:, tc, 0:512],
                              waits=W(xm_toks[tc], bn2_last[0]))
                    b1 = p.do("v", "bn_stats", bns2[:, 6:12], xmid[:, tc, 512:1024],
                              waits=[b1])
                    b2 = p.do("v", "bn_aggr", cl[:, 0:2], bns2[:],
                              waits=W(b1, cols_free[0][tc]))
                    cols_free[0][tc] = None
                    bn2_last[0] = b2
                    s1 = p.do("s", "activation", cl[:, 2:3], cl[:, 1:2], AF.Sqrt,
                              bias=eps_c[:, 0:1], waits=W(b2, EPSC))
                    v1 = p.do("v", "reciprocal", cl[:, 3:4], cl[:, 2:3], waits=[s1])
                    v2 = p.do("v", "scalar_tensor_tensor", cl[:, 4:5], cl[:, 0:1],
                              -1.0, cl[:, 3:4], ALU.mult, ALU.mult,
                              waits=[v1, b2])
                    xh = p.do("v", "tensor_scalar", h2t[:], xmid[:, tc, :],
                              cl[:, 3:4], cl[:, 4:5], ALU.mult, ALU.add,
                              waits=W(v2, h2_free[0]))
                    h2a = p.do("v", "tensor_tensor", h2t[:], h2t[:], g2bc[:],
                               ALU.mult, waits=W(xh, t_g2bc))
                    h2a = p.do("v", "tensor_tensor", h2t[:], h2t[:], s2bc[:],
                               ALU.add, waits=W(h2a, t_s2bc))
                    am = p.do("v", "tensor_reduce", cl[:, 5:6], h2t[:], AX,
                              ALU.max, apply_absolute_value=True, waits=[h2a])
                    am = p.do("v", "tensor_scalar", cl[:, 5:6], cl[:, 5:6], 1e-5,
                              None, ALU.max, waits=[am])
                    sc = p.do("v", "reciprocal", cl[:, 6:7], cl[:, 5:6],
                              waits=[am])
                    sc = p.do("v", "tensor_scalar", cl[:, 6:7], cl[:, 6:7], 127.0,
                              None, ALU.mult, waits=[sc])
                    hi = None
                    for s2_ in range(2):
                        ss = slice(s2_ * 512, (s2_ + 1) * 512)
                        z1 = p.do("v", "tensor_scalar", zt_[:], h2t[:, ss],
                                  cl[:, 6:7], None, ALU.mult,
                                  waits=W(sc, z_free[0]))
                        ra = p.do("v", "tensor_scalar", r32t[:], zt_[:], C32,
                                  -C32, ALU.add, ALU.add,
                                  waits=W(z1, r32_free[0]))
                        rb = p.do("v", "tensor_scalar", r64t[:], zt_[:], C64,
                                  -C64, ALU.add, ALU.add,
                                  waits=W(z1, r64_free[0]))
                        z_free[0] = rb
                        td = p.do("v", "tensor_tensor", r32t[:], r32t[:], r64t[:],
                                  ALU.subtract, waits=[ra, rb])
                        hi = p.do("v", "scalar_tensor_tensor", h2i[:, ss],
                                  r32t[:], gdisc[:, 0:1], r64t[:], ALU.mult,
                                  ALU.add, waits=W(td, GDISC, h2i_free[0]
                                                   if s2_ == 0 else None))
                        r32_free[0] = hi
                        r64_free[0] = hi
                    h2i_free[0] = None
                    h2_free[0] = hi
                    m = None
                    for kc in range(8):
                        m = TRb(2, slice(kc * 128, (kc + 1) * 128),
                                h2i[:, kc * 128:(kc + 1) * 128],
                                waits=W(hi, ch5[0]) if kc == 0 else ())
                    e = p.do("v", "tensor_copy",
                             h2iT[:, tc, :, :].rearrange("p c m -> p (c m)"),
                             pb[2].bitcast(BF16)[:, 0:1024],
                             waits=W(m, h2iT_free[0] if tc == 0 else None))
                    ch5[0] = e
                    h2i_free[0] = e
                    h2it_toks.append(e)
                h2iT_free[0] = None
                if fvar < 3:
                    continue

                dq_toks = []
                for tc in range(CPT):
                    cl = cols2[:, tc, :]
                    dg = p.do("v", "reciprocal", cl[:, 7:8], cl[:, 6:7],
                              waits=[h2it_toks[tc]])
                    dg = p.do("v", "tensor_tensor", cl[:, 8:9], cl[:, 7:8],
                              wsc[:, 3:4], ALU.mult, waits=W(dg, *FFN_RDY[:1]))
                    dq_toks.append(dg)

                gu_toks = [None] * 4
                mg = None
                mu_ = None
                for sl in range(8):
                    spr = sl % 2
                    dwg = p.dma("sd", f"wg{spr}", wgst[spr][:],
                                wg8_d[:, :, sl * 512:(sl + 1) * 512],
                                waits=W(wg_free[spr], *FFN_RDY))
                    dwu = p.dma("sd", "wuq", wust[spr][:],
                                wu8_d[:, :, sl * 512:(sl + 1) * 512],
                                waits=W(wu_free[0]))
                    for tc in range(CPT):
                        mg = None
                        for kc in range(8):
                            mg = MM(0, slice(0, 512), h2iT[:, tc, kc, :],
                                    wgst[spr][:, kc, :], start=(kc == 0),
                                    stop=(kc == 7),
                                    waits=W(h2it_toks[tc], dwg, big5[0])
                                    if kc == 0 else ())
                        sg_ = p.do("s", "activation", silg[:], pb[0][:, 0:512],
                                   AF.Silu, scale=cols2[:, tc, 8:9],
                                   waits=W(mg, dq_toks[tc], z_free[0]))
                        big5[0] = sg_
                        mu_ = None
                        for kc in range(8):
                            mu_ = MM(1, slice(0, 512), h2iT[:, tc, kc, :],
                                     wust[spr][:, kc, :], start=(kc == 0),
                                     stop=(kc == 7),
                                     waits=W(dwu, big5[1]) if kc == 0 else ())
                        gu_ = p.do("v", "tensor_tensor",
                                   gub[tc][:, sl * 512:(sl + 1) * 512], silg[:],
                                   pb[1][:, 0:512], ALU.mult,
                                   waits=W(mu_, sg_, gu_free[0][tc]
                                           if sl == 0 else None))
                        if sl == 0:
                            gu_free[0][tc] = None
                        big5[1] = gu_
                        z_free[0] = gu_
                        gu_toks[tc] = gu_
                    wg_free[spr] = mg
                    wu_free[0] = mu_
                if fvar < 4:
                    continue

                for tc in range(CPT):
                    cl = cols2[:, tc, :]
                    ag = p.do("v", "tensor_reduce", cl[:, 9:10], gub[tc][:], AX,
                              ALU.max, apply_absolute_value=True,
                              waits=W(*gu_toks))
                    th = p.do("v", "reciprocal", cl[:, 10:11], wsc[:, 4:5],
                              waits=W(abs_done.get(1)))
                    th = p.do("v", "tensor_tensor", cl[:, 10:11], cl[:, 10:11],
                              cl[:, 6:7], ALU.mult, waits=[th])
                    th = p.do("v", "tensor_scalar", cl[:, 10:11], cl[:, 10:11],
                              1e-5, None, ALU.mult, waits=[th])
                    me = p.do("v", "tensor_tensor", cl[:, 11:12], cl[:, 9:10],
                              cl[:, 10:11], ALU.max, waits=[ag, th])
                    s3 = p.do("v", "reciprocal", cl[:, 12:13], cl[:, 11:12],
                              waits=[me])
                    s3 = p.do("v", "tensor_scalar", cl[:, 12:13], cl[:, 12:13],
                              127.0, None, ALU.mult, waits=[s3])
                    dd_ = p.do("v", "tensor_tensor", cl[:, 13:14], cl[:, 11:12],
                               cl[:, 7:8], ALU.mult, waits=[me])
                    dd_ = p.do("v", "tensor_tensor", cl[:, 13:14], cl[:, 13:14],
                               wsc[:, 6:7], ALU.mult, waits=W(dd_))
                    cols_free[0][tc] = dd_
                    gl = None
                    for s2_ in range(8):
                        ss = slice(s2_ * 512, (s2_ + 1) * 512)
                        z1 = p.do("v", "tensor_scalar", zt_[:], gub[tc][:, ss],
                                  cl[:, 12:13], None, ALU.mult,
                                  waits=W(s3, z_free[0]))
                        ra = p.do("v", "tensor_scalar", r32t[:], zt_[:], C32,
                                  -C32, ALU.add, ALU.add,
                                  waits=W(z1, r32_free[0]))
                        rb = p.do("v", "tensor_scalar", r64t[:], zt_[:], C64,
                                  -C64, ALU.add, ALU.add,
                                  waits=W(z1, r64_free[0]))
                        z_free[0] = rb
                        td = p.do("v", "tensor_tensor", r32t[:], r32t[:],
                                  r64t[:], ALU.subtract, waits=[ra, rb])
                        gl = p.do("v", "scalar_tensor_tensor", gub[tc][:, ss],
                                  r32t[:], gdisc[:, 0:1], r64t[:], ALU.mult,
                                  ALU.add, waits=[td])
                        r32_free[0] = gl
                        r64_free[0] = gl
                    glast = None
                    for gq in range(4):
                        bq = 2 + (gq % 2)
                        m = None
                        for kc in range(8):
                            m = TRb(bq, slice(kc * 128, (kc + 1) * 128),
                                    gub[tc][:, (gq * 8 + kc) * 128:(gq * 8 + kc + 1) * 128],
                                    waits=W(gl, ch5[bq - 2]) if kc == 0 else ())
                        glast = p.do("v", "tensor_copy",
                                     guiT[:, gq * 8:(gq + 1) * 8, :].rearrange("p c m -> p (c m)"),
                                     pb[bq].bitcast(BF16)[:, 0:1024],
                                     waits=W(m, guiT_free[0] if gq == 0 else None))
                        ch5[bq - 2] = glast
                    guiT_free[0] = None
                    gu_free[0][tc] = glast
                    if fvar < 5:
                        oat_free[0] = glast
                        continue
                    o2 = None
                    mlast = None
                    for nn in range(2):
                        bk = nn
                        m = None
                        for kc in range(32):
                            m = MM(bk, slice(0, 512), guiT[:, kc, :],
                                   Wd8[:, kc, nn * 512:(nn + 1) * 512],
                                   start=(kc == 0), stop=(kc == 31),
                                   waits=W(glast, big5[bk], *FFN_RDY)
                                   if kc == 0 else ())
                        mlast = m
                        o1 = p.do("v", "scalar_tensor_tensor",
                                  outt[:, nn * 512:(nn + 1) * 512],
                                  pb[bk][:, 0:512], cl[:, 13:14],
                                  gbbc[:, nn * 512:(nn + 1) * 512], ALU.mult,
                                  ALU.mult,
                                  waits=W(m, dd_, t_gbbc, outt_free[0]
                                          if nn == 0 else None))
                        o2 = p.do("v", "tensor_tensor",
                                  outt[:, nn * 512:(nn + 1) * 512],
                                  outt[:, nn * 512:(nn + 1) * 512],
                                  xmid[:, tc, nn * 512:(nn + 1) * 512], ALU.add,
                                  waits=[o1])
                        big5[bk] = o2
                    guiT_free[0] = mlast
                    oat_free[0] = mlast
                    dout = p.dma("yd", "outq", out_d[to + tc * 128:to + (tc + 1) * 128, :],
                                 outt[:], waits=[o2])
                    outt_free[0] = dout
                xmid_free[0] = bn2_last[0]

            if variant < 4:
                p.dma("yd", "outq", out_d[0:128, :],
                      outt[:], waits=[GATE5])
            dstreams = [k for k in p.cnt if k not in ENGS]
            p.fence("y", [(k, p.cnt[k]) for k in dstreams])
            p.fence("y", [("t", p.cnt["t"]), ("v", p.cnt["v"]),
                          ("s", p.cnt["s"]), ("g", p.cnt["g"])])

            with nc.Block() as block:
                p.emit(block)
    return nc


_NC_CACHE = {}


def get_nc(variant=4):
    if variant not in _NC_CACHE:
        _NC_CACHE[variant] = build_program(variant)
    return _NC_CACHE[variant]


def make_in_maps(inputs):
    x = np.asarray(inputs["L0_attn_x"], dtype=np.float32)
    Wq_ = np.asarray(inputs["L0_attn_Wq"], dtype=np.float32)
    Wk_ = np.asarray(inputs["L0_attn_Wk"], dtype=np.float32)
    Wv_ = np.asarray(inputs["L0_attn_Wv"], dtype=np.float32)
    Wbeta = np.asarray(inputs["L0_attn_Wbeta"], dtype=np.float32)
    Wo_ = np.asarray(inputs["L0_attn_Wo"], dtype=np.float32)
    g_norm = np.asarray(inputs["L0_attn_g_norm"], dtype=np.float32)
    w_gate = np.asarray(inputs["L0_ffn_w_gate"], dtype=np.float32)
    w_up = np.asarray(inputs["L0_ffn_w_up"], dtype=np.float32)
    w_down = np.asarray(inputs["L0_ffn_w_down"], dtype=np.float32)
    c_ = np.asarray(inputs["L0_c"], dtype=np.float32)
    wmod = np.asarray(inputs["L0_wmod"], dtype=np.float32)

    wgT = np.ascontiguousarray(w_gate.T)
    wuT = np.ascontiguousarray(w_up.T)
    wdT = np.ascontiguousarray(w_down.T)
    gnt = np.ascontiguousarray(np.tile(g_norm, 4).reshape(8, 128).T)
    in_maps = []
    for cidx in range(8):
        b = cidx // 2
        hp = cidx % 2
        xb = x[b]
        in_maps.append(dict(
            x_tm=np.ascontiguousarray(xb),
            x_own=np.ascontiguousarray(xb[hp * NOWN:(hp + 1) * NOWN]),
            xT=np.ascontiguousarray(xb.T),
            wq=np.ascontiguousarray(Wq_[:, hp * 512:(hp + 1) * 512]),
            wk=np.ascontiguousarray(Wk_[:, hp * 512:(hp + 1) * 512]),
            wv=np.ascontiguousarray(Wv_[:, hp * 512:(hp + 1) * 512]),
            wb=np.ascontiguousarray(Wbeta[:, hp * 2:hp * 2 + 2]),
            wo=Wo_, wgT=wgT, wuT=wuT, wdT=wdT, wmod=wmod,
            cvec=np.ascontiguousarray(c_[b].reshape(8, 128).T),
            gnorm=gnt,
            tok_base=np.array([[hp * NOWN]], np.int32),
        ))
    return in_maps


import os
def kernel(**inputs):
    nc = get_nc(int(os.environ.get("KVARIANT", "4")))
    in_maps = make_in_maps(inputs)
    res = run_bass_kernel_spmd(nc, in_maps, core_ids=list(range(8)))
    out = np.empty((B, N, D), np.float32)
    for cidx in range(8):
        b, hp = cidx // 2, cidx % 2
        out[b, hp * NOWN:(hp + 1) * NOWN] = res.results[cidx]["out"]
    return out


# version 08
# speedup vs baseline: 1.0066x; 1.0066x over previous
import numpy as np
import ml_dtypes
import traceback
from contextlib import ExitStack

_DBG_ORIGIN = True

import concourse.bass as bass
import concourse.mybir as mybir
from concourse.bass_utils import run_bass_kernel_spmd

F32 = mybir.dt.float32
BF16 = mybir.dt.bfloat16
FP8 = mybir.dt.float8e4
I32 = mybir.dt.int32
AF = mybir.ActivationFunctionType
ALU = mybir.AluOpType
AX = mybir.AxisListType.X

B, N, D = 4, 8192, 1024
C = 128
TW = 512
NT = N // TW
CPT = TW // C
NOWN = N // 2
NOT = NOWN // TW
IF = 4096
C32 = 12582912.0
C64 = 6755399441055744.0
EPS = 1e-5

ENGS = ("t", "v", "s", "g", "y")
QUEUES = ("yd", "sd", "gd")


class Prog:

    def __init__(self, nc):
        self.nc = nc
        self.items = {e: [] for e in ENGS}
        self.cnt = {k: 0 for k in ENGS}
        self.wm = {e: {} for e in ENGS}
        self.sem = {}
        self.waited = {e: set() for e in ENGS}

    def make_sems(self, st):
        self._semstack = st
        for k in ENGS:
            self.sem[k] = st.enter_context(self.nc.semaphore("sem_" + k))

    def _stream_sem(self, stream):
        if stream not in self.sem:
            self.sem[stream] = self._semstack.enter_context(
                self.nc.semaphore("sem_" + stream))
            self.cnt[stream] = 0
        return self.sem[stream]

    def _need(self, eng, waits):
        wm = self.wm[eng]
        best = {}
        for tok in waits:
            if tok is None:
                continue
            s, v = tok
            if v is None:
                continue
            if v > best.get(s, -1):
                best[s] = v
        need = []
        for s, v in best.items():
            if wm.get(s, 0) >= v:
                continue
            wm[s] = v
            need.append((s, v))
            if s in ENGS:
                self.waited[s].add(v)
        return need

    def _add(self, eng, need, val, fn):
        self.items[eng].append((need, val, fn))

    def do(self, eng, op, *args, waits=(), inc=True, **kw):
        need = self._need(eng, waits)
        self.cnt[eng] += 1
        val = self.cnt[eng]

        def fn(e, op=op, args=args, kw=kw):
            return getattr(e, op)(*args, **kw)
        self._add(eng, need, val, fn)
        return (eng, val)

    def raw(self, eng, fn, waits=(), inc=True):
        need = self._need(eng, waits)
        self.cnt[eng] += 1
        val = self.cnt[eng]
        self._add(eng, need, val, fn)
        return (eng, val)

    def dma(self, queue, stream, out, in_, waits=()):
        eng = {"yd": "y", "sd": "s", "gd": "g"}[queue]
        self._stream_sem(stream)
        need = self._need(eng, waits)
        self.cnt[stream] += 16
        val = self.cnt[stream]
        sems = self.sem

        def fn(e, out=out, in_=in_, stream=stream):
            h = e.dma_start(out=out, in_=in_)
            h.then_inc(sems[stream], 16)
            return None
        self._add(eng, need, None, fn)
        return (stream, val)

    def dma_reg(self, queue, stream, fn0, waits=()):
        eng = {"yd": "y", "sd": "s", "gd": "g"}[queue]
        self._stream_sem(stream)
        need = self._need(eng, waits)
        self.cnt[stream] += 16
        sems = self.sem

        def fn(e, fn0=fn0, stream=stream):
            fn0(e).then_inc(sems[stream], 16)
            return None
        self._add(eng, need, None, fn)
        return (stream, self.cnt[stream])

    def fence(self, eng, waits):
        need = self._need(eng, waits)
        if not need:
            return
        self._add(eng, need, None, None)

    def emit(self, block):
        rank = {}
        for e in ENGS:
            rank[e] = {v: i + 1 for i, v in enumerate(sorted(self.waited[e]))}
        sems = self.sem

        def mkbody(name, items):
            r_all = rank
            waited = self.waited[name]
            rk = rank[name]

            def body(e):
                for need, val, fn in items:
                    for s, v in need:
                        if s in ENGS:
                            e.wait_ge(sems[s], r_all[s][v])
                        else:
                            e.wait_ge(sems[s], v)
                    if fn is None:
                        continue
                    h = fn(e)
                    if h is not None and val is not None and val in waited:
                        h.then_inc(sems[name], 1)
            return body
        for name, attr in (("t", "tensor"), ("v", "vector"), ("s", "scalar"),
                           ("g", "gpsimd"), ("y", "sync")):
            getattr(block, attr)(mkbody(name, self.items[name]))


def W(*toks):
    return [t for t in toks if t is not None]


def build_program(variant=4):
    nc = bass.Bass("TRN2", target_bir_lowering=False)

    def din(name, shape, dt=F32):
        return nc.dram_tensor(name, shape, dt, kind="ExternalInput").ap()

    x_tm = din("x_tm", [N, D])
    x_own = din("x_own", [NOWN, D])
    xT = din("xT", [D, N])
    wq_d = din("wq", [D, 512])
    wk_d = din("wk", [D, 512])
    wv_d = din("wv", [D, 512])
    wb_d = din("wb", [D, 2])
    wo_d = din("wo", [D, D])
    wgT_d = din("wgT", [D, IF])
    wuT_d = din("wuT", [D, IF])
    wdT_d = din("wdT", [IF, D])
    wmod_d = din("wmod", [D, 6 * D])
    cvec_d = din("cvec", [128, 8])
    gnorm_d = din("gnorm", [128, 8])
    tokbase_d = din("tok_base", [1, 1], I32)
    out_d = nc.dram_tensor("out", [NOWN, D], F32, kind="ExternalOutput").ap()
    coll_in = nc.dram_tensor("coll_in", [512, N], BF16, kind="Internal",
                             addr_space="Local").ap()
    coll_out = nc.dram_tensor("coll_out", [2, 512, N], BF16, kind="Internal",
                              addr_space="Local").ap()
    wg8_d = nc.dram_tensor("wg8_d", [128, 8, IF], FP8, kind="Internal",
                           addr_space="Local").ap()
    wu8_d = nc.dram_tensor("wu8_d", [128, 8, IF], FP8, kind="Internal",
                           addr_space="Local").ap()

    p = Prog(nc)
    with ExitStack() as stA:
        p.make_sems(stA)

        def sbA(name, shape, dt=F32):
            return stA.enter_context(nc.sbuf_tensor(name, shape, dt))

        Wq = sbA("Wq", [128, 8, 512], BF16)
        Wk = sbA("Wk", [128, 8, 512], BF16)
        Wv = sbA("Wv", [128, 8, 512], BF16)
        Wbt = sbA("Wbt", [128, 8, 2], BF16)
        Wo = sbA("Wo", [128, 8, 1024], BF16)
        Wd8 = sbA("Wd8", [128, 32, 1024], FP8)
        qaug = sbA("qaug", [1, 1024], BF16)
        kaug = sbA("kaug", [1, 1024], BF16)
        vaug = sbA("vaug", [1, 1024], BF16)
        baug = sbA("baug", [1, 4], BF16)
        eyeb = sbA("eyeb", [128, 128], BF16)
        eyef = sbA("eyef", [128, 128], F32)
        mask_sl = sbA("mask_sl", [128, 128], BF16)
        mask_ule = sbA("mask_ule", [128, 128], BF16)
        ones_c = sbA("ones_c", [128, 1], BF16)
        ones_cf = sbA("ones_cf", [128, 1], F32)
        nones_c = sbA("nones_c", [128, 1], BF16)
        ones_rf = sbA("ones_rf", [1, 128], F32)
        gdisc = sbA("gdisc", [128, 4], F32)
        g2bc = sbA("g2bc", [128, D], F32)
        s2bc = sbA("s2bc", [128, D], F32)
        gbbc = sbA("gbbc", [128, D], F32)
        gabc = sbA("gabc", [128, D], F32)
        gncol = sbA("gncol", [128, 8], F32)
        sagcol = sbA("sagcol", [128, 8], BF16)
        g1col = sbA("g1col", [128, 8], F32)
        wsc = sbA("wsc", [128, 8], F32)
        nthr = sbA("nthr", [128, 4], F32)
        tb_s = sbA("tb_s", [1, 1], I32)
        cv = sbA("cv", [128, 8], F32)
        silc = sbA("silc", [128, 8], BF16)
        eps_c = sbA("eps_c", [128, 1], F32)
        gatescr = sbA("gatescr", [1, 1], F32)

        pb = [stA.enter_context(nc.psum_tensor(f"pb{i}", [128, 512], F32))
              for i in range(8)]

        def MM(bank, sl, lhsT, rhs, start, stop, waits=(), inc=True):
            return p.do("t", "matmul", pb[bank][:, sl], lhsT, rhs, waits=waits,
                        start=start, stop=stop, inc=inc)

        def MMr(bank, psl, sl, lhsT, rhs, start, stop, waits=(), inc=True):
            return p.do("t", "matmul", pb[bank][psl, sl], lhsT, rhs, waits=waits,
                        start=start, stop=stop, inc=inc)

        def TRb(bank, bsl, in_, waits=(), inc=True):
            return p.do("t", "transpose", pb[bank].bitcast(BF16)[:, bsl], in_,
                        eyeb[:], waits=waits, inc=inc)

        stI = ExitStack()
        ri = stI.enter_context(nc.sbuf_tensor("ri", [128, 128], I32))
        ci = stI.enter_context(nc.sbuf_tensor("ci", [128, 128], I32))
        tio = p.do("g", "iota", ri[:], [[0, 128]], channel_multiplier=1)
        tio = p.do("g", "iota", ci[:], [[1, 128]], channel_multiplier=0)
        EYEB = p.do("v", "tensor_tensor", eyeb[:], ri[:], ci[:], ALU.is_equal,
                    waits=[tio])
        EYEF = p.do("v", "tensor_tensor", eyef[:], ri[:], ci[:], ALU.is_equal)
        MSL = p.do("v", "tensor_tensor", mask_sl[:], ri[:], ci[:], ALU.is_gt)
        MULE = p.do("v", "tensor_tensor", mask_ule[:], ri[:], ci[:], ALU.is_le)
        ONESC = p.do("v", "memset", ones_c[:], 1.0)
        ONESCF = p.do("v", "memset", ones_cf[:], 1.0)
        NONESC = p.do("v", "memset", nones_c[:], -1.0)
        ONESRF = p.do("v", "memset", ones_rf[:], 1.0)
        tg = p.do("v", "memset", gdisc[:, 0:1], 0.5)
        tg = p.do("v", "tensor_scalar", gdisc[:, 0:1], gdisc[:, 0:1], C32, -C32, ALU.add,
                  ALU.add, waits=[tg])
        GDISC = p.do("v", "tensor_scalar", gdisc[:, 0:1], gdisc[:, 0:1], -2.0, 1.0, ALU.mult,
                     ALU.add, waits=[tg])
        GDISC = p.do("v", "tensor_scalar", gdisc[:, 1:2], gdisc[:, 0:1], -1.0,
                     1.0, ALU.mult, ALU.add, waits=[GDISC])
        GDISC = p.do("v", "tensor_scalar", gdisc[:, 2:3], gdisc[:, 0:1], -C32,
                     None, ALU.mult, waits=[GDISC])
        GDISC = p.do("v", "tensor_scalar", gdisc[:, 3:4], gdisc[:, 1:2], -C64,
                     None, ALU.mult, waits=[GDISC])
        GDISC = p.do("v", "tensor_tensor", gdisc[:, 2:3], gdisc[:, 2:3],
                     gdisc[:, 3:4], ALU.add, waits=[GDISC])
        stI.close()
        EPSC = p.do("v", "memset", eps_c[:], EPS)
        d_tb = p.dma("yd", "tbq", tb_s[:], tokbase_d)
        d_cv = p.dma("yd", "cvq", cv[:], cvec_d)
        t_sc = p.do("s", "activation", cv[:], cv[:], AF.Silu, waits=[d_cv])
        t_scb = p.do("v", "tensor_copy", silc[:], cv[:], waits=[t_sc])
        gn = p.dma("yd", "gnq", gncol[:], gnorm_d)

        b7busy = [None]

        with ExitStack() as stP:
            def sbP(name, shape, dt=F32):
                return stP.enter_context(nc.sbuf_tensor(name, shape, dt))

            modrow = sbP("modrow", [1, 6 * D], F32)
            garow = sbP("garow", [1, D], F32)
            gbrow = sbP("gbrow", [1, D], F32)
            magrow = sbP("magrow", [1, D], F32)
            g1row = sbP("g1row", [1, D], F32)
            g2row = sbP("g2row", [1, D], F32)
            sagrow = sbP("sagrow", [1, D], F32)
            sagf = sbP("sagf", [128, 8], F32)
            wmsb = [sbP(f"wmsb{i}", [128, 8, 512], BF16) for i in range(2)]
            wstb = [sbP(f"wstb{i}", [128, 8, 512], BF16) for i in range(2)]
            wotf = [sbP(f"wotf{i}", [128, 1024], F32) for i in range(2)]
            wbstb = sbP("wbstb", [128, 8, 2], BF16)

            wm_free = [None, None]
            last_ev = None
            for nb in range(12):
                pr = nb % 2
                src = wmod_d.rearrange("(c p) m -> p c m", p=128)[:, :, nb * 512:(nb + 1) * 512]
                d = p.dma("gd", f"wmod{pr}", wmsb[pr][:], src, waits=W(wm_free[pr]))
                m = None
                for kc in range(8):
                    m = MMr(7, slice(0, 1), slice(0, 512), silc[:, kc:kc + 1],
                            wmsb[pr][:, kc, :], start=(kc == 0), stop=(kc == 7),
                            waits=W(t_scb, d, b7busy[0]) if kc == 0 else ())
                last_ev = p.do("s", "copy", modrow[0:1, nb * 512:(nb + 1) * 512],
                               pb[7][0:1, 0:512], waits=[m])
                b7busy[0] = last_ev
                wm_free[pr] = m
            MOD = last_ev

            def mrow(k):
                return modrow[0:1, k * D:(k + 1) * D]

            def clamp_gate(k, dst, after):
                t1 = p.do("v", "tensor_scalar", dst, mrow(k), 0.0, None, ALU.is_ge,
                          waits=W(after))
                t1 = p.do("v", "tensor_scalar", dst, dst, 2.0, -1.0, ALU.mult,
                          ALU.add, waits=[t1])
                t2 = p.do("v", "scalar_tensor_tensor", magrow[:], mrow(k), -1.0,
                          mrow(k), ALU.mult, ALU.max, waits=W(after))
                t2 = p.do("v", "tensor_scalar", magrow[:], magrow[:], 0.25, None,
                          ALU.max, waits=[t2])
                return p.do("v", "tensor_tensor", dst, dst, magrow[:], ALU.mult,
                            waits=[t1, t2])

            t_ga = clamp_gate(2, garow[:], MOD)
            t_gb = clamp_gate(5, gbrow[:], t_ga)
            t_g1r = p.do("v", "tensor_scalar", g1row[:], mrow(1), 1.0, None,
                         ALU.add, waits=[MOD])
            t_g2r = p.do("v", "tensor_scalar", g2row[:], mrow(4), 1.0, None,
                         ALU.add, waits=[MOD])

            def row_to_cols(row_ap, dst, after):
                m = None
                for j in range(8):
                    m = p.do("t", "transpose", pb[7][:, j:j + 1],
                             row_ap[0:1, j * 128:(j + 1) * 128], eyef[0:1, 0:1],
                             waits=W(after, EYEF, b7busy[0]) if j == 0 else ())
                ev = p.do("s", "copy", dst, pb[7][:, 0:8], waits=[m])
                b7busy[0] = ev
                return ev

            t_g1c = row_to_cols(g1row, g1col[:, :], t_g1r)
            t_r = p.do("v", "reciprocal", sagrow[:], g1row[:], waits=[t_g1r])
            t_r = p.do("v", "tensor_tensor", sagrow[:], sagrow[:], mrow(0),
                       ALU.mult, waits=[t_r])
            t_sagc = row_to_cols(sagrow, sagf[:, :], t_r)
            t_sagb = p.do("v", "tensor_copy", sagcol[:], sagf[:], waits=[t_sagc])

            def bcast_tile(row_ap, dst, after):
                ev = None
                for h in range(2):
                    m = p.do("t", "matmul", pb[7][:, 0:512], ones_rf[:],
                             row_ap[0:1, h * 512:(h + 1) * 512], start=True,
                             stop=True, waits=W(after, ONESRF, b7busy[0]))
                    ev = p.do("s", "copy", dst[:, h * 512:(h + 1) * 512],
                              pb[7][:, 0:512], waits=[m])
                    b7busy[0] = ev
                return ev

            t_g2bc = bcast_tile(g2row, g2bc, t_g2r)
            t_s2bc = bcast_tile(modrow[0:1, 3 * D:4 * D], s2bc, MOD)
            t_gbbc = bcast_tile(gbrow, gbbc, t_gb)
            t_gabc = bcast_tile(garow, gabc, t_ga)

            wfree = [None, None]

            def fold_qkv(widx, dram, Wt, augt):
                pr = widx % 2
                d = p.dma("gd", f"wst{pr}", wstb[pr][:],
                          dram.rearrange("(c p) m -> p c m", p=128),
                          waits=W(wfree[pr]))
                tt = None
                for kc in range(8):
                    tt = p.do("v", "tensor_scalar", Wt[:, kc, :], wstb[pr][:, kc, :],
                              g1col[:, kc:kc + 1], None, ALU.mult,
                              waits=W(d, t_g1c) if kc == 0 else ())
                wfree[pr] = tt
                m = None
                for kc in range(8):
                    m = MMr(7, slice(0, 1), slice(0, 512), ones_c[:], Wt[:, kc, :],
                            start=(kc == 0), stop=(kc == 7),
                            waits=W(tt, ONESC, b7busy[0]) if kc == 0 else ())
                e1 = p.do("v", "tensor_copy", augt[0:1, 0:512], pb[7][0:1, 0:512],
                          waits=[m])
                b7busy[0] = e1
                m = None
                for kc in range(8):
                    m = MMr(7, slice(0, 1), slice(0, 512), sagcol[:, kc:kc + 1],
                            Wt[:, kc, :], start=(kc == 0), stop=(kc == 7),
                            waits=W(t_sagb, b7busy[0]) if kc == 0 else ())
                e2 = p.do("v", "tensor_copy", augt[0:1, 512:1024],
                          pb[7][0:1, 0:512], waits=[m])
                b7busy[0] = e2
                return e2

            t_wq = fold_qkv(0, wq_d, Wq, qaug)
            t_wk = fold_qkv(1, wk_d, Wk, kaug)
            t_wv = fold_qkv(2, wv_d, Wv, vaug)

            d = p.dma("gd", "wst0", wbstb[:], wb_d.rearrange("(c p) m -> p c m", p=128), waits=W(wfree[0]))
            tt = None
            for kc in range(8):
                tt = p.do("v", "tensor_scalar", Wbt[:, kc, :], wbstb[:, kc, :],
                          g1col[:, kc:kc + 1], None, ALU.mult,
                          waits=W(d, t_g1c) if kc == 0 else ())
            m = None
            for kc in range(8):
                m = MMr(7, slice(0, 1), slice(0, 2), ones_c[:], Wbt[:, kc, :],
                        start=(kc == 0), stop=(kc == 7),
                        waits=W(tt, b7busy[0]) if kc == 0 else ())
            e1 = p.do("v", "tensor_copy", baug[0:1, 0:2], pb[7][0:1, 0:2], waits=[m])
            b7busy[0] = e1
            m = None
            for kc in range(8):
                m = MMr(7, slice(0, 1), slice(0, 2), sagcol[:, kc:kc + 1],
                        Wbt[:, kc, :], start=(kc == 0), stop=(kc == 7),
                        waits=W(t_sagb, b7busy[0]) if kc == 0 else ())
            t_wb = p.do("v", "tensor_copy", baug[0:1, 2:4], pb[7][0:1, 0:2], waits=[m])
            b7busy[0] = t_wb

            wofree = [None, None]
            t_wo = None
            for kc in range(8):
                pr = kc % 2
                d = p.dma("yd", f"wo{pr}", wotf[pr][:], wo_d[kc * 128:(kc + 1) * 128, :],
                          waits=W(wofree[pr]))
                u = p.do("v", "tensor_scalar", wotf[pr][:], wotf[pr][:],
                         gncol[:, kc:kc + 1], None, ALU.mult, waits=[d, gn])
                t_wo = p.do("v", "tensor_tensor", Wo[:, kc, :], wotf[pr][:],
                            gabc[:], ALU.mult, waits=[u, t_gabc])
                wofree[pr] = t_wo

            PRO_GATE = [t_wq, t_wk, t_wv, t_wb, t_wo, t_g2bc, t_s2bc, t_gbbc,
                        t_sagb, MOD, t_gabc, t_sagc]

        PROJ_RDY = [t_wq, t_wk, t_wv, t_wb]
        GATEP = p.do("v", "memset", gatescr[:], 0.0, waits=PRO_GATE)

        with ExitStack() as stB:
            def sbB(name, shape, dt=F32):
                return stB.enter_context(nc.sbuf_tensor(name, shape, dt))

            xTb = [sbB(f"xTb{i}", [128, 8, TW], BF16) for i in range(2)]
            xtmt = sbB("xtmt", [128, 2, D], F32)
            qT = [sbB(f"qT{i}", [128, 4, TW], BF16) for i in range(2)]
            kT = [sbB(f"kT{i}", [128, 4, TW], BF16) for i in range(2)]
            vt = [sbB(f"vt{i}", [128, 4, TW], BF16) for i in range(2)]
            sqt = sbB("sqt", [128, TW], BF16)
            osq = sqt
            augT_ = sbB("augT_", [1, 2 * TW], BF16)
            augT = [augT_, augT_]
            rows_ = sbB("rows_", [1, 10 * TW], F32)
            rowsb = sbB("rowsb", [1, 4 * TW], F32)
            bnsS = sbB("bnsS", [128, 12], F32)
            bnaS = sbB("bnaS", [128, 8], F32)
            ost_ = sbB("ost_", [128, 4, TW], BF16)
            ost = [ost_, ost_]
            Pt = sbB("Pt", [128, 2, 128], BF16)
            Zt = sbB("Zt", [128, 2, 128], BF16)
            Rt = sbB("Rt", [128, 2, 128], BF16)
            Mtt = sbB("Mtt", [128, 2, 128], BF16)
            Ktm = sbB("Ktm", [128, 2, 256], BF16)
            Kkap = sbB("Kkap", [128, 2, 256], BF16)
            Wpt = sbB("Wpt", [128, 4, 128], BF16)
            Ut = sbB("Ut", [128, 256], BF16)
            Sst = [sbB(f"Sst{h}", [128, 512], BF16) for h in range(2)]
            Sf32 = [sbB(f"Sf32{h}", [128, 512], F32) for h in range(2)]
            statc = sbB("statc", [128, 4, 4], F32)
            rowsc = sbB("rowsc", [1, 128], F32)
            rbc = sbB("rbc", [128, 128], BF16)
            ffst_ = sbB("ffst_", [128, IF // 2], BF16)
            ffst = [ffst_, ffst_]
            tpos = sbB("tpos", [128, IF], BF16)
            absdmp = tpos
            w8tmp = sbB("w8tmp", [128, IF], FP8)
            tneg = w8tmp
            absacc = sbB("absacc", [128, 35], F32)

            RM, RVAR, RSIG, RINV = 0, 1, 2, 3
            RB0, RNN0, REQ0 = 4, 6, 8

            def R(pr, r, n=1, cs=None):
                if cs is None:
                    return rows_[0:1, r * TW:(r + n) * TW]
                return rows_[0:1, r * TW + cs.start:r * TW + cs.stop]

            def Rb(r, n=1, cs=None):
                if cs is None:
                    return rowsb[0:1, r * TW:(r + n) * TW]
                return rowsb[0:1, r * TW + cs.start:r * TW + cs.stop]

            bg = []
            ffree = [None, None]
            abs_done = {}
            abs_last = {}
            dmp_last = [None]
            tern_last = [None]
            w8_free = [None]
            GATE_BG = [None]

            def mk_abs(mi, dram, nch, chw):
                def mkgo(j, hh):
                    def go():
                        pr = 0
                        c2 = chw // 2
                        dd = p.dma("gd", "ffst", ffst[pr][:, 0:c2],
                                   dram[j * 128:(j + 1) * 128,
                                        hh * c2:(hh + 1) * c2],
                                   waits=W(ffree[pr]))
                        col = j if hh == 0 else 32
                        a = p.do("s", "activation", absdmp[:, 0:c2],
                                 ffst[pr][:, 0:c2], AF.Abs,
                                 accum_out=absacc[:, col:col + 1],
                                 waits=W(dd, dmp_last[0],
                                         abs_done.get(mi - 1) if mi > 0 else None,
                                         abs_last.get(mi)))
                        dmp_last[0] = a
                        ffree[pr] = a
                        if hh == 1:
                            a = p.do("v", "tensor_tensor", absacc[:, j:j + 1],
                                     absacc[:, j:j + 1], absacc[:, 32:33],
                                     ALU.add, waits=[a, abs_last.get(mi)])
                        abs_last[mi] = a
                    return go
                for j in range(nch):
                    for hh in range(2):
                        bg.append(mkgo(j, hh))

                def fin():
                    r = p.do("v", "tensor_reduce", absacc[:, 33:34],
                             absacc[:, 0:nch], AX, ALU.add,
                             waits=[abs_last[mi]])
                    mm_ = p.do("t", "matmul", pb[7][0:1, 0:1], ones_cf[:],
                               absacc[:, 33:34], start=True, stop=True,
                               waits=W(r, ONESCF, b7busy[0]))
                    nelem = float(D * IF)
                    s1 = p.do("v", "tensor_scalar", absacc[0:1, 34:35],
                              pb[7][0:1, 0:1], 1.0 / nelem, 1e-5, ALU.mult,
                              ALU.max, waits=[mm_])
                    b7busy[0] = s1
                    mm2 = p.do("t", "matmul", pb[7][:, 4:5], ones_rf[:],
                               absacc[0:1, 34:35], start=True, stop=True,
                               waits=W(s1, b7busy[0]))
                    e1 = p.do("v", "tensor_copy", wsc[:, 3 + mi:4 + mi],
                              pb[7][:, 4:5], waits=[mm2])
                    b7busy[0] = e1
                    w1 = p.do("v", "tensor_scalar", wsc[:, mi:mi + 1],
                              wsc[:, 3 + mi:4 + mi], 0.5, None, ALU.mult,
                              waits=[e1])
                    w2 = p.do("v", "tensor_scalar", nthr[:, mi:mi + 1],
                              wsc[:, mi:mi + 1], -1.0, None, ALU.mult, waits=[w1])
                    abs_done[mi] = w2
                bg.append(fin)

            def mk_tern(mi, dram, dst_sb, dst_dram, nch, chw):
                def mkgo(j, hh):
                    def go():
                        pr = 0
                        c2 = chw // 2
                        dd = p.dma("gd", "ffst", ffst[pr][:, 0:c2],
                                   dram[j * 128:(j + 1) * 128,
                                        hh * c2:(hh + 1) * c2],
                                   waits=W(ffree[pr]))
                        a1 = p.do("v", "tensor_scalar", tpos[:, 0:c2],
                                  ffst[pr][:, 0:c2], wsc[:, mi:mi + 1], None,
                                  ALU.is_gt, waits=W(dd, abs_done[mi], tern_last[0]))
                        a2 = p.do("v", "tensor_scalar", w8tmp[:, 0:c2],
                                  ffst[pr][:, 0:c2], nthr[:, mi:mi + 1], None,
                                  ALU.is_lt, waits=W(dd, w8_free[0]))
                        if dst_sb is not None:
                            a3 = p.do("v", "tensor_tensor",
                                      dst_sb[:, j, hh * c2:(hh + 1) * c2],
                                      tpos[:, 0:c2], w8tmp[:, 0:c2],
                                      ALU.subtract, waits=[a1, a2])
                            w8_free[0] = a3
                        else:
                            a3 = p.do("v", "tensor_tensor",
                                      w8tmp[:, 0:c2], tpos[:, 0:c2],
                                      w8tmp[:, 0:c2], ALU.subtract,
                                      waits=[a1, a2])
                            a4 = p.dma("gd", "w8out",
                                       dst_dram[:, j, hh * c2:(hh + 1) * c2],
                                       w8tmp[:, 0:c2], waits=[a3])
                            w8_free[0] = a4
                        ffree[pr] = a3
                        tern_last[0] = a3
                    return go
                for j in range(nch):
                    for hh in range(2):
                        bg.append(mkgo(j, hh))

            mk_abs(0, wgT_d, 8, IF)
            mk_tern(0, wgT_d, None, wg8_d, 8, IF)
            mk_abs(1, wuT_d, 8, IF)
            mk_tern(1, wuT_d, None, wu8_d, 8, IF)
            mk_abs(2, wdT_d, 32, 1024)
            mk_tern(2, wdT_d, Wd8, None, 32, 1024)

            def fin_muxd():
                t1 = p.do("v", "tensor_tensor", wsc[:, 6:7], wsc[:, 4:5],
                          wsc[:, 5:6], ALU.mult, waits=[abs_done[1], abs_done[2]])
                t1 = p.do("v", "tensor_scalar", wsc[:, 6:7], wsc[:, 6:7],
                          1.0 / 127.0, None, ALU.mult, waits=[t1])
                abs_done[3] = t1
            bg.append(fin_muxd)
            bgi = [0]

            def run_bg(k):
                n = 0
                while n < k and bgi[0] < len(bg):
                    bg[bgi[0]]()
                    bgi[0] += 1
                    n += 1

            GATE0 = GATEP
            ffree[0] = GATE0
            ffree[1] = GATE0
            dmp_last[0] = GATE0
            tern_last[0] = GATE0
            w8_free[0] = GATE0
            xT_free = [GATE0, GATE0]
            xtm_free = [GATE0]
            q_free = [GATE0, GATE0]
            k_free = [GATE0, GATE0]
            v_free = [GATE0, GATE0]
            aug_free = [GATE0, GATE0]
            rows_free = [GATE0]
            ost_free = [GATE0]
            bn_last = [None]
            bna_free = [None]
            big_free = [None, None]
            ch_free = [(), ()]
            b6_free = [None]
            sq_last = [None]
            osq_free = [None]
            Pt_free = [GATE0, GATE0]
            Zt_free = [GATE0, GATE0]
            Rt_free = [GATE0, GATE0]
            Mt_free = [GATE0, GATE0]
            Ktm_free = [GATE0, GATE0]
            Kkap_free = [GATE0, GATE0]
            Wpt_free = [GATE0, GATE0]
            Ut_free = [GATE0]
            rowsc_free = [GATE0]
            rbc_free = [GATE0]
            statc_free = [GATE0]
            chain_tok = [None, None]
            state_guard = [None, None]

            for i in (range(NT) if variant >= 1 else []):
                pr = i % 2
                t0 = i * TW
                dx = p.dma("gd", f"xTb{pr}", xTb[pr][:],
                           xT.rearrange("(c p) t -> p c t", p=128)[:, :, t0:t0 + TW],
                           waits=W(xT_free[pr]))
                bt = None
                bn_prev = [xtm_free[0], xtm_free[0]]
                for c_ in range(CPT):
                    sl2 = c_ % 2
                    dxm = p.dma("yd", f"xtm{sl2}", xtmt[:, sl2, :],
                                x_tm[t0 + c_ * 128:t0 + (c_ + 1) * 128, :],
                                waits=W(bn_prev[sl2]))
                    b1 = p.do("v", "bn_stats", bnsS[:, 0:6], xtmt[:, sl2, 0:512],
                              waits=W(dxm, bn_last[0]))
                    b1 = p.do("v", "bn_stats", bnsS[:, 6:12], xtmt[:, sl2, 512:1024],
                              waits=[b1])
                    b2 = p.do("v", "bn_aggr", bnaS[:, 2 * c_:2 * c_ + 2], bnsS[:],
                              waits=W(b1, bna_free[0] if c_ == 0 else None))
                    bn_last[0] = b2
                    bn_prev[sl2] = b1
                    bt = b2
                xtm_free[0] = bt
                m = None
                for c_ in range(CPT):
                    m = p.do("t", "transpose", pb[7][0:1, c_ * 128:(c_ + 1) * 128],
                             bnaS[:, 2 * c_:2 * c_ + 1], eyef[:],
                             waits=W(bt, EYEF, b7busy[0]) if c_ == 0 else ())
                ev = p.do("s", "copy", R(pr, RM), pb[7][0:1, 0:512],
                          waits=W(m, rows_free[0]))
                b7busy[0] = ev
                rows_free[0] = None
                m = None
                for c_ in range(CPT):
                    m = p.do("t", "transpose", pb[7][0:1, c_ * 128:(c_ + 1) * 128],
                             bnaS[:, 2 * c_ + 1:2 * c_ + 2], eyef[:],
                             waits=W(b7busy[0]) if c_ == 0 else ())
                bna_free[0] = m
                ev = p.do("s", "copy", R(pr, RVAR), pb[7][0:1, 0:512], waits=[m])
                b7busy[0] = ev
                t_sig = p.do("s", "activation", R(pr, RSIG), R(pr, RVAR), AF.Sqrt,
                             bias=eps_c[0:1, 0:1], waits=W(ev, EPSC))
                t_inv = p.do("v", "reciprocal", R(pr, RINV), R(pr, RSIG),
                             waits=[t_sig])
                a1 = p.do("v", "tensor_scalar", augT[pr][0:1, 0:TW],
                          R(pr, RM), -1.0, None, ALU.mult,
                          waits=W(ev, aug_free[pr]))
                aug_free[pr] = None
                a2 = p.do("v", "tensor_copy", augT[pr][0:1, TW:2 * TW],
                          R(pr, RSIG), waits=[t_sig])
                AUG = [a1, a2]

                def proj_fm(Wt, waug, dstT, dst_free_tok, sign_ones, row0, sgn_tok):
                    last_tok = None
                    for oc in range(4):
                        bk = oc % 2
                        m = None
                        for kc in range(8):
                            m = MM(bk, slice(0, 512),
                                   Wt[:, kc, oc * 128:(oc + 1) * 128],
                                   xTb[pr][:, kc, :], start=(kc == 0), stop=False,
                                   waits=W(dx, *PROJ_RDY, big_free[bk], *AUG)
                                   if kc == 0 else ())
                        m = MM(bk, slice(0, 512),
                               waug[0:1, oc * 128:(oc + 1) * 128],
                               augT[pr][0:1, 0:TW], start=False, stop=False)
                        m = MM(bk, slice(0, 512),
                               waug[0:1, 512 + oc * 128:512 + (oc + 1) * 128],
                               augT[pr][0:1, TW:2 * TW], start=False, stop=True)
                        e = p.do("s", "copy", dstT[:, oc, :], pb[bk][:, 0:512],
                                 waits=W(m, dst_free_tok if oc == 0 else None))
                        sq = p.do("s", "activation", sqt[:], pb[bk][:, 0:512],
                                  AF.Square, waits=W(sq_last[0]))
                        mm2 = MMr(7, slice(0, 1), slice(0, 512), sign_ones, sqt[:],
                                  start=(oc % 2 == 0), stop=(oc % 2 == 1),
                                  waits=W(sq, b7busy[0], sgn_tok)
                                  if oc % 2 == 0 else [sq])
                        big_free[bk] = sq
                        sq_last[0] = mm2
                        if oc % 2 == 1:
                            rr = row0 + oc // 2
                            e2 = p.do("v", "tensor_copy", R(pr, rr),
                                      pb[7][0:1, 0:512], waits=[mm2])
                            b7busy[0] = e2
                            last_tok = e2
                    return last_tok

                qsq = proj_fm(Wq, qaug, qT[pr], q_free[pr], ones_c[:], REQ0, ONESC)
                q_free[pr] = None
                tq = p.do("v", "tensor_scalar", R(pr, REQ0, 2), R(pr, REQ0, 2),
                          EPS, None, ALU.mult, waits=[qsq])
                ksq = proj_fm(Wk, kaug, kT[pr], k_free[pr], nones_c[:], RNN0,
                              NONESC)
                k_free[pr] = None

                tb1 = None
                for hh in range(2):
                    m = None
                    for kc in range(8):
                        m = MMr(7, slice(0, 1), slice(0, 512),
                                Wbt[:, kc, hh:hh + 1],
                                xTb[pr][:, kc, :], start=(kc == 0), stop=False,
                                waits=W(b7busy[0]) if kc == 0 else ())
                    m = MMr(7, slice(0, 1), slice(0, 512), baug[0:1, hh:hh + 1],
                            augT[pr][0:1, 0:TW], start=False, stop=False)
                    m = MMr(7, slice(0, 1), slice(0, 512), baug[0:1, 2 + hh:3 + hh],
                            augT[pr][0:1, TW:2 * TW], start=False, stop=True)
                    tb1 = p.do("v", "tensor_tensor", R(pr, RB0 + hh),
                               pb[7][0:1, 0:512], R(pr, RINV), ALU.mult,
                               waits=[m, t_inv])
                    b7busy[0] = tb1
                t_bsig = p.do("s", "activation", R(pr, RB0, 2), R(pr, RB0, 2),
                              AF.Sigmoid, waits=[tb1])
                t_sn = p.do("s", "activation", R(pr, RNN0, 2), R(pr, RNN0, 2),
                            AF.Sqrt, scale=-1.0, waits=[ksq])
                t_dl = p.do("v", "reciprocal", R(pr, RNN0, 2), R(pr, RNN0, 2),
                            waits=[t_sn])
                t_bd = p.do("v", "tensor_tensor", R(pr, RB0, 2), R(pr, RB0, 2),
                            R(pr, RNN0, 2), ALU.mult, waits=[t_dl, t_bsig])
                t_ng = p.do("v", "scalar_tensor_tensor", Rb(0, 2), R(pr, RB0, 2),
                            -1.0, R(pr, RNN0, 2), ALU.mult, ALU.mult,
                            waits=[t_bd])
                t_gv0 = p.do("v", "tensor_tensor", Rb(2), R(pr, RB0),
                             R(pr, RINV), ALU.mult, waits=[t_bd])
                t_gv1 = p.do("v", "tensor_tensor", Rb(3), R(pr, RB0 + 1),
                             R(pr, RINV), ALU.mult, waits=[t_bd])

                m = None
                first = True
                for c_ in range(CPT):
                    for st_ in range(4):
                        m = p.do("t", "transpose",
                                 pb[7][:, c_ * 4 + st_:c_ * 4 + st_ + 1],
                                 Rb(st_, cs=slice(c_ * 128, (c_ + 1) * 128)),
                                 ones_cf[0:1, 0:1],
                                 waits=W(t_ng, t_gv0, t_gv1, b7busy[0],
                                         statc_free[0]) if first else ())
                        first = False
                STATC = p.do("s", "copy", statc[:].rearrange("p c m -> p (c m)"),
                             pb[7][:, 0:16], waits=[m])
                b7busy[0] = STATC
                statc_free[0] = None

                vlast = None
                for tc in range(CPT):
                    bk = tc % 2
                    m = None
                    for kc in range(8):
                        m = MM(bk, slice(0, 512),
                               xTb[pr][:, kc, tc * 128:(tc + 1) * 128],
                               Wv[:, kc, :], start=(kc == 0), stop=False,
                               waits=W(big_free[bk]) if kc == 0 else ())
                    m = MM(bk, slice(0, 512),
                           augT[pr][0:1, tc * 128:(tc + 1) * 128],
                           vaug[0:1, 0:512], start=False, stop=False)
                    m = MM(bk, slice(0, 512),
                           augT[pr][0:1, TW + tc * 128:TW + (tc + 1) * 128],
                           vaug[0:1, 512:1024], start=False, stop=True)
                    e0 = p.do("v", "tensor_scalar", vt[pr][:, tc, 0:256],
                              pb[bk][:, 0:256], statc[:, tc, 2:3], None, ALU.mult,
                              waits=W(m, STATC, v_free[pr] if tc == 0 else None))
                    e1 = p.do("v", "tensor_scalar", vt[pr][:, tc, 256:512],
                              pb[bk][:, 256:512], statc[:, tc, 3:4], None,
                              ALU.mult, waits=[m])
                    big_free[bk] = e1
                    vlast = e1
                v_free[pr] = None
                aug_free[pr] = m
                xT_free[pr] = m

                run_bg(4)

                for c_ in (range(CPT) if variant >= 2 else []):
                    sub = variant if variant < 10 else variant - 20
                    g = i * CPT + c_
                    cs = slice(c_ * 128, (c_ + 1) * 128)
                    pp = 2 + (g % 2)
                    ppo = 2 + ((g + 1) % 2)
                    solved = [None, None]
                    for h in range(2):
                        m = None
                        for jj in range(2):
                            m = MM(pp, slice(h * 128, (h + 1) * 128),
                                   kT[pr][:, 2 * h + jj, cs],
                                   kT[pr][:, 2 * h + jj, cs],
                                   start=(jj == 0), stop=(jj == 1),
                                   waits=W(*ch_free[pp - 2], ksq) if jj == 0 else ())
                        mA = m
                        m = None
                        for jj in range(2):
                            m = MM(pp, slice(256 + h * 128, 256 + (h + 1) * 128),
                                   kT[pr][:, 2 * h + jj, cs],
                                   qT[pr][:, 2 * h + jj, cs],
                                   start=(jj == 0), stop=(jj == 1),
                                   waits=W(qsq) if jj == 0 else ())
                        mM = m
                        eP = p.do("v", "scalar_tensor_tensor", Pt[:, h, :],
                                  pb[pp][:, h * 128:(h + 1) * 128],
                                  statc[:, c_, h:h + 1], mask_sl[:], ALU.mult,
                                  ALU.mult, waits=W(mA, STATC, MSL, Pt_free[h]))
                        Pt_free[h] = None
                        eM = p.do("v", "tensor_tensor", Mtt[:, h, :],
                                  pb[pp][:, 256 + h * 128:256 + (h + 1) * 128],
                                  mask_ule[:], ALU.mult,
                                  waits=W(mM, MULE, Mt_free[h]))
                        Mt_free[h] = None
                        ch_free[pp - 2] = (eM,)
                        if variant > 10 and sub < 2:
                            solved[h] = eM
                            continue
                        mz = TRb(ppo, slice(512 * h, 512 * h + 128), Pt[:, h, :],
                                 waits=W(eP, EYEB, *ch_free[ppo - 2]))
                        mk2 = TRb(ppo, slice(512 * h + 128, 512 * h + 256),
                                  kT[pr][:, 2 * h, cs], waits=())
                        mk2 = TRb(ppo, slice(512 * h + 256, 512 * h + 384),
                                  kT[pr][:, 2 * h + 1, cs], waits=())
                        eZ = p.do("s", "copy", Zt[:, h, :],
                                  pb[ppo].bitcast(BF16)[:, 512 * h:512 * h + 128],
                                  waits=W(mz, Zt_free[h]))
                        Zt_free[h] = None
                        eK = p.do("s", "copy", Ktm[:, h, :],
                                  pb[ppo].bitcast(BF16)[:, 512 * h + 128:512 * h + 384],
                                  waits=W(mk2, Ktm_free[h]))
                        Ktm_free[h] = None
                        ch_free[ppo - 2] = (eK,)
                        eKk = p.do("v", "tensor_scalar", Kkap[:, h, :],
                                   Ktm[:, h, :], statc[:, c_, h:h + 1], None,
                                   ALU.mult, waits=W(eK, Kkap_free[h]))
                        Kkap_free[h] = None
                        if variant > 10 and sub < 3:
                            solved[h] = eKk
                            continue
                        r1 = MM(6, slice(h * 128, (h + 1) * 128), eyeb[:], eyeb[:],
                                start=True, stop=False, waits=W(EYEB, b6_free[0]))
                        r2 = MM(6, slice(h * 128, (h + 1) * 128), Pt[:, h, :],
                                eyeb[:], start=False, stop=True, waits=[eP])
                        lr = p.do("s", "copy", Rt[:, h, :],
                                  pb[6][:, h * 128:(h + 1) * 128],
                                  waits=W(r2, Rt_free[h]))
                        Rt_free[h] = None
                        b6_free[0] = lr
                        lp, lz = eP, eZ
                        for lvl in range(1, 7):
                            bkl = 2 + ((g + lvl) % 2)
                            m1 = MM(bkl, slice(h * 128, (h + 1) * 128), Zt[:, h, :],
                                    Pt[:, h, :], start=True, stop=True,
                                    waits=W(lp, lz, *ch_free[bkl - 2])
                                    if lvl == 1 else ())
                            m2 = MM(bkl, slice(256 + h * 128, 256 + (h + 1) * 128),
                                    Pt[:, h, :], Zt[:, h, :], start=True, stop=True)
                            lz = p.do("s", "copy", Zt[:, h, :],
                                      pb[bkl][:, 256 + h * 128:256 + (h + 1) * 128],
                                      waits=[m2])
                            lp = p.do("v", "tensor_copy", Pt[:, h, :],
                                      pb[bkl][:, h * 128:(h + 1) * 128],
                                      waits=[lz])
                            ch_free[bkl - 2] = (lp, lz)
                            rm = MM(6, slice(h * 128, (h + 1) * 128), eyeb[:],
                                    Rt[:, h, :], start=True, stop=False,
                                    waits=[lp])
                            rm = MM(6, slice(h * 128, (h + 1) * 128), Pt[:, h, :],
                                    Rt[:, h, :], start=False, stop=True)
                            lr = p.do("s", "copy", Rt[:, h, :],
                                      pb[6][:, h * 128:(h + 1) * 128], waits=[rm])
                            b6_free[0] = lr
                        if variant > 10 and sub < 4:
                            solved[h] = lr
                            continue
                        m = MM(pp, slice(0, 128), Kkap[:, h, 0:128], Rt[:, h, :],
                               start=True, stop=True,
                               waits=W(eKk, lr, *ch_free[pp - 2]))
                        m2 = MM(pp, slice(128, 256), Kkap[:, h, 128:256],
                                Rt[:, h, :], start=True, stop=True)
                        eW = p.do("s", "copy",
                                  Wpt[:].rearrange("p c m -> p (c m)")[:, h * 256:(h + 1) * 256],
                                  pb[pp][:, 0:256], waits=W(m2, Wpt_free[h]))
                        Wpt_free[h] = None
                        ch_free[pp - 2] = (eW,)
                        Kkap_free[h] = eW
                        solved[h] = eW

                    for h in (range(2) if (variant < 10 or sub >= 4) else []):
                        mU = MM(6, slice(256, 512), Rt[:, h, :],
                                vt[pr][:, c_, h * 256:(h + 1) * 256],
                                start=True, stop=(g == 0),
                                waits=W(solved[0], solved[1], vlast, b6_free[0],
                                        chain_tok[h]))
                        if g > 0:
                            for jj in range(2):
                                mU = MM(6, slice(256, 512),
                                        Wpt[:, 2 * h + jj, :],
                                        Sst[h][:, jj * 256:(jj + 1) * 256],
                                        start=False, stop=(jj == 1))
                        eU = p.do("s", "copy", Ut[:], pb[6][:, 256:512],
                                  waits=W(mU, Ut_free[0]))
                        b6_free[0] = eU
                        mo = None
                        for dd in range(2):
                            if g > 0:
                                for jj in range(2):
                                    mo = MM(6, slice(256 + dd * 128, 256 + (dd + 1) * 128),
                                            Sst[h][:, jj * 256 + dd * 128:jj * 256 + (dd + 1) * 128],
                                            qT[pr][:, 2 * h + jj, cs],
                                            start=(jj == 0), stop=False,
                                            waits=W(eU, b6_free[0])
                                            if (dd == 0 and jj == 0) else ())
                                mo = MM(6, slice(256 + dd * 128, 256 + (dd + 1) * 128),
                                        Ut[:, dd * 128:(dd + 1) * 128], Mtt[:, h, :],
                                        start=False, stop=True)
                            else:
                                mo = MM(6, slice(256 + dd * 128, 256 + (dd + 1) * 128),
                                        Ut[:, dd * 128:(dd + 1) * 128], Mtt[:, h, :],
                                        start=True, stop=True,
                                        waits=W(eU, b6_free[0]) if dd == 0 else ())
                        sqo = p.do("s", "activation", osq[:, 0:256],
                                   pb[6][:, 256:512],
                                   AF.Square, waits=W(mo, osq_free[0], sq_last[0]))
                        sq_last[0] = sqo
                        mo2 = MMr(7, slice(0, 1), slice(0, 128), ones_c[:],
                                  osq[:, 0:128], start=True, stop=False,
                                  waits=W(sqo, b7busy[0]))
                        mo2 = MMr(7, slice(0, 1), slice(0, 128), ones_c[:],
                                  osq[:, 128:256], start=False, stop=True)
                        t1 = p.do("v", "tensor_scalar", rowsc[0:1, :],
                                  pb[7][0:1, 0:128], 1.0 / 256.0, None, ALU.mult,
                                  waits=W(mo2, rowsc_free[0]))
                        b7busy[0] = t1
                        t1 = p.do("v", "tensor_tensor", rowsc[0:1, :],
                                  rowsc[0:1, :], R(pr, REQ0 + h, cs=cs), ALU.add,
                                  waits=[t1, tq])
                        t1 = p.do("s", "activation", rowsc[0:1, :],
                                  rowsc[0:1, :], AF.Sqrt, waits=[t1])
                        t1 = p.do("v", "reciprocal", rowsc[0:1, :],
                                  rowsc[0:1, :], waits=[t1])
                        mb = p.do("t", "matmul", pb[7][:, 128:256], ones_rf[:],
                                  rowsc[0:1, :], start=True, stop=True,
                                  waits=W(t1, b7busy[0]))
                        eb = p.do("s", "copy", rbc[:], pb[7][:, 128:256],
                                  waits=W(mb, rbc_free[0]))
                        b7busy[0] = eb
                        oe = None
                        for dd in range(2):
                            oe = p.do("v", "tensor_tensor",
                                      ost[pr][:, 2 * h + dd, cs],
                                      pb[6][:, 256 + dd * 128:256 + (dd + 1) * 128],
                                      rbc[:], ALU.mult,
                                      waits=W(eb, ost_free[0]
                                              if (c_ == 0 and h == 0 and dd == 0)
                                              else None))
                        if c_ == 0 and h == 0:
                            ost_free[0] = None
                        rbc_free[0] = oe
                        rowsc_free[0] = oe
                        b6_free[0] = oe
                        osq_free[0] = mo2
                        sq_last[0] = mo2
                        ms = None
                        for jj in range(2):
                            ms = MM(4 + h, slice(jj * 256, (jj + 1) * 256),
                                    Ktm[:, h, jj * 128:(jj + 1) * 128], Ut[:],
                                    start=(jj == 0), stop=(jj == 1),
                                    waits=W(eU, state_guard[h]) if jj == 0 else ())
                        if g == 0:
                            es = p.do("v", "tensor_copy", Sf32[h][:], pb[4 + h][:],
                                      waits=[ms])
                        else:
                            es = p.do("v", "tensor_tensor", Sf32[h][:], Sf32[h][:],
                                      pb[4 + h][:], ALU.add, waits=[ms])
                        es = p.do("s", "copy", Sst[h][:], Sf32[h][:], waits=[es])
                        state_guard[h] = es
                        Ut_free[0] = ms
                        chain_tok[h] = es
                        Ktm_free[h] = ms
                        Rt_free[h] = mU
                        Mt_free[h] = mo
                        Wpt_free[h] = mU
                    for h in range(2):
                        Pt_free[h] = chain_tok[h]
                        Zt_free[h] = chain_tok[h]

                if variant >= 2 and (variant < 10 or (variant - 20) >= 4):
                    do_ = p.dma("gd", f"ocoll{pr}",
                                coll_in.rearrange("(c p) t -> p c t", p=128)[:, :, t0:t0 + TW],
                                ost[pr][:], waits=W(chain_tok[0], chain_tok[1]))
                    ost_free[0] = do_
                q_free[pr] = chain_tok[1]
                k_free[pr] = chain_tok[1]
                v_free[pr] = chain_tok[1]
                rows_free[0] = chain_tok[1]
                statc_free[0] = chain_tok[1]
                run_bg(4)

            run_bg(len(bg))
            FFN_RDY = W(tern_last[0], w8_free[0], abs_done.get(0),
                        abs_done.get(1), abs_done.get(2), abs_done.get(3))
            DUMMY_OUT = ost[0]
            PH23 = [("t", p.cnt["t"]), ("v", p.cnt["v"]), ("s", p.cnt["s"]),
                    ("g", p.cnt["g"])]

        GATE5 = p.do("v", "memset", gatescr[:], 0.0, waits=PH23 + FFN_RDY)
        if variant >= 3:
            COLL = p.do("g", "collective_compute", "AllGather", ALU.bypass,
                        replica_groups=[[0, 1], [2, 3], [4, 5], [6, 7]],
                        ins=[coll_in], outs=[coll_out],
                        waits=[("ocoll0", p.cnt.get("ocoll0", 0)),
                               ("ocoll1", p.cnt.get("ocoll1", 0))])
        else:
            COLL = GATE5

        with ExitStack() as stC:
            def sbC(name, shape, dt=F32):
                return stC.enter_context(nc.sbuf_tensor(name, shape, dt))

            sc8k = sbC("sc8k", [128, 4096], BF16)
            oat = sc8k.rearrange("p (c t) -> p c t", c=8)
            guiT = sc8k.rearrange("p (c t) -> p c t", c=32)
            xot = sbC("xot", [128, 4, D], F32)
            xmid = sbC("xmid", [128, 4, D], F32)
            h2t = sbC("h2t", [128, D], F32)
            h2i = sbC("h2i", [128, D], BF16)
            h2iT = sbC("h2iT", [128, 4, 8, 128], BF16)
            gub = [sbC(f"gub{t}", [128, IF], BF16) for t in range(4)]
            zt_ = sbC("zt_", [128, 512], F32)
            silg = zt_
            r32t = sbC("r32t", [128, 512], F32)
            r64t = sbC("r64t", [128, 512], F32)
            outt = sbC("outt", [128, 1024], F32)
            bns2 = sbC("bns2", [128, 12], F32)
            cols2 = sbC("cols2", [128, 4, 16], F32)
            wgst = [sbC(f"wgst{t}", [128, 8, 512], FP8) for t in range(2)]
            wust_ = sbC("wust_", [128, 8, 512], FP8)
            wust = [wust_, wust_]

            r0 = stC.enter_context(nc.sync.register("r0"))
            r1 = stC.enter_context(nc.sync.register("r1"))
            p.raw("y", lambda e: e.reg_load(r0, tb_s[:1, :1]),
                  waits=[d_tb], inc=False)

            G5 = PH23[-1]
            oat_free = [None]
            xot_free = [None]
            xmid_free = [None]
            big5 = [None, None]
            ch5 = [None, None]
            h2_free = [None]
            h2i_free = [None]
            h2iT_free = [None]
            gu_free = [[None] * 4]
            guiT_free = [None]
            z_free = [None]
            r32_free = [None]
            r64_free = [None]
            cols_free = [[None] * 4]
            bn2_last = [None]
            outt_free = [None]
            wg_free = [None, None]
            wu_free = [None, None]

            for j in (range(NOT) if variant >= 4 else []):
                fvar = 99 if variant == 4 else (variant - 40 if variant > 40 else 99)
                to = j * TW

                def load_oat(e, to=to):
                    e.reg_mov(r1, to)
                    e.reg_add(r1, r1, r0)
                    in_ap = bass.AP(coll_out.tensor, r1,
                                    [[N, 128], [512 * N, 2], [128 * N, 4], [1, TW]])
                    return e.dma_start(
                        out=oat.rearrange("p s2 t -> p s2 t") if False else sc8k[:].rearrange("p (s f t) -> p s f t", s=2, f=4),
                        in_=in_ap)
                dol = p.dma_reg("yd", "oat", load_oat,
                                waits=W(COLL, GATE5, oat_free[0]))
                dxo = p.dma("yd", "xot", xot[:],
                            x_own[to:to + TW, :].rearrange("(c p) m -> p c m", p=128),
                            waits=W(COLL, GATE5, xot_free[0]))

                xm_toks = []
                e = None
                m = None
                for tc in range(CPT):
                    for nn in range(2):
                        bk = nn
                        m = None
                        for kc in range(8):
                            m = MM(bk, slice(0, 512),
                                   oat[:, kc, tc * 128:(tc + 1) * 128],
                                   Wo[:, kc, nn * 512:(nn + 1) * 512],
                                   start=(kc == 0), stop=(kc == 7),
                                   waits=W(dol, dxo, t_wo, big5[bk], *PH23)
                                   if kc == 0 else ())
                        e = p.do("v", "tensor_tensor",
                                 xmid[:, tc, nn * 512:(nn + 1) * 512],
                                 pb[bk][:, 0:512],
                                 xot[:, tc, nn * 512:(nn + 1) * 512], ALU.add,
                                 waits=W(m, xmid_free[0]
                                         if (tc == 0 and nn == 0) else None))
                        big5[bk] = e
                    xm_toks.append(e)
                xmid_free[0] = None
                oat_free[0] = m
                guiT_free[0] = m
                xot_free[0] = e
                if fvar < 2:
                    continue

                h2it_toks = []
                for tc in range(CPT):
                    cl = cols2[:, tc, :]
                    b1 = p.do("v", "bn_stats", bns2[:, 0:6], xmid[:, tc, 0:512],
                              waits=W(xm_toks[tc], bn2_last[0]))
                    b1 = p.do("v", "bn_stats", bns2[:, 6:12], xmid[:, tc, 512:1024],
                              waits=[b1])
                    b2 = p.do("v", "bn_aggr", cl[:, 0:2], bns2[:],
                              waits=W(b1, cols_free[0][tc]))
                    cols_free[0][tc] = None
                    bn2_last[0] = b2
                    s1 = p.do("s", "activation", cl[:, 2:3], cl[:, 1:2], AF.Sqrt,
                              bias=eps_c[:, 0:1], waits=W(b2, EPSC))
                    v1 = p.do("v", "reciprocal", cl[:, 3:4], cl[:, 2:3], waits=[s1])
                    v2 = p.do("v", "scalar_tensor_tensor", cl[:, 4:5], cl[:, 0:1],
                              -1.0, cl[:, 3:4], ALU.mult, ALU.mult,
                              waits=[v1, b2])
                    xh = p.do("v", "tensor_scalar", h2t[:], xmid[:, tc, :],
                              cl[:, 3:4], cl[:, 4:5], ALU.mult, ALU.add,
                              waits=W(v2, h2_free[0]))
                    h2a = p.do("v", "tensor_tensor", h2t[:], h2t[:], g2bc[:],
                               ALU.mult, waits=W(xh, t_g2bc))
                    h2a = p.do("v", "tensor_tensor", h2t[:], h2t[:], s2bc[:],
                               ALU.add, waits=W(h2a, t_s2bc))
                    am = p.do("v", "tensor_reduce", cl[:, 5:6], h2t[:], AX,
                              ALU.max, apply_absolute_value=True, waits=[h2a])
                    am = p.do("v", "tensor_scalar", cl[:, 5:6], cl[:, 5:6], 1e-5,
                              None, ALU.max, waits=[am])
                    sc = p.do("v", "reciprocal", cl[:, 6:7], cl[:, 5:6],
                              waits=[am])
                    sc = p.do("v", "tensor_scalar", cl[:, 6:7], cl[:, 6:7], 127.0,
                              None, ALU.mult, waits=[sc])
                    hi = None
                    for s2_ in range(2):
                        ss = slice(s2_ * 512, (s2_ + 1) * 512)
                        ra = p.do("v", "tensor_scalar", r32t[:], h2t[:, ss],
                                  cl[:, 6:7], C32, ALU.mult, ALU.add,
                                  waits=W(sc, r32_free[0]))
                        rb = p.do("v", "tensor_scalar", r64t[:], h2t[:, ss],
                                  cl[:, 6:7], C64, ALU.mult, ALU.add,
                                  waits=W(r64_free[0]))
                        td = p.do("v", "tensor_scalar", r64t[:], r64t[:],
                                  gdisc[:, 1:2], gdisc[:, 2:3], ALU.mult,
                                  ALU.add, waits=W(rb, GDISC))
                        hi = p.do("v", "scalar_tensor_tensor", h2i[:, ss],
                                  r32t[:], gdisc[:, 0:1], r64t[:], ALU.mult,
                                  ALU.add, waits=W(td, h2i_free[0]
                                                   if s2_ == 0 else None))
                        r32_free[0] = hi
                        r64_free[0] = hi
                    h2i_free[0] = None
                    h2_free[0] = hi
                    m = None
                    for kc in range(8):
                        m = TRb(2, slice(kc * 128, (kc + 1) * 128),
                                h2i[:, kc * 128:(kc + 1) * 128],
                                waits=W(hi, ch5[0]) if kc == 0 else ())
                    e = p.do("s", "copy",
                             h2iT[:, tc, :, :].rearrange("p c m -> p (c m)"),
                             pb[2].bitcast(BF16)[:, 0:1024],
                             waits=W(m, h2iT_free[0] if tc == 0 else None))
                    ch5[0] = e
                    h2i_free[0] = e
                    h2it_toks.append(e)
                h2iT_free[0] = None
                if fvar < 3:
                    continue

                dq_toks = []
                for tc in range(CPT):
                    cl = cols2[:, tc, :]
                    dg = p.do("v", "reciprocal", cl[:, 7:8], cl[:, 6:7],
                              waits=[h2it_toks[tc]])
                    dg = p.do("v", "tensor_tensor", cl[:, 8:9], cl[:, 7:8],
                              wsc[:, 3:4], ALU.mult, waits=W(dg, *FFN_RDY[:1]))
                    dq_toks.append(dg)

                gu_toks = [None] * 4
                mg = None
                mu_ = None
                for sl in range(8):
                    spr = sl % 2
                    dwg = p.dma("sd", f"wg{spr}", wgst[spr][:],
                                wg8_d[:, :, sl * 512:(sl + 1) * 512],
                                waits=W(wg_free[spr], *FFN_RDY))
                    dwu = p.dma("sd", "wuq", wust[spr][:],
                                wu8_d[:, :, sl * 512:(sl + 1) * 512],
                                waits=W(wu_free[0]))
                    for tc in range(CPT):
                        mg = None
                        for kc in range(8):
                            mg = MM(0, slice(0, 512), h2iT[:, tc, kc, :],
                                    wgst[spr][:, kc, :], start=(kc == 0),
                                    stop=(kc == 7),
                                    waits=W(h2it_toks[tc], dwg, big5[0])
                                    if kc == 0 else ())
                        sg_ = p.do("s", "activation", silg[:], pb[0][:, 0:512],
                                   AF.Silu, scale=cols2[:, tc, 8:9],
                                   waits=W(mg, dq_toks[tc], z_free[0]))
                        big5[0] = sg_
                        mu_ = None
                        for kc in range(8):
                            mu_ = MM(1, slice(0, 512), h2iT[:, tc, kc, :],
                                     wust[spr][:, kc, :], start=(kc == 0),
                                     stop=(kc == 7),
                                     waits=W(dwu, big5[1]) if kc == 0 else ())
                        gu_ = p.do("v", "tensor_tensor",
                                   gub[tc][:, sl * 512:(sl + 1) * 512], silg[:],
                                   pb[1][:, 0:512], ALU.mult,
                                   waits=W(mu_, sg_, gu_free[0][tc]
                                           if sl == 0 else None))
                        if sl == 0:
                            gu_free[0][tc] = None
                        big5[1] = gu_
                        z_free[0] = gu_
                        gu_toks[tc] = gu_
                    wg_free[spr] = mg
                    wu_free[0] = mu_
                if fvar < 4:
                    continue

                for tc in range(CPT):
                    cl = cols2[:, tc, :]
                    ag = p.do("v", "tensor_reduce", cl[:, 9:10], gub[tc][:], AX,
                              ALU.max, apply_absolute_value=True,
                              waits=W(*gu_toks))
                    th = p.do("v", "reciprocal", cl[:, 10:11], wsc[:, 4:5],
                              waits=W(abs_done.get(1)))
                    th = p.do("v", "tensor_tensor", cl[:, 10:11], cl[:, 10:11],
                              cl[:, 6:7], ALU.mult, waits=[th])
                    th = p.do("v", "tensor_scalar", cl[:, 10:11], cl[:, 10:11],
                              1e-5, None, ALU.mult, waits=[th])
                    me = p.do("v", "tensor_tensor", cl[:, 11:12], cl[:, 9:10],
                              cl[:, 10:11], ALU.max, waits=[ag, th])
                    s3 = p.do("v", "reciprocal", cl[:, 12:13], cl[:, 11:12],
                              waits=[me])
                    s3 = p.do("v", "tensor_scalar", cl[:, 12:13], cl[:, 12:13],
                              127.0, None, ALU.mult, waits=[s3])
                    dd_ = p.do("v", "tensor_tensor", cl[:, 13:14], cl[:, 11:12],
                               cl[:, 7:8], ALU.mult, waits=[me])
                    dd_ = p.do("v", "tensor_tensor", cl[:, 13:14], cl[:, 13:14],
                               wsc[:, 6:7], ALU.mult, waits=W(dd_))
                    cols_free[0][tc] = dd_
                    gl = None
                    for s2_ in range(8):
                        ss = slice(s2_ * 512, (s2_ + 1) * 512)
                        ra = p.do("v", "tensor_scalar", r32t[:], gub[tc][:, ss],
                                  cl[:, 12:13], C32, ALU.mult, ALU.add,
                                  waits=W(s3, r32_free[0]))
                        rb = p.do("v", "tensor_scalar", r64t[:], gub[tc][:, ss],
                                  cl[:, 12:13], C64, ALU.mult, ALU.add,
                                  waits=W(r64_free[0]))
                        td = p.do("v", "tensor_scalar", r64t[:], r64t[:],
                                  gdisc[:, 1:2], gdisc[:, 2:3], ALU.mult,
                                  ALU.add, waits=[rb])
                        gl = p.do("v", "scalar_tensor_tensor", gub[tc][:, ss],
                                  r32t[:], gdisc[:, 0:1], r64t[:], ALU.mult,
                                  ALU.add, waits=[td])
                        r32_free[0] = gl
                        r64_free[0] = gl
                    glast = None
                    for gq in range(4):
                        bq = 2 + (gq % 2)
                        m = None
                        for kc in range(8):
                            m = TRb(bq, slice(kc * 128, (kc + 1) * 128),
                                    gub[tc][:, (gq * 8 + kc) * 128:(gq * 8 + kc + 1) * 128],
                                    waits=W(gl, ch5[bq - 2]) if kc == 0 else ())
                        glast = p.do("s", "copy",
                                     guiT[:, gq * 8:(gq + 1) * 8, :].rearrange("p c m -> p (c m)"),
                                     pb[bq].bitcast(BF16)[:, 0:1024],
                                     waits=W(m, guiT_free[0] if gq == 0 else None))
                        ch5[bq - 2] = glast
                    guiT_free[0] = None
                    gu_free[0][tc] = glast
                    if fvar < 5:
                        oat_free[0] = glast
                        continue
                    o2 = None
                    mlast = None
                    for nn in range(2):
                        bk = nn
                        m = None
                        for kc in range(32):
                            m = MM(bk, slice(0, 512), guiT[:, kc, :],
                                   Wd8[:, kc, nn * 512:(nn + 1) * 512],
                                   start=(kc == 0), stop=(kc == 31),
                                   waits=W(glast, big5[bk], *FFN_RDY)
                                   if kc == 0 else ())
                        mlast = m
                        o1 = p.do("v", "scalar_tensor_tensor",
                                  outt[:, nn * 512:(nn + 1) * 512],
                                  pb[bk][:, 0:512], cl[:, 13:14],
                                  gbbc[:, nn * 512:(nn + 1) * 512], ALU.mult,
                                  ALU.mult,
                                  waits=W(m, dd_, t_gbbc, outt_free[0]
                                          if nn == 0 else None))
                        o2 = p.do("v", "tensor_tensor",
                                  outt[:, nn * 512:(nn + 1) * 512],
                                  outt[:, nn * 512:(nn + 1) * 512],
                                  xmid[:, tc, nn * 512:(nn + 1) * 512], ALU.add,
                                  waits=[o1])
                        big5[bk] = o2
                    guiT_free[0] = mlast
                    oat_free[0] = mlast
                    dout = p.dma("yd", "outq", out_d[to + tc * 128:to + (tc + 1) * 128, :],
                                 outt[:], waits=[o2])
                    outt_free[0] = dout
                xmid_free[0] = bn2_last[0]

            if variant < 4:
                p.dma("yd", "outq", out_d[0:128, :],
                      outt[:], waits=[GATE5])
            dstreams = [k for k in p.cnt if k not in ENGS]
            p.fence("y", [(k, p.cnt[k]) for k in dstreams])
            p.fence("y", [("t", p.cnt["t"]), ("v", p.cnt["v"]),
                          ("s", p.cnt["s"]), ("g", p.cnt["g"])])

            with nc.Block() as block:
                p.emit(block)
    return nc


_NC_CACHE = {}


def get_nc(variant=4):
    if variant not in _NC_CACHE:
        _NC_CACHE[variant] = build_program(variant)
    return _NC_CACHE[variant]


def make_in_maps(inputs):
    x = np.asarray(inputs["L0_attn_x"], dtype=np.float32)
    Wq_ = np.asarray(inputs["L0_attn_Wq"], dtype=np.float32)
    Wk_ = np.asarray(inputs["L0_attn_Wk"], dtype=np.float32)
    Wv_ = np.asarray(inputs["L0_attn_Wv"], dtype=np.float32)
    Wbeta = np.asarray(inputs["L0_attn_Wbeta"], dtype=np.float32)
    Wo_ = np.asarray(inputs["L0_attn_Wo"], dtype=np.float32)
    g_norm = np.asarray(inputs["L0_attn_g_norm"], dtype=np.float32)
    w_gate = np.asarray(inputs["L0_ffn_w_gate"], dtype=np.float32)
    w_up = np.asarray(inputs["L0_ffn_w_up"], dtype=np.float32)
    w_down = np.asarray(inputs["L0_ffn_w_down"], dtype=np.float32)
    c_ = np.asarray(inputs["L0_c"], dtype=np.float32)
    wmod = np.asarray(inputs["L0_wmod"], dtype=np.float32)

    wgT = np.ascontiguousarray(w_gate.T)
    wuT = np.ascontiguousarray(w_up.T)
    wdT = np.ascontiguousarray(w_down.T)
    gnt = np.ascontiguousarray(np.tile(g_norm, 4).reshape(8, 128).T)
    in_maps = []
    for cidx in range(8):
        b = cidx // 2
        hp = cidx % 2
        xb = x[b]
        in_maps.append(dict(
            x_tm=np.ascontiguousarray(xb),
            x_own=np.ascontiguousarray(xb[hp * NOWN:(hp + 1) * NOWN]),
            xT=np.ascontiguousarray(xb.T),
            wq=np.ascontiguousarray(Wq_[:, hp * 512:(hp + 1) * 512]),
            wk=np.ascontiguousarray(Wk_[:, hp * 512:(hp + 1) * 512]),
            wv=np.ascontiguousarray(Wv_[:, hp * 512:(hp + 1) * 512]),
            wb=np.ascontiguousarray(Wbeta[:, hp * 2:hp * 2 + 2]),
            wo=Wo_, wgT=wgT, wuT=wuT, wdT=wdT, wmod=wmod,
            cvec=np.ascontiguousarray(c_[b].reshape(8, 128).T),
            gnorm=gnt,
            tok_base=np.array([[hp * NOWN]], np.int32),
        ))
    return in_maps


import os
def kernel(**inputs):
    nc = get_nc(int(os.environ.get("KVARIANT", "4")))
    in_maps = make_in_maps(inputs)
    res = run_bass_kernel_spmd(nc, in_maps, core_ids=list(range(8)))
    out = np.empty((B, N, D), np.float32)
    for cidx in range(8):
        b, hp = cidx // 2, cidx % 2
        out[b, hp * NOWN:(hp + 1) * NOWN] = res.results[cidx]["out"]
    return out


# version 09 (final)
# speedup vs baseline: 1.0110x; 1.0044x over previous
import numpy as np
import ml_dtypes
import traceback
from contextlib import ExitStack

_DBG_ORIGIN = True

import concourse.bass as bass
import concourse.mybir as mybir
from concourse.bass_utils import run_bass_kernel_spmd

F32 = mybir.dt.float32
BF16 = mybir.dt.bfloat16
FP8 = mybir.dt.float8e4
I32 = mybir.dt.int32
AF = mybir.ActivationFunctionType
ALU = mybir.AluOpType
AX = mybir.AxisListType.X

B, N, D = 4, 8192, 1024
C = 128
TW = 512
NT = N // TW
CPT = TW // C
NOWN = N // 2
NOT = NOWN // TW
IF = 4096
C32 = 12582912.0
C64 = 6755399441055744.0
EPS = 1e-5

ENGS = ("t", "v", "s", "g", "y")
QUEUES = ("yd", "sd", "gd")


class Prog:

    def __init__(self, nc):
        self.nc = nc
        self.items = {e: [] for e in ENGS}
        self.cnt = {k: 0 for k in ENGS}
        self.wm = {e: {} for e in ENGS}
        self.sem = {}
        self.waited = {e: set() for e in ENGS}

    def make_sems(self, st):
        self._semstack = st
        for k in ENGS:
            self.sem[k] = st.enter_context(self.nc.semaphore("sem_" + k))

    def _stream_sem(self, stream):
        if stream not in self.sem:
            self.sem[stream] = self._semstack.enter_context(
                self.nc.semaphore("sem_" + stream))
            self.cnt[stream] = 0
        return self.sem[stream]

    def _need(self, eng, waits):
        wm = self.wm[eng]
        best = {}
        for tok in waits:
            if tok is None:
                continue
            s, v = tok
            if v is None:
                continue
            if v > best.get(s, -1):
                best[s] = v
        need = []
        for s, v in best.items():
            if wm.get(s, 0) >= v:
                continue
            wm[s] = v
            need.append((s, v))
            if s in ENGS:
                self.waited[s].add(v)
        return need

    def _add(self, eng, need, val, fn):
        self.items[eng].append((need, val, fn))

    def do(self, eng, op, *args, waits=(), inc=True, **kw):
        need = self._need(eng, waits)
        self.cnt[eng] += 1
        val = self.cnt[eng]

        def fn(e, op=op, args=args, kw=kw):
            return getattr(e, op)(*args, **kw)
        self._add(eng, need, val, fn)
        return (eng, val)

    def raw(self, eng, fn, waits=(), inc=True):
        need = self._need(eng, waits)
        self.cnt[eng] += 1
        val = self.cnt[eng]
        self._add(eng, need, val, fn)
        return (eng, val)

    def dma(self, queue, stream, out, in_, waits=()):
        eng = {"yd": "y", "sd": "s", "gd": "g"}[queue]
        self._stream_sem(stream)
        need = self._need(eng, waits)
        self.cnt[stream] += 16
        val = self.cnt[stream]
        sems = self.sem

        def fn(e, out=out, in_=in_, stream=stream):
            h = e.dma_start(out=out, in_=in_)
            h.then_inc(sems[stream], 16)
            return None
        self._add(eng, need, None, fn)
        return (stream, val)

    def dma_reg(self, queue, stream, fn0, waits=()):
        eng = {"yd": "y", "sd": "s", "gd": "g"}[queue]
        self._stream_sem(stream)
        need = self._need(eng, waits)
        self.cnt[stream] += 16
        sems = self.sem

        def fn(e, fn0=fn0, stream=stream):
            fn0(e).then_inc(sems[stream], 16)
            return None
        self._add(eng, need, None, fn)
        return (stream, self.cnt[stream])

    def fence(self, eng, waits):
        need = self._need(eng, waits)
        if not need:
            return
        self._add(eng, need, None, None)

    def emit(self, block):
        rank = {}
        for e in ENGS:
            rank[e] = {v: i + 1 for i, v in enumerate(sorted(self.waited[e]))}
        sems = self.sem

        def mkbody(name, items):
            r_all = rank
            waited = self.waited[name]
            rk = rank[name]

            def body(e):
                for need, val, fn in items:
                    for s, v in need:
                        if s in ENGS:
                            e.wait_ge(sems[s], r_all[s][v])
                        else:
                            e.wait_ge(sems[s], v)
                    if fn is None:
                        continue
                    h = fn(e)
                    if h is not None and val is not None and val in waited:
                        h.then_inc(sems[name], 1)
            return body
        for name, attr in (("t", "tensor"), ("v", "vector"), ("s", "scalar"),
                           ("g", "gpsimd"), ("y", "sync")):
            getattr(block, attr)(mkbody(name, self.items[name]))


def W(*toks):
    return [t for t in toks if t is not None]


def build_program(variant=4):
    nc = bass.Bass("TRN2", target_bir_lowering=False)

    def din(name, shape, dt=F32):
        return nc.dram_tensor(name, shape, dt, kind="ExternalInput").ap()

    x_tm = din("x_tm", [N, D])
    x_own = din("x_own", [NOWN, D])
    xT = din("xT", [D, N])
    wq_d = din("wq", [D, 512])
    wk_d = din("wk", [D, 512])
    wv_d = din("wv", [D, 512])
    wb_d = din("wb", [D, 2])
    wo_d = din("wo", [D, D])
    wgT_d = din("wgT", [D, IF])
    wuT_d = din("wuT", [D, IF])
    wdT_d = din("wdT", [IF, D])
    wmod_d = din("wmod", [D, 6 * D])
    cvec_d = din("cvec", [128, 8])
    gnorm_d = din("gnorm", [128, 8])
    tokbase_d = din("tok_base", [1, 1], I32)
    out_d = nc.dram_tensor("out", [NOWN, D], F32, kind="ExternalOutput").ap()
    coll_in = nc.dram_tensor("coll_in", [512, N], BF16, kind="Internal",
                             addr_space="Local").ap()
    coll_out = nc.dram_tensor("coll_out", [2, 512, N], BF16, kind="Internal",
                              addr_space="Local").ap()
    wg8_d = nc.dram_tensor("wg8_d", [128, 8, IF], FP8, kind="Internal",
                           addr_space="Local").ap()
    wu8_d = nc.dram_tensor("wu8_d", [128, 8, IF], FP8, kind="Internal",
                           addr_space="Local").ap()

    p = Prog(nc)
    with ExitStack() as stA:
        p.make_sems(stA)

        def sbA(name, shape, dt=F32):
            return stA.enter_context(nc.sbuf_tensor(name, shape, dt))

        Wq = sbA("Wq", [128, 8, 512], BF16)
        Wk = sbA("Wk", [128, 8, 512], BF16)
        Wv = sbA("Wv", [128, 8, 512], BF16)
        Wbt = sbA("Wbt", [128, 8, 2], BF16)
        Wo = sbA("Wo", [128, 8, 1024], BF16)
        Wd8 = sbA("Wd8", [128, 32, 1024], FP8)
        qaug = sbA("qaug", [1, 1024], BF16)
        kaug = sbA("kaug", [1, 1024], BF16)
        vaug = sbA("vaug", [1, 1024], BF16)
        baug = sbA("baug", [1, 4], BF16)
        eyeb = sbA("eyeb", [128, 128], BF16)
        eyef = sbA("eyef", [128, 128], F32)
        mask_sl = sbA("mask_sl", [128, 128], BF16)
        mask_ule = sbA("mask_ule", [128, 128], BF16)
        ones_c = sbA("ones_c", [128, 1], BF16)
        ones_cf = sbA("ones_cf", [128, 1], F32)
        nones_c = sbA("nones_c", [128, 1], BF16)
        ones_rf = sbA("ones_rf", [1, 128], F32)
        gdisc = sbA("gdisc", [128, 4], F32)
        g2bc = sbA("g2bc", [128, D], F32)
        s2bc = sbA("s2bc", [128, D], F32)
        gbbc = sbA("gbbc", [128, D], F32)
        gncol = sbA("gncol", [128, 8], F32)
        sagcol = sbA("sagcol", [128, 8], BF16)
        g1col = sbA("g1col", [128, 8], F32)
        wsc = sbA("wsc", [128, 8], F32)
        nthr = sbA("nthr", [128, 4], F32)
        tb_s = sbA("tb_s", [1, 1], I32)
        cv = sbA("cv", [128, 8], F32)
        silc = sbA("silc", [128, 8], BF16)
        eps_c = sbA("eps_c", [128, 1], F32)
        gatescr = sbA("gatescr", [1, 1], F32)

        pb = [stA.enter_context(nc.psum_tensor(f"pb{i}", [128, 512], F32))
              for i in range(8)]

        def MM(bank, sl, lhsT, rhs, start, stop, waits=(), inc=True):
            return p.do("t", "matmul", pb[bank][:, sl], lhsT, rhs, waits=waits,
                        start=start, stop=stop, inc=inc)

        def MMr(bank, psl, sl, lhsT, rhs, start, stop, waits=(), inc=True):
            return p.do("t", "matmul", pb[bank][psl, sl], lhsT, rhs, waits=waits,
                        start=start, stop=stop, inc=inc)

        def TRb(bank, bsl, in_, waits=(), inc=True):
            return p.do("t", "transpose", pb[bank].bitcast(BF16)[:, bsl], in_,
                        eyeb[:], waits=waits, inc=inc)

        stI = ExitStack()
        ri = stI.enter_context(nc.sbuf_tensor("ri", [128, 128], I32))
        ci = stI.enter_context(nc.sbuf_tensor("ci", [128, 128], I32))
        tio = p.do("g", "iota", ri[:], [[0, 128]], channel_multiplier=1)
        tio = p.do("g", "iota", ci[:], [[1, 128]], channel_multiplier=0)
        EYEB = p.do("v", "tensor_tensor", eyeb[:], ri[:], ci[:], ALU.is_equal,
                    waits=[tio])
        EYEF = p.do("v", "tensor_tensor", eyef[:], ri[:], ci[:], ALU.is_equal)
        MSL = p.do("v", "tensor_tensor", mask_sl[:], ri[:], ci[:], ALU.is_gt)
        MULE = p.do("v", "tensor_tensor", mask_ule[:], ri[:], ci[:], ALU.is_le)
        ONESC = p.do("v", "memset", ones_c[:], 1.0)
        ONESCF = p.do("v", "memset", ones_cf[:], 1.0)
        NONESC = p.do("v", "memset", nones_c[:], -1.0)
        ONESRF = p.do("v", "memset", ones_rf[:], 1.0)
        tg = p.do("v", "memset", gdisc[:, 0:1], 0.5)
        tg = p.do("v", "tensor_scalar", gdisc[:, 0:1], gdisc[:, 0:1], C32, -C32, ALU.add,
                  ALU.add, waits=[tg])
        GDISC = p.do("v", "tensor_scalar", gdisc[:, 0:1], gdisc[:, 0:1], -2.0, 1.0, ALU.mult,
                     ALU.add, waits=[tg])
        GDISC = p.do("v", "tensor_scalar", gdisc[:, 1:2], gdisc[:, 0:1], -1.0,
                     1.0, ALU.mult, ALU.add, waits=[GDISC])
        GDISC = p.do("v", "tensor_scalar", gdisc[:, 2:3], gdisc[:, 0:1], -C32,
                     None, ALU.mult, waits=[GDISC])
        GDISC = p.do("v", "tensor_scalar", gdisc[:, 3:4], gdisc[:, 1:2], -C64,
                     None, ALU.mult, waits=[GDISC])
        GDISC = p.do("v", "tensor_tensor", gdisc[:, 2:3], gdisc[:, 2:3],
                     gdisc[:, 3:4], ALU.add, waits=[GDISC])
        stI.close()
        EPSC = p.do("v", "memset", eps_c[:], EPS)
        d_tb = p.dma("yd", "tbq", tb_s[:], tokbase_d)
        d_cv = p.dma("yd", "cvq", cv[:], cvec_d)
        t_sc = p.do("s", "activation", cv[:], cv[:], AF.Silu, waits=[d_cv])
        t_scb = p.do("v", "tensor_copy", silc[:], cv[:], waits=[t_sc])
        gn = p.dma("yd", "gnq", gncol[:], gnorm_d)

        b7busy = [None]

        with ExitStack() as stP:
            def sbP(name, shape, dt=F32):
                return stP.enter_context(nc.sbuf_tensor(name, shape, dt))

            modrow = sbP("modrow", [1, 6 * D], F32)
            garow = sbP("garow", [1, D], F32)
            gbrow = sbP("gbrow", [1, D], F32)
            magrow = sbP("magrow", [1, D], F32)
            g1row = sbP("g1row", [1, D], F32)
            g2row = sbP("g2row", [1, D], F32)
            sagrow = sbP("sagrow", [1, D], F32)
            sagf = sbP("sagf", [128, 8], F32)
            gabc = sbP("gabc", [128, D], F32)
            wmsb = [sbP(f"wmsb{i}", [128, 8, 512], BF16) for i in range(2)]
            wstb = [sbP(f"wstb{i}", [128, 8, 512], BF16) for i in range(2)]
            wotf = [sbP(f"wotf{i}", [128, 1024], F32) for i in range(2)]
            wbstb = sbP("wbstb", [128, 8, 2], BF16)

            wm_free = [None, None]
            last_ev = None
            for nb in range(12):
                pr = nb % 2
                src = wmod_d.rearrange("(c p) m -> p c m", p=128)[:, :, nb * 512:(nb + 1) * 512]
                d = p.dma("gd", f"wmod{pr}", wmsb[pr][:], src, waits=W(wm_free[pr]))
                m = None
                for kc in range(8):
                    m = MMr(7, slice(0, 1), slice(0, 512), silc[:, kc:kc + 1],
                            wmsb[pr][:, kc, :], start=(kc == 0), stop=(kc == 7),
                            waits=W(t_scb, d, b7busy[0]) if kc == 0 else ())
                last_ev = p.do("s", "copy", modrow[0:1, nb * 512:(nb + 1) * 512],
                               pb[7][0:1, 0:512], waits=[m])
                b7busy[0] = last_ev
                wm_free[pr] = m
            MOD = last_ev

            def mrow(k):
                return modrow[0:1, k * D:(k + 1) * D]

            def clamp_gate(k, dst, after):
                t1 = p.do("v", "tensor_scalar", dst, mrow(k), 0.0, None, ALU.is_ge,
                          waits=W(after))
                t1 = p.do("v", "tensor_scalar", dst, dst, 2.0, -1.0, ALU.mult,
                          ALU.add, waits=[t1])
                t2 = p.do("v", "scalar_tensor_tensor", magrow[:], mrow(k), -1.0,
                          mrow(k), ALU.mult, ALU.max, waits=W(after))
                t2 = p.do("v", "tensor_scalar", magrow[:], magrow[:], 0.25, None,
                          ALU.max, waits=[t2])
                return p.do("v", "tensor_tensor", dst, dst, magrow[:], ALU.mult,
                            waits=[t1, t2])

            t_ga = clamp_gate(2, garow[:], MOD)
            t_gb = clamp_gate(5, gbrow[:], t_ga)
            t_g1r = p.do("v", "tensor_scalar", g1row[:], mrow(1), 1.0, None,
                         ALU.add, waits=[MOD])
            t_g2r = p.do("v", "tensor_scalar", g2row[:], mrow(4), 1.0, None,
                         ALU.add, waits=[MOD])

            def row_to_cols(row_ap, dst, after):
                m = None
                for j in range(8):
                    m = p.do("t", "transpose", pb[7][:, j:j + 1],
                             row_ap[0:1, j * 128:(j + 1) * 128], eyef[0:1, 0:1],
                             waits=W(after, EYEF, b7busy[0]) if j == 0 else ())
                ev = p.do("s", "copy", dst, pb[7][:, 0:8], waits=[m])
                b7busy[0] = ev
                return ev

            t_g1c = row_to_cols(g1row, g1col[:, :], t_g1r)
            t_r = p.do("v", "reciprocal", sagrow[:], g1row[:], waits=[t_g1r])
            t_r = p.do("v", "tensor_tensor", sagrow[:], sagrow[:], mrow(0),
                       ALU.mult, waits=[t_r])
            t_sagc = row_to_cols(sagrow, sagf[:, :], t_r)
            t_sagb = p.do("v", "tensor_copy", sagcol[:], sagf[:], waits=[t_sagc])

            def bcast_tile(row_ap, dst, after):
                ev = None
                for h in range(2):
                    m = p.do("t", "matmul", pb[7][:, 0:512], ones_rf[:],
                             row_ap[0:1, h * 512:(h + 1) * 512], start=True,
                             stop=True, waits=W(after, ONESRF, b7busy[0]))
                    ev = p.do("s", "copy", dst[:, h * 512:(h + 1) * 512],
                              pb[7][:, 0:512], waits=[m])
                    b7busy[0] = ev
                return ev

            t_g2bc = bcast_tile(g2row, g2bc, t_g2r)
            t_s2bc = bcast_tile(modrow[0:1, 3 * D:4 * D], s2bc, MOD)
            t_gbbc = bcast_tile(gbrow, gbbc, t_gb)
            t_gabc = bcast_tile(garow, gabc, t_ga)

            wfree = [None, None]

            def fold_qkv(widx, dram, Wt, augt):
                pr = widx % 2
                d = p.dma("gd", f"wst{pr}", wstb[pr][:],
                          dram.rearrange("(c p) m -> p c m", p=128),
                          waits=W(wfree[pr]))
                tt = None
                for kc in range(8):
                    tt = p.do("v", "tensor_scalar", Wt[:, kc, :], wstb[pr][:, kc, :],
                              g1col[:, kc:kc + 1], None, ALU.mult,
                              waits=W(d, t_g1c) if kc == 0 else ())
                wfree[pr] = tt
                m = None
                for kc in range(8):
                    m = MMr(7, slice(0, 1), slice(0, 512), ones_c[:], Wt[:, kc, :],
                            start=(kc == 0), stop=(kc == 7),
                            waits=W(tt, ONESC, b7busy[0]) if kc == 0 else ())
                e1 = p.do("v", "tensor_copy", augt[0:1, 0:512], pb[7][0:1, 0:512],
                          waits=[m])
                b7busy[0] = e1
                m = None
                for kc in range(8):
                    m = MMr(7, slice(0, 1), slice(0, 512), sagcol[:, kc:kc + 1],
                            Wt[:, kc, :], start=(kc == 0), stop=(kc == 7),
                            waits=W(t_sagb, b7busy[0]) if kc == 0 else ())
                e2 = p.do("v", "tensor_copy", augt[0:1, 512:1024],
                          pb[7][0:1, 0:512], waits=[m])
                b7busy[0] = e2
                return e2

            t_wq = fold_qkv(0, wq_d, Wq, qaug)
            t_wk = fold_qkv(1, wk_d, Wk, kaug)
            t_wv = fold_qkv(2, wv_d, Wv, vaug)

            d = p.dma("gd", "wst0", wbstb[:], wb_d.rearrange("(c p) m -> p c m", p=128), waits=W(wfree[0]))
            tt = None
            for kc in range(8):
                tt = p.do("v", "tensor_scalar", Wbt[:, kc, :], wbstb[:, kc, :],
                          g1col[:, kc:kc + 1], None, ALU.mult,
                          waits=W(d, t_g1c) if kc == 0 else ())
            m = None
            for kc in range(8):
                m = MMr(7, slice(0, 1), slice(0, 2), ones_c[:], Wbt[:, kc, :],
                        start=(kc == 0), stop=(kc == 7),
                        waits=W(tt, b7busy[0]) if kc == 0 else ())
            e1 = p.do("v", "tensor_copy", baug[0:1, 0:2], pb[7][0:1, 0:2], waits=[m])
            b7busy[0] = e1
            m = None
            for kc in range(8):
                m = MMr(7, slice(0, 1), slice(0, 2), sagcol[:, kc:kc + 1],
                        Wbt[:, kc, :], start=(kc == 0), stop=(kc == 7),
                        waits=W(t_sagb, b7busy[0]) if kc == 0 else ())
            t_wb = p.do("v", "tensor_copy", baug[0:1, 2:4], pb[7][0:1, 0:2], waits=[m])
            b7busy[0] = t_wb

            wofree = [None, None]
            t_wo = None
            for kc in range(8):
                pr = kc % 2
                d = p.dma("yd", f"wo{pr}", wotf[pr][:], wo_d[kc * 128:(kc + 1) * 128, :],
                          waits=W(wofree[pr]))
                u = p.do("v", "tensor_scalar", wotf[pr][:], wotf[pr][:],
                         gncol[:, kc:kc + 1], None, ALU.mult, waits=[d, gn])
                t_wo = p.do("v", "tensor_tensor", Wo[:, kc, :], wotf[pr][:],
                            gabc[:], ALU.mult, waits=[u, t_gabc])
                wofree[pr] = t_wo

            PRO_GATE = [t_wq, t_wk, t_wv, t_wb, t_wo, t_g2bc, t_s2bc, t_gbbc,
                        t_sagb, MOD, t_gabc, t_sagc]

        PROJ_RDY = [t_wq, t_wk, t_wv, t_wb]
        GATEP = p.do("v", "memset", gatescr[:], 0.0, waits=PRO_GATE)

        with ExitStack() as stB:
            def sbB(name, shape, dt=F32):
                return stB.enter_context(nc.sbuf_tensor(name, shape, dt))

            xTb = [sbB(f"xTb{i}", [128, 8, TW], BF16) for i in range(2)]
            xtmt = sbB("xtmt", [128, 2, D], F32)
            qT = [sbB(f"qT{i}", [128, 4, TW], BF16) for i in range(2)]
            kT = [sbB(f"kT{i}", [128, 4, TW], BF16) for i in range(2)]
            vt = [sbB(f"vt{i}", [128, 4, TW], BF16) for i in range(2)]
            sqt = sbB("sqt", [128, TW], BF16)
            osq = sqt
            augT_ = sbB("augT_", [1, 2 * TW], BF16)
            augT = [augT_, augT_]
            rows_ = sbB("rows_", [1, 10 * TW], F32)
            rowsb = sbB("rowsb", [1, 4 * TW], F32)
            bnsS = sbB("bnsS", [128, 12], F32)
            bnaS = sbB("bnaS", [128, 8], F32)
            ost_ = sbB("ost_", [128, 4, TW], BF16)
            ost = [ost_, ost_]
            Pt = sbB("Pt", [128, 2, 128], BF16)
            Zt = sbB("Zt", [128, 2, 128], BF16)
            Rt = [sbB(f"Rt{j_}", [128, 2, 128], BF16) for j_ in range(2)]
            Mtt = [sbB(f"Mtt{j_}", [128, 2, 128], BF16) for j_ in range(2)]
            Ktm = [sbB(f"Ktm{j_}", [128, 2, 256], BF16) for j_ in range(2)]
            Kkap = sbB("Kkap", [128, 2, 256], BF16)
            Wpt = [sbB(f"Wpt{j_}", [128, 4, 128], BF16) for j_ in range(2)]
            Ut = sbB("Ut", [128, 256], BF16)
            Sst = [sbB(f"Sst{h}", [128, 512], BF16) for h in range(2)]
            Sf32 = [sbB(f"Sf32{h}", [128, 512], F32) for h in range(2)]
            statc = sbB("statc", [128, 4, 4], F32)
            rowsc = sbB("rowsc", [1, 128], F32)
            rbc = sbB("rbc", [128, 128], BF16)
            ffst_ = sbB("ffst_", [128, IF // 2], BF16)
            ffst = [ffst_, ffst_]
            tpos = sbB("tpos", [128, IF], BF16)
            absdmp = tpos
            w8tmp = sbB("w8tmp", [128, IF], FP8)
            tneg = w8tmp
            absacc = sbB("absacc", [128, 35], F32)

            RM, RVAR, RSIG, RINV = 0, 1, 2, 3
            RB0, RNN0, REQ0 = 4, 6, 8

            def R(pr, r, n=1, cs=None):
                if cs is None:
                    return rows_[0:1, r * TW:(r + n) * TW]
                return rows_[0:1, r * TW + cs.start:r * TW + cs.stop]

            def Rb(r, n=1, cs=None):
                if cs is None:
                    return rowsb[0:1, r * TW:(r + n) * TW]
                return rowsb[0:1, r * TW + cs.start:r * TW + cs.stop]

            bg = []
            ffree = [None, None]
            abs_done = {}
            abs_last = {}
            dmp_last = [None]
            tern_last = [None]
            w8_free = [None]
            GATE_BG = [None]

            def mk_abs(mi, dram, nch, chw):
                def mkgo(j, hh):
                    def go():
                        pr = 0
                        c2 = chw // 2
                        dd = p.dma("gd", "ffst", ffst[pr][:, 0:c2],
                                   dram[j * 128:(j + 1) * 128,
                                        hh * c2:(hh + 1) * c2],
                                   waits=W(ffree[pr]))
                        col = j if hh == 0 else 32
                        a = p.do("s", "activation", absdmp[:, 0:c2],
                                 ffst[pr][:, 0:c2], AF.Abs,
                                 accum_out=absacc[:, col:col + 1],
                                 waits=W(dd, dmp_last[0],
                                         abs_done.get(mi - 1) if mi > 0 else None,
                                         abs_last.get(mi)))
                        dmp_last[0] = a
                        ffree[pr] = a
                        if hh == 1:
                            a = p.do("v", "tensor_tensor", absacc[:, j:j + 1],
                                     absacc[:, j:j + 1], absacc[:, 32:33],
                                     ALU.add, waits=[a, abs_last.get(mi)])
                        abs_last[mi] = a
                    return go
                for j in range(nch):
                    for hh in range(2):
                        bg.append(mkgo(j, hh))

                def fin():
                    r = p.do("v", "tensor_reduce", absacc[:, 33:34],
                             absacc[:, 0:nch], AX, ALU.add,
                             waits=[abs_last[mi]])
                    mm_ = p.do("t", "matmul", pb[7][0:1, 0:1], ones_cf[:],
                               absacc[:, 33:34], start=True, stop=True,
                               waits=W(r, ONESCF, b7busy[0]))
                    nelem = float(D * IF)
                    s1 = p.do("v", "tensor_scalar", absacc[0:1, 34:35],
                              pb[7][0:1, 0:1], 1.0 / nelem, 1e-5, ALU.mult,
                              ALU.max, waits=[mm_])
                    b7busy[0] = s1
                    mm2 = p.do("t", "matmul", pb[7][:, 4:5], ones_rf[:],
                               absacc[0:1, 34:35], start=True, stop=True,
                               waits=W(s1, b7busy[0]))
                    e1 = p.do("v", "tensor_copy", wsc[:, 3 + mi:4 + mi],
                              pb[7][:, 4:5], waits=[mm2])
                    b7busy[0] = e1
                    w1 = p.do("v", "tensor_scalar", wsc[:, mi:mi + 1],
                              wsc[:, 3 + mi:4 + mi], 0.5, None, ALU.mult,
                              waits=[e1])
                    w2 = p.do("v", "tensor_scalar", nthr[:, mi:mi + 1],
                              wsc[:, mi:mi + 1], -1.0, None, ALU.mult, waits=[w1])
                    abs_done[mi] = w2
                bg.append(fin)

            def mk_tern(mi, dram, dst_sb, dst_dram, nch, chw):
                def mkgo(j, hh):
                    def go():
                        pr = 0
                        c2 = chw // 2
                        dd = p.dma("gd", "ffst", ffst[pr][:, 0:c2],
                                   dram[j * 128:(j + 1) * 128,
                                        hh * c2:(hh + 1) * c2],
                                   waits=W(ffree[pr]))
                        a1 = p.do("v", "tensor_scalar", tpos[:, 0:c2],
                                  ffst[pr][:, 0:c2], wsc[:, mi:mi + 1], None,
                                  ALU.is_gt, waits=W(dd, abs_done[mi], tern_last[0]))
                        a2 = p.do("v", "tensor_scalar", w8tmp[:, 0:c2],
                                  ffst[pr][:, 0:c2], nthr[:, mi:mi + 1], None,
                                  ALU.is_lt, waits=W(dd, w8_free[0]))
                        if dst_sb is not None:
                            a3 = p.do("v", "tensor_tensor",
                                      dst_sb[:, j, hh * c2:(hh + 1) * c2],
                                      tpos[:, 0:c2], w8tmp[:, 0:c2],
                                      ALU.subtract, waits=[a1, a2])
                            w8_free[0] = a3
                        else:
                            a3 = p.do("v", "tensor_tensor",
                                      w8tmp[:, 0:c2], tpos[:, 0:c2],
                                      w8tmp[:, 0:c2], ALU.subtract,
                                      waits=[a1, a2])
                            a4 = p.dma("gd", "w8out",
                                       dst_dram[:, j, hh * c2:(hh + 1) * c2],
                                       w8tmp[:, 0:c2], waits=[a3])
                            w8_free[0] = a4
                        ffree[pr] = a3
                        tern_last[0] = a3
                    return go
                for j in range(nch):
                    for hh in range(2):
                        bg.append(mkgo(j, hh))

            mk_abs(0, wgT_d, 8, IF)
            mk_tern(0, wgT_d, None, wg8_d, 8, IF)
            mk_abs(1, wuT_d, 8, IF)
            mk_tern(1, wuT_d, None, wu8_d, 8, IF)
            mk_abs(2, wdT_d, 32, 1024)
            mk_tern(2, wdT_d, Wd8, None, 32, 1024)

            def fin_muxd():
                t1 = p.do("v", "tensor_tensor", wsc[:, 6:7], wsc[:, 4:5],
                          wsc[:, 5:6], ALU.mult, waits=[abs_done[1], abs_done[2]])
                t1 = p.do("v", "tensor_scalar", wsc[:, 6:7], wsc[:, 6:7],
                          1.0 / 127.0, None, ALU.mult, waits=[t1])
                abs_done[3] = t1
            bg.append(fin_muxd)
            bgi = [0]

            def run_bg(k):
                n = 0
                while n < k and bgi[0] < len(bg):
                    bg[bgi[0]]()
                    bgi[0] += 1
                    n += 1

            GATE0 = GATEP
            ffree[0] = GATE0
            ffree[1] = GATE0
            dmp_last[0] = GATE0
            tern_last[0] = GATE0
            w8_free[0] = GATE0
            xT_free = [GATE0, GATE0]
            xtm_free = [GATE0]
            q_free = [GATE0, GATE0]
            k_free = [GATE0, GATE0]
            v_free = [GATE0, GATE0]
            aug_free = [GATE0, GATE0]
            rows_free = [GATE0]
            ost_free = [GATE0]
            bn_last = [None]
            bna_free = [None]
            big_free = [None, None]
            ch_free = [(), ()]
            b6_free = [None]
            b6u_free = [None]
            pend_chain = [None]
            sq_last = [None]
            osq_free = [None]
            Pt_free = [GATE0, GATE0]
            Zt_free = [GATE0, GATE0]
            Rt_free = [[GATE0, GATE0], [GATE0, GATE0]]
            Mt_free = [[GATE0, GATE0], [GATE0, GATE0]]
            Ktm_free = [[GATE0, GATE0], [GATE0, GATE0]]
            Kkap_free = [GATE0, GATE0]
            Wpt_free = [[GATE0, GATE0], [GATE0, GATE0]]
            Ut_free = [GATE0]
            rowsc_free = [GATE0]
            rbc_free = [GATE0]
            statc_free = [GATE0]
            chain_tok = [None, None]
            state_guard = [None, None]

            for i in (range(NT) if variant >= 1 else []):
                pr = i % 2
                t0 = i * TW
                dx = p.dma("gd", f"xTb{pr}", xTb[pr][:],
                           xT.rearrange("(c p) t -> p c t", p=128)[:, :, t0:t0 + TW],
                           waits=W(xT_free[pr]))
                bt = None
                bn_prev = [xtm_free[0], xtm_free[0]]
                for c_ in range(CPT):
                    sl2 = c_ % 2
                    dxm = p.dma("yd", f"xtm{sl2}", xtmt[:, sl2, :],
                                x_tm[t0 + c_ * 128:t0 + (c_ + 1) * 128, :],
                                waits=W(bn_prev[sl2]))
                    b1 = p.do("v", "bn_stats", bnsS[:, 0:6], xtmt[:, sl2, 0:512],
                              waits=W(dxm, bn_last[0]))
                    b1 = p.do("v", "bn_stats", bnsS[:, 6:12], xtmt[:, sl2, 512:1024],
                              waits=[b1])
                    b2 = p.do("v", "bn_aggr", bnaS[:, 2 * c_:2 * c_ + 2], bnsS[:],
                              waits=W(b1, bna_free[0] if c_ == 0 else None))
                    bn_last[0] = b2
                    bn_prev[sl2] = b1
                    bt = b2
                xtm_free[0] = bt
                m = None
                for c_ in range(CPT):
                    m = p.do("t", "transpose", pb[7][0:1, c_ * 128:(c_ + 1) * 128],
                             bnaS[:, 2 * c_:2 * c_ + 1], eyef[:],
                             waits=W(bt, EYEF, b7busy[0]) if c_ == 0 else ())
                ev = p.do("s", "copy", R(pr, RM), pb[7][0:1, 0:512],
                          waits=W(m, rows_free[0]))
                b7busy[0] = ev
                rows_free[0] = None
                m = None
                for c_ in range(CPT):
                    m = p.do("t", "transpose", pb[7][0:1, c_ * 128:(c_ + 1) * 128],
                             bnaS[:, 2 * c_ + 1:2 * c_ + 2], eyef[:],
                             waits=W(b7busy[0]) if c_ == 0 else ())
                bna_free[0] = m
                ev = p.do("s", "copy", R(pr, RVAR), pb[7][0:1, 0:512], waits=[m])
                b7busy[0] = ev
                t_sig = p.do("s", "activation", R(pr, RSIG), R(pr, RVAR), AF.Sqrt,
                             bias=eps_c[0:1, 0:1], waits=W(ev, EPSC))
                t_inv = p.do("v", "reciprocal", R(pr, RINV), R(pr, RSIG),
                             waits=[t_sig])
                a1 = p.do("v", "tensor_scalar", augT[pr][0:1, 0:TW],
                          R(pr, RM), -1.0, None, ALU.mult,
                          waits=W(ev, aug_free[pr]))
                aug_free[pr] = None
                a2 = p.do("v", "tensor_copy", augT[pr][0:1, TW:2 * TW],
                          R(pr, RSIG), waits=[t_sig])
                AUG = [a1, a2]

                def proj_fm(Wt, waug, dstT, dst_free_tok, sign_ones, row0, sgn_tok):
                    last_tok = None
                    for oc in range(4):
                        bk = oc % 2
                        m = None
                        for kc in range(8):
                            m = MM(bk, slice(0, 512),
                                   Wt[:, kc, oc * 128:(oc + 1) * 128],
                                   xTb[pr][:, kc, :], start=(kc == 0), stop=False,
                                   waits=W(dx, *PROJ_RDY, big_free[bk], *AUG)
                                   if kc == 0 else ())
                        m = MM(bk, slice(0, 512),
                               waug[0:1, oc * 128:(oc + 1) * 128],
                               augT[pr][0:1, 0:TW], start=False, stop=False)
                        m = MM(bk, slice(0, 512),
                               waug[0:1, 512 + oc * 128:512 + (oc + 1) * 128],
                               augT[pr][0:1, TW:2 * TW], start=False, stop=True)
                        e = p.do("s", "copy", dstT[:, oc, :], pb[bk][:, 0:512],
                                 waits=W(m, dst_free_tok if oc == 0 else None))
                        sq = p.do("s", "activation", sqt[:], pb[bk][:, 0:512],
                                  AF.Square, waits=W(sq_last[0]))
                        mm2 = MMr(7, slice(0, 1), slice(0, 512), sign_ones, sqt[:],
                                  start=(oc % 2 == 0), stop=(oc % 2 == 1),
                                  waits=W(sq, b7busy[0], sgn_tok)
                                  if oc % 2 == 0 else [sq])
                        big_free[bk] = sq
                        sq_last[0] = mm2
                        if oc % 2 == 1:
                            rr = row0 + oc // 2
                            e2 = p.do("v", "tensor_copy", R(pr, rr),
                                      pb[7][0:1, 0:512], waits=[mm2])
                            b7busy[0] = e2
                            last_tok = e2
                    return last_tok

                qsq = proj_fm(Wq, qaug, qT[pr], q_free[pr], ones_c[:], REQ0, ONESC)
                q_free[pr] = None
                tq = p.do("v", "tensor_scalar", R(pr, REQ0, 2), R(pr, REQ0, 2),
                          EPS, None, ALU.mult, waits=[qsq])
                ksq = proj_fm(Wk, kaug, kT[pr], k_free[pr], nones_c[:], RNN0,
                              NONESC)
                k_free[pr] = None

                tb1 = None
                for hh in range(2):
                    m = None
                    for kc in range(8):
                        m = MMr(7, slice(0, 1), slice(0, 512),
                                Wbt[:, kc, hh:hh + 1],
                                xTb[pr][:, kc, :], start=(kc == 0), stop=False,
                                waits=W(b7busy[0]) if kc == 0 else ())
                    m = MMr(7, slice(0, 1), slice(0, 512), baug[0:1, hh:hh + 1],
                            augT[pr][0:1, 0:TW], start=False, stop=False)
                    m = MMr(7, slice(0, 1), slice(0, 512), baug[0:1, 2 + hh:3 + hh],
                            augT[pr][0:1, TW:2 * TW], start=False, stop=True)
                    tb1 = p.do("v", "tensor_tensor", R(pr, RB0 + hh),
                               pb[7][0:1, 0:512], R(pr, RINV), ALU.mult,
                               waits=[m, t_inv])
                    b7busy[0] = tb1
                t_bsig = p.do("s", "activation", R(pr, RB0, 2), R(pr, RB0, 2),
                              AF.Sigmoid, waits=[tb1])
                t_sn = p.do("s", "activation", R(pr, RNN0, 2), R(pr, RNN0, 2),
                            AF.Sqrt, scale=-1.0, waits=[ksq])
                t_dl = p.do("v", "reciprocal", R(pr, RNN0, 2), R(pr, RNN0, 2),
                            waits=[t_sn])
                t_bd = p.do("v", "tensor_tensor", R(pr, RB0, 2), R(pr, RB0, 2),
                            R(pr, RNN0, 2), ALU.mult, waits=[t_dl, t_bsig])
                t_ng = p.do("v", "scalar_tensor_tensor", Rb(0, 2), R(pr, RB0, 2),
                            -1.0, R(pr, RNN0, 2), ALU.mult, ALU.mult,
                            waits=[t_bd])
                t_gv0 = p.do("v", "tensor_tensor", Rb(2), R(pr, RB0),
                             R(pr, RINV), ALU.mult, waits=[t_bd])
                t_gv1 = p.do("v", "tensor_tensor", Rb(3), R(pr, RB0 + 1),
                             R(pr, RINV), ALU.mult, waits=[t_bd])

                m = None
                first = True
                for c_ in range(CPT):
                    for st_ in range(4):
                        m = p.do("t", "transpose",
                                 pb[7][:, c_ * 4 + st_:c_ * 4 + st_ + 1],
                                 Rb(st_, cs=slice(c_ * 128, (c_ + 1) * 128)),
                                 ones_cf[0:1, 0:1],
                                 waits=W(t_ng, t_gv0, t_gv1, b7busy[0],
                                         statc_free[0]) if first else ())
                        first = False
                STATC = p.do("s", "copy", statc[:].rearrange("p c m -> p (c m)"),
                             pb[7][:, 0:16], waits=[m])
                b7busy[0] = STATC
                statc_free[0] = None

                vlast = None
                for tc in range(CPT):
                    bk = tc % 2
                    m = None
                    for kc in range(8):
                        m = MM(bk, slice(0, 512),
                               xTb[pr][:, kc, tc * 128:(tc + 1) * 128],
                               Wv[:, kc, :], start=(kc == 0), stop=False,
                               waits=W(big_free[bk]) if kc == 0 else ())
                    m = MM(bk, slice(0, 512),
                           augT[pr][0:1, tc * 128:(tc + 1) * 128],
                           vaug[0:1, 0:512], start=False, stop=False)
                    m = MM(bk, slice(0, 512),
                           augT[pr][0:1, TW + tc * 128:TW + (tc + 1) * 128],
                           vaug[0:1, 512:1024], start=False, stop=True)
                    e0 = p.do("v", "tensor_scalar", vt[pr][:, tc, 0:256],
                              pb[bk][:, 0:256], statc[:, tc, 2:3], None, ALU.mult,
                              waits=W(m, STATC, v_free[pr] if tc == 0 else None))
                    e1 = p.do("v", "tensor_scalar", vt[pr][:, tc, 256:512],
                              pb[bk][:, 256:512], statc[:, tc, 3:4], None,
                              ALU.mult, waits=[m])
                    big_free[bk] = e1
                    vlast = e1
                v_free[pr] = None
                aug_free[pr] = m
                xT_free[pr] = m

                run_bg(4)

                for c_ in (range(CPT) if variant >= 2 else []):
                    sub = variant if variant < 10 else variant - 20
                    g = i * CPT + c_
                    cs = slice(c_ * 128, (c_ + 1) * 128)
                    bp = c_ % 2
                    pp = 2 + (g % 2)
                    ppo = 2 + ((g + 1) % 2)
                    solved = [None, None]
                    for h in range(2):
                        m = None
                        for jj in range(2):
                            m = MM(pp, slice(h * 128, (h + 1) * 128),
                                   kT[pr][:, 2 * h + jj, cs],
                                   kT[pr][:, 2 * h + jj, cs],
                                   start=(jj == 0), stop=(jj == 1),
                                   waits=W(*ch_free[pp - 2], ksq) if jj == 0 else ())
                        mA = m
                        m = None
                        for jj in range(2):
                            m = MM(pp, slice(256 + h * 128, 256 + (h + 1) * 128),
                                   kT[pr][:, 2 * h + jj, cs],
                                   qT[pr][:, 2 * h + jj, cs],
                                   start=(jj == 0), stop=(jj == 1),
                                   waits=W(qsq) if jj == 0 else ())
                        mM = m
                        eP = p.do("v", "scalar_tensor_tensor", Pt[:, h, :],
                                  pb[pp][:, h * 128:(h + 1) * 128],
                                  statc[:, c_, h:h + 1], mask_sl[:], ALU.mult,
                                  ALU.mult, waits=W(mA, STATC, MSL, Pt_free[h]))
                        Pt_free[h] = None
                        eM = p.do("v", "tensor_tensor", Mtt[bp][:, h, :],
                                  pb[pp][:, 256 + h * 128:256 + (h + 1) * 128],
                                  mask_ule[:], ALU.mult,
                                  waits=W(mM, MULE, Mt_free[bp][h]))
                        Mt_free[bp][h] = None
                        ch_free[pp - 2] = (eM,)
                        if variant > 10 and sub < 2:
                            solved[h] = eM
                            continue
                        mz = TRb(ppo, slice(512 * h, 512 * h + 128), Pt[:, h, :],
                                 waits=W(eP, EYEB, *ch_free[ppo - 2]))
                        mk2 = TRb(ppo, slice(512 * h + 128, 512 * h + 256),
                                  kT[pr][:, 2 * h, cs], waits=())
                        mk2 = TRb(ppo, slice(512 * h + 256, 512 * h + 384),
                                  kT[pr][:, 2 * h + 1, cs], waits=())
                        eZ = p.do("s", "copy", Zt[:, h, :],
                                  pb[ppo].bitcast(BF16)[:, 512 * h:512 * h + 128],
                                  waits=W(mz, Zt_free[h]))
                        Zt_free[h] = None
                        eK = p.do("s", "copy", Ktm[bp][:, h, :],
                                  pb[ppo].bitcast(BF16)[:, 512 * h + 128:512 * h + 384],
                                  waits=W(mk2, Ktm_free[bp][h]))
                        Ktm_free[bp][h] = None
                        ch_free[ppo - 2] = (eK,)
                        eKk = p.do("v", "tensor_scalar", Kkap[:, h, :],
                                   Ktm[bp][:, h, :], statc[:, c_, h:h + 1], None,
                                   ALU.mult, waits=W(eK, Kkap_free[h]))
                        Kkap_free[h] = None
                        if variant > 10 and sub < 3:
                            solved[h] = eKk
                            continue
                        r1 = MM(6, slice(h * 128, (h + 1) * 128), eyeb[:], eyeb[:],
                                start=True, stop=False, waits=W(EYEB, b6_free[0]))
                        r2 = MM(6, slice(h * 128, (h + 1) * 128), Pt[:, h, :],
                                eyeb[:], start=False, stop=True, waits=[eP])
                        lr = p.do("s", "copy", Rt[bp][:, h, :],
                                  pb[6][:, h * 128:(h + 1) * 128],
                                  waits=W(r2, Rt_free[bp][h]))
                        Rt_free[bp][h] = None
                        b6_free[0] = lr
                        lp, lz = eP, eZ
                        for lvl in range(1, 7):
                            bkl = 2 + ((g + lvl) % 2)
                            m1 = MM(bkl, slice(h * 128, (h + 1) * 128), Zt[:, h, :],
                                    Pt[:, h, :], start=True, stop=True,
                                    waits=W(lp, lz, *ch_free[bkl - 2])
                                    if lvl == 1 else ())
                            m2 = MM(bkl, slice(256 + h * 128, 256 + (h + 1) * 128),
                                    Pt[:, h, :], Zt[:, h, :], start=True, stop=True)
                            lz = p.do("s", "copy", Zt[:, h, :],
                                      pb[bkl][:, 256 + h * 128:256 + (h + 1) * 128],
                                      waits=[m2])
                            lp = p.do("v", "tensor_copy", Pt[:, h, :],
                                      pb[bkl][:, h * 128:(h + 1) * 128],
                                      waits=[lz])
                            ch_free[bkl - 2] = (lp, lz)
                            rm = MM(6, slice(h * 128, (h + 1) * 128), eyeb[:],
                                    Rt[bp][:, h, :], start=True, stop=False,
                                    waits=[lp])
                            rm = MM(6, slice(h * 128, (h + 1) * 128), Pt[:, h, :],
                                    Rt[bp][:, h, :], start=False, stop=True)
                            lr = p.do("s", "copy", Rt[bp][:, h, :],
                                      pb[6][:, h * 128:(h + 1) * 128], waits=[rm])
                            b6_free[0] = lr
                        if variant > 10 and sub < 4:
                            solved[h] = lr
                            continue
                        m = MM(pp, slice(0, 128), Kkap[:, h, 0:128], Rt[bp][:, h, :],
                               start=True, stop=True,
                               waits=W(eKk, lr, *ch_free[pp - 2]))
                        m2 = MM(pp, slice(128, 256), Kkap[:, h, 128:256],
                                Rt[bp][:, h, :], start=True, stop=True)
                        eW = p.do("s", "copy",
                                  Wpt[bp][:].rearrange("p c m -> p (c m)")[:, h * 256:(h + 1) * 256],
                                  pb[pp][:, 0:256], waits=W(m2, Wpt_free[bp][h]))
                        Wpt_free[bp][h] = None
                        ch_free[pp - 2] = (eW,)
                        Kkap_free[h] = eW
                        solved[h] = eW
                        Pt_free[h] = eW
                        Zt_free[h] = lz

                    def mk_chain(c_=c_, g=g, cs=cs, pp=pp, bp=bp,
                                 solved=tuple(solved), vlast=vlast, pr=pr,
                                 sub=sub):
                      for h in (range(2) if (variant < 10 or sub >= 4) else []):
                        mU = MM(6, slice(256, 512), Rt[bp][:, h, :],
                                vt[pr][:, c_, h * 256:(h + 1) * 256],
                                start=True, stop=(g == 0),
                                waits=W(solved[0], solved[1], vlast, b6u_free[0],
                                        chain_tok[h]))
                        if g > 0:
                            for jj in range(2):
                                mU = MM(6, slice(256, 512),
                                        Wpt[bp][:, 2 * h + jj, :],
                                        Sst[h][:, jj * 256:(jj + 1) * 256],
                                        start=False, stop=(jj == 1))
                        eU = p.do("s", "copy", Ut[:], pb[6][:, 256:512],
                                  waits=W(mU, Ut_free[0]))
                        b6u_free[0] = eU
                        mo = None
                        for dd in range(2):
                            if g > 0:
                                for jj in range(2):
                                    mo = MM(6, slice(256 + dd * 128, 256 + (dd + 1) * 128),
                                            Sst[h][:, jj * 256 + dd * 128:jj * 256 + (dd + 1) * 128],
                                            qT[pr][:, 2 * h + jj, cs],
                                            start=(jj == 0), stop=False,
                                            waits=W(eU, b6u_free[0])
                                            if (dd == 0 and jj == 0) else ())
                                mo = MM(6, slice(256 + dd * 128, 256 + (dd + 1) * 128),
                                        Ut[:, dd * 128:(dd + 1) * 128], Mtt[bp][:, h, :],
                                        start=False, stop=True)
                            else:
                                mo = MM(6, slice(256 + dd * 128, 256 + (dd + 1) * 128),
                                        Ut[:, dd * 128:(dd + 1) * 128], Mtt[bp][:, h, :],
                                        start=True, stop=True,
                                        waits=W(eU, b6u_free[0]) if dd == 0 else ())
                        sqo = p.do("s", "activation", osq[:, 0:256],
                                   pb[6][:, 256:512],
                                   AF.Square, waits=W(mo, osq_free[0], sq_last[0]))
                        sq_last[0] = sqo
                        mo2 = MMr(7, slice(0, 1), slice(0, 128), ones_c[:],
                                  osq[:, 0:128], start=True, stop=False,
                                  waits=W(sqo, b7busy[0]))
                        mo2 = MMr(7, slice(0, 1), slice(0, 128), ones_c[:],
                                  osq[:, 128:256], start=False, stop=True)
                        t1 = p.do("v", "tensor_scalar", rowsc[0:1, :],
                                  pb[7][0:1, 0:128], 1.0 / 256.0, None, ALU.mult,
                                  waits=W(mo2, rowsc_free[0]))
                        b7busy[0] = t1
                        t1 = p.do("v", "tensor_tensor", rowsc[0:1, :],
                                  rowsc[0:1, :], R(pr, REQ0 + h, cs=cs), ALU.add,
                                  waits=[t1, tq])
                        t1 = p.do("s", "activation", rowsc[0:1, :],
                                  rowsc[0:1, :], AF.Sqrt, waits=[t1])
                        t1 = p.do("v", "reciprocal", rowsc[0:1, :],
                                  rowsc[0:1, :], waits=[t1])
                        mb = p.do("t", "matmul", pb[7][:, 128:256], ones_rf[:],
                                  rowsc[0:1, :], start=True, stop=True,
                                  waits=W(t1, b7busy[0]))
                        eb = p.do("s", "copy", rbc[:], pb[7][:, 128:256],
                                  waits=W(mb, rbc_free[0]))
                        b7busy[0] = eb
                        oe = None
                        for dd in range(2):
                            oe = p.do("v", "tensor_tensor",
                                      ost[pr][:, 2 * h + dd, cs],
                                      pb[6][:, 256 + dd * 128:256 + (dd + 1) * 128],
                                      rbc[:], ALU.mult,
                                      waits=W(eb, ost_free[0]
                                              if (c_ == 0 and h == 0 and dd == 0)
                                              else None))
                        if c_ == 0 and h == 0:
                            ost_free[0] = None
                        rbc_free[0] = oe
                        rowsc_free[0] = oe
                        b6u_free[0] = oe
                        osq_free[0] = mo2
                        sq_last[0] = mo2
                        ms = None
                        for jj in range(2):
                            ms = MM(4 + h, slice(jj * 256, (jj + 1) * 256),
                                    Ktm[bp][:, h, jj * 128:(jj + 1) * 128], Ut[:],
                                    start=(jj == 0), stop=(jj == 1),
                                    waits=W(eU, state_guard[h]) if jj == 0 else ())
                        if g == 0:
                            es = p.do("v", "tensor_copy", Sf32[h][:], pb[4 + h][:],
                                      waits=[ms])
                        else:
                            es = p.do("v", "tensor_tensor", Sf32[h][:], Sf32[h][:],
                                      pb[4 + h][:], ALU.add, waits=[ms])
                        es = p.do("s", "copy", Sst[h][:], Sf32[h][:], waits=[es])
                        state_guard[h] = es
                        Ut_free[0] = ms
                        chain_tok[h] = es
                        Ktm_free[bp][h] = ms
                        Rt_free[bp][h] = mU
                        Mt_free[bp][h] = mo
                        Wpt_free[bp][h] = mU
                    if pend_chain[0] is not None:
                        pend_chain[0]()
                    pend_chain[0] = mk_chain

                if pend_chain[0] is not None:
                    pend_chain[0]()
                    pend_chain[0] = None
                if variant >= 2 and (variant < 10 or (variant - 20) >= 4):
                    do_ = p.dma("gd", f"ocoll{pr}",
                                coll_in.rearrange("(c p) t -> p c t", p=128)[:, :, t0:t0 + TW],
                                ost[pr][:], waits=W(chain_tok[0], chain_tok[1]))
                    ost_free[0] = do_
                q_free[pr] = chain_tok[1]
                k_free[pr] = chain_tok[1]
                v_free[pr] = chain_tok[1]
                rows_free[0] = chain_tok[1]
                statc_free[0] = chain_tok[1]
                run_bg(4)

            run_bg(len(bg))
            FFN_RDY = W(tern_last[0], w8_free[0], abs_done.get(0),
                        abs_done.get(1), abs_done.get(2), abs_done.get(3))
            DUMMY_OUT = ost[0]
            PH23 = [("t", p.cnt["t"]), ("v", p.cnt["v"]), ("s", p.cnt["s"]),
                    ("g", p.cnt["g"])]

        GATE5 = p.do("v", "memset", gatescr[:], 0.0, waits=PH23 + FFN_RDY)
        if variant >= 3:
            COLL = p.do("g", "collective_compute", "AllGather", ALU.bypass,
                        replica_groups=[[0, 1], [2, 3], [4, 5], [6, 7]],
                        ins=[coll_in], outs=[coll_out],
                        waits=[("ocoll0", p.cnt.get("ocoll0", 0)),
                               ("ocoll1", p.cnt.get("ocoll1", 0))])
        else:
            COLL = GATE5

        with ExitStack() as stC:
            def sbC(name, shape, dt=F32):
                return stC.enter_context(nc.sbuf_tensor(name, shape, dt))

            sc8k = sbC("sc8k", [128, 4096], BF16)
            oat = sc8k.rearrange("p (c t) -> p c t", c=8)
            guiT = sc8k.rearrange("p (c t) -> p c t", c=32)
            xot = sbC("xot", [128, 4, D], F32)
            xmid = sbC("xmid", [128, 4, D], F32)
            h2t = sbC("h2t", [128, D], F32)
            h2i = sbC("h2i", [128, D], BF16)
            h2iT = sbC("h2iT", [128, 4, 8, 128], BF16)
            gub = [sbC(f"gub{t}", [128, IF], BF16) for t in range(4)]
            zt_ = sbC("zt_", [128, 512], F32)
            silg = zt_
            r32t = sbC("r32t", [128, 512], F32)
            r64t = sbC("r64t", [128, 512], F32)
            outt = sbC("outt", [128, 1024], F32)
            bns2 = sbC("bns2", [128, 12], F32)
            cols2 = sbC("cols2", [128, 4, 16], F32)
            wgst = [sbC(f"wgst{t}", [128, 8, 512], FP8) for t in range(2)]
            wust_ = sbC("wust_", [128, 8, 512], FP8)
            wust = [wust_, wust_]

            r0 = stC.enter_context(nc.sync.register("r0"))
            r1 = stC.enter_context(nc.sync.register("r1"))
            p.raw("y", lambda e: e.reg_load(r0, tb_s[:1, :1]),
                  waits=[d_tb], inc=False)

            G5 = PH23[-1]
            oat_free = [None]
            xot_free = [None]
            xmid_free = [None]
            big5 = [None, None]
            ch5 = [None, None]
            h2_free = [None]
            h2i_free = [None]
            h2iT_free = [None]
            gu_free = [[None] * 4]
            guiT_free = [None]
            z_free = [None]
            r32_free = [None]
            r64_free = [None]
            cols_free = [[None] * 4]
            bn2_last = [None]
            outt_free = [None]
            wg_free = [None, None]
            wu_free = [None, None]

            for j in (range(NOT) if variant >= 4 else []):
                fvar = 99 if variant == 4 else (variant - 40 if variant > 40 else 99)
                to = j * TW

                def load_oat(e, to=to):
                    e.reg_mov(r1, to)
                    e.reg_add(r1, r1, r0)
                    in_ap = bass.AP(coll_out.tensor, r1,
                                    [[N, 128], [512 * N, 2], [128 * N, 4], [1, TW]])
                    return e.dma_start(
                        out=oat.rearrange("p s2 t -> p s2 t") if False else sc8k[:].rearrange("p (s f t) -> p s f t", s=2, f=4),
                        in_=in_ap)
                dol = p.dma_reg("yd", "oat", load_oat,
                                waits=W(COLL, GATE5, oat_free[0]))
                dxo = p.dma("yd", "xot", xot[:],
                            x_own[to:to + TW, :].rearrange("(c p) m -> p c m", p=128),
                            waits=W(COLL, GATE5, xot_free[0]))

                xm_toks = []
                e = None
                m = None
                for tc in range(CPT):
                    for nn in range(2):
                        bk = nn
                        m = None
                        for kc in range(8):
                            m = MM(bk, slice(0, 512),
                                   oat[:, kc, tc * 128:(tc + 1) * 128],
                                   Wo[:, kc, nn * 512:(nn + 1) * 512],
                                   start=(kc == 0), stop=(kc == 7),
                                   waits=W(dol, dxo, t_wo, big5[bk], *PH23)
                                   if kc == 0 else ())
                        e = p.do("v", "tensor_tensor",
                                 xmid[:, tc, nn * 512:(nn + 1) * 512],
                                 pb[bk][:, 0:512],
                                 xot[:, tc, nn * 512:(nn + 1) * 512], ALU.add,
                                 waits=W(m, xmid_free[0]
                                         if (tc == 0 and nn == 0) else None))
                        big5[bk] = e
                    xm_toks.append(e)
                xmid_free[0] = None
                oat_free[0] = m
                guiT_free[0] = m
                xot_free[0] = e
                if fvar < 2:
                    continue

                h2it_toks = []
                for tc in range(CPT):
                    cl = cols2[:, tc, :]
                    b1 = p.do("v", "bn_stats", bns2[:, 0:6], xmid[:, tc, 0:512],
                              waits=W(xm_toks[tc], bn2_last[0]))
                    b1 = p.do("v", "bn_stats", bns2[:, 6:12], xmid[:, tc, 512:1024],
                              waits=[b1])
                    b2 = p.do("v", "bn_aggr", cl[:, 0:2], bns2[:],
                              waits=W(b1, cols_free[0][tc]))
                    cols_free[0][tc] = None
                    bn2_last[0] = b2
                    s1 = p.do("s", "activation", cl[:, 2:3], cl[:, 1:2], AF.Sqrt,
                              bias=eps_c[:, 0:1], waits=W(b2, EPSC))
                    v1 = p.do("v", "reciprocal", cl[:, 3:4], cl[:, 2:3], waits=[s1])
                    v2 = p.do("v", "scalar_tensor_tensor", cl[:, 4:5], cl[:, 0:1],
                              -1.0, cl[:, 3:4], ALU.mult, ALU.mult,
                              waits=[v1, b2])
                    xh = p.do("v", "tensor_scalar", h2t[:], xmid[:, tc, :],
                              cl[:, 3:4], cl[:, 4:5], ALU.mult, ALU.add,
                              waits=W(v2, h2_free[0]))
                    h2a = p.do("v", "tensor_tensor", h2t[:], h2t[:], g2bc[:],
                               ALU.mult, waits=W(xh, t_g2bc))
                    h2a = p.do("v", "tensor_tensor", h2t[:], h2t[:], s2bc[:],
                               ALU.add, waits=W(h2a, t_s2bc))
                    am = p.do("v", "tensor_reduce", cl[:, 5:6], h2t[:], AX,
                              ALU.max, apply_absolute_value=True, waits=[h2a])
                    am = p.do("v", "tensor_scalar", cl[:, 5:6], cl[:, 5:6], 1e-5,
                              None, ALU.max, waits=[am])
                    sc = p.do("v", "reciprocal", cl[:, 6:7], cl[:, 5:6],
                              waits=[am])
                    sc = p.do("v", "tensor_scalar", cl[:, 6:7], cl[:, 6:7], 127.0,
                              None, ALU.mult, waits=[sc])
                    hi = None
                    for s2_ in range(2):
                        ss = slice(s2_ * 512, (s2_ + 1) * 512)
                        ra = p.do("v", "tensor_scalar", r32t[:], h2t[:, ss],
                                  cl[:, 6:7], C32, ALU.mult, ALU.add,
                                  waits=W(sc, r32_free[0]))
                        rb = p.do("v", "tensor_scalar", r64t[:], h2t[:, ss],
                                  cl[:, 6:7], C64, ALU.mult, ALU.add,
                                  waits=W(r64_free[0]))
                        td = p.do("v", "tensor_scalar", r64t[:], r64t[:],
                                  gdisc[:, 1:2], gdisc[:, 2:3], ALU.mult,
                                  ALU.add, waits=W(rb, GDISC))
                        hi = p.do("v", "scalar_tensor_tensor", h2i[:, ss],
                                  r32t[:], gdisc[:, 0:1], r64t[:], ALU.mult,
                                  ALU.add, waits=W(td, h2i_free[0]
                                                   if s2_ == 0 else None))
                        r32_free[0] = hi
                        r64_free[0] = hi
                    h2i_free[0] = None
                    h2_free[0] = hi
                    m = None
                    for kc in range(8):
                        m = TRb(2, slice(kc * 128, (kc + 1) * 128),
                                h2i[:, kc * 128:(kc + 1) * 128],
                                waits=W(hi, ch5[0]) if kc == 0 else ())
                    e = p.do("s", "copy",
                             h2iT[:, tc, :, :].rearrange("p c m -> p (c m)"),
                             pb[2].bitcast(BF16)[:, 0:1024],
                             waits=W(m, h2iT_free[0] if tc == 0 else None))
                    ch5[0] = e
                    h2i_free[0] = e
                    h2it_toks.append(e)
                h2iT_free[0] = None
                if fvar < 3:
                    continue

                dq_toks = []
                for tc in range(CPT):
                    cl = cols2[:, tc, :]
                    dg = p.do("v", "reciprocal", cl[:, 7:8], cl[:, 6:7],
                              waits=[h2it_toks[tc]])
                    dg = p.do("v", "tensor_tensor", cl[:, 8:9], cl[:, 7:8],
                              wsc[:, 3:4], ALU.mult, waits=W(dg, *FFN_RDY[:1]))
                    dq_toks.append(dg)

                gu_toks = [None] * 4
                mg = None
                mu_ = None
                for sl in range(8):
                    spr = sl % 2
                    dwg = p.dma("sd", f"wg{spr}", wgst[spr][:],
                                wg8_d[:, :, sl * 512:(sl + 1) * 512],
                                waits=W(wg_free[spr], *FFN_RDY))
                    dwu = p.dma("sd", "wuq", wust[spr][:],
                                wu8_d[:, :, sl * 512:(sl + 1) * 512],
                                waits=W(wu_free[0]))
                    for tc in range(CPT):
                        mg = None
                        for kc in range(8):
                            mg = MM(0, slice(0, 512), h2iT[:, tc, kc, :],
                                    wgst[spr][:, kc, :], start=(kc == 0),
                                    stop=(kc == 7),
                                    waits=W(h2it_toks[tc], dwg, big5[0])
                                    if kc == 0 else ())
                        sg_ = p.do("s", "activation", silg[:], pb[0][:, 0:512],
                                   AF.Silu, scale=cols2[:, tc, 8:9],
                                   waits=W(mg, dq_toks[tc], z_free[0]))
                        big5[0] = sg_
                        mu_ = None
                        for kc in range(8):
                            mu_ = MM(1, slice(0, 512), h2iT[:, tc, kc, :],
                                     wust[spr][:, kc, :], start=(kc == 0),
                                     stop=(kc == 7),
                                     waits=W(dwu, big5[1]) if kc == 0 else ())
                        gu_ = p.do("v", "tensor_tensor",
                                   gub[tc][:, sl * 512:(sl + 1) * 512], silg[:],
                                   pb[1][:, 0:512], ALU.mult,
                                   waits=W(mu_, sg_, gu_free[0][tc]
                                           if sl == 0 else None))
                        if sl == 0:
                            gu_free[0][tc] = None
                        big5[1] = gu_
                        z_free[0] = gu_
                        gu_toks[tc] = gu_
                    wg_free[spr] = mg
                    wu_free[0] = mu_
                if fvar < 4:
                    continue

                for tc in range(CPT):
                    cl = cols2[:, tc, :]
                    ag = p.do("v", "tensor_reduce", cl[:, 9:10], gub[tc][:], AX,
                              ALU.max, apply_absolute_value=True,
                              waits=W(*gu_toks))
                    th = p.do("v", "reciprocal", cl[:, 10:11], wsc[:, 4:5],
                              waits=W(abs_done.get(1)))
                    th = p.do("v", "tensor_tensor", cl[:, 10:11], cl[:, 10:11],
                              cl[:, 6:7], ALU.mult, waits=[th])
                    th = p.do("v", "tensor_scalar", cl[:, 10:11], cl[:, 10:11],
                              1e-5, None, ALU.mult, waits=[th])
                    me = p.do("v", "tensor_tensor", cl[:, 11:12], cl[:, 9:10],
                              cl[:, 10:11], ALU.max, waits=[ag, th])
                    s3 = p.do("v", "reciprocal", cl[:, 12:13], cl[:, 11:12],
                              waits=[me])
                    s3 = p.do("v", "tensor_scalar", cl[:, 12:13], cl[:, 12:13],
                              127.0, None, ALU.mult, waits=[s3])
                    dd_ = p.do("v", "tensor_tensor", cl[:, 13:14], cl[:, 11:12],
                               cl[:, 7:8], ALU.mult, waits=[me])
                    dd_ = p.do("v", "tensor_tensor", cl[:, 13:14], cl[:, 13:14],
                               wsc[:, 6:7], ALU.mult, waits=W(dd_))
                    cols_free[0][tc] = dd_
                    gl = None
                    for s2_ in range(8):
                        ss = slice(s2_ * 512, (s2_ + 1) * 512)
                        ra = p.do("v", "tensor_scalar", r32t[:], gub[tc][:, ss],
                                  cl[:, 12:13], C32, ALU.mult, ALU.add,
                                  waits=W(s3, r32_free[0]))
                        rb = p.do("v", "tensor_scalar", r64t[:], gub[tc][:, ss],
                                  cl[:, 12:13], C64, ALU.mult, ALU.add,
                                  waits=W(r64_free[0]))
                        td = p.do("v", "tensor_scalar", r64t[:], r64t[:],
                                  gdisc[:, 1:2], gdisc[:, 2:3], ALU.mult,
                                  ALU.add, waits=[rb])
                        gl = p.do("v", "scalar_tensor_tensor", gub[tc][:, ss],
                                  r32t[:], gdisc[:, 0:1], r64t[:], ALU.mult,
                                  ALU.add, waits=[td])
                        r32_free[0] = gl
                        r64_free[0] = gl
                    glast = None
                    for gq in range(4):
                        bq = 2 + (gq % 2)
                        m = None
                        for kc in range(8):
                            m = TRb(bq, slice(kc * 128, (kc + 1) * 128),
                                    gub[tc][:, (gq * 8 + kc) * 128:(gq * 8 + kc + 1) * 128],
                                    waits=W(gl, ch5[bq - 2]) if kc == 0 else ())
                        glast = p.do("s", "copy",
                                     guiT[:, gq * 8:(gq + 1) * 8, :].rearrange("p c m -> p (c m)"),
                                     pb[bq].bitcast(BF16)[:, 0:1024],
                                     waits=W(m, guiT_free[0] if gq == 0 else None))
                        ch5[bq - 2] = glast
                    guiT_free[0] = None
                    gu_free[0][tc] = glast
                    if fvar < 5:
                        oat_free[0] = glast
                        continue
                    o2 = None
                    mlast = None
                    for nn in range(2):
                        bk = nn
                        m = None
                        for kc in range(32):
                            m = MM(bk, slice(0, 512), guiT[:, kc, :],
                                   Wd8[:, kc, nn * 512:(nn + 1) * 512],
                                   start=(kc == 0), stop=(kc == 31),
                                   waits=W(glast, big5[bk], *FFN_RDY)
                                   if kc == 0 else ())
                        mlast = m
                        o1 = p.do("v", "scalar_tensor_tensor",
                                  outt[:, nn * 512:(nn + 1) * 512],
                                  pb[bk][:, 0:512], cl[:, 13:14],
                                  gbbc[:, nn * 512:(nn + 1) * 512], ALU.mult,
                                  ALU.mult,
                                  waits=W(m, dd_, t_gbbc, outt_free[0]
                                          if nn == 0 else None))
                        o2 = p.do("v", "tensor_tensor",
                                  outt[:, nn * 512:(nn + 1) * 512],
                                  outt[:, nn * 512:(nn + 1) * 512],
                                  xmid[:, tc, nn * 512:(nn + 1) * 512], ALU.add,
                                  waits=[o1])
                        big5[bk] = o2
                    guiT_free[0] = mlast
                    oat_free[0] = mlast
                    dout = p.dma("yd", "outq", out_d[to + tc * 128:to + (tc + 1) * 128, :],
                                 outt[:], waits=[o2])
                    outt_free[0] = dout
                xmid_free[0] = bn2_last[0]

            if variant < 4:
                p.dma("yd", "outq", out_d[0:128, :],
                      outt[:], waits=[GATE5])
            dstreams = [k for k in p.cnt if k not in ENGS]
            p.fence("y", [(k, p.cnt[k]) for k in dstreams])
            p.fence("y", [("t", p.cnt["t"]), ("v", p.cnt["v"]),
                          ("s", p.cnt["s"]), ("g", p.cnt["g"])])

            with nc.Block() as block:
                p.emit(block)
    return nc


_NC_CACHE = {}


def get_nc(variant=4):
    if variant not in _NC_CACHE:
        _NC_CACHE[variant] = build_program(variant)
    return _NC_CACHE[variant]


def make_in_maps(inputs):
    x = np.asarray(inputs["L0_attn_x"], dtype=np.float32)
    Wq_ = np.asarray(inputs["L0_attn_Wq"], dtype=np.float32)
    Wk_ = np.asarray(inputs["L0_attn_Wk"], dtype=np.float32)
    Wv_ = np.asarray(inputs["L0_attn_Wv"], dtype=np.float32)
    Wbeta = np.asarray(inputs["L0_attn_Wbeta"], dtype=np.float32)
    Wo_ = np.asarray(inputs["L0_attn_Wo"], dtype=np.float32)
    g_norm = np.asarray(inputs["L0_attn_g_norm"], dtype=np.float32)
    w_gate = np.asarray(inputs["L0_ffn_w_gate"], dtype=np.float32)
    w_up = np.asarray(inputs["L0_ffn_w_up"], dtype=np.float32)
    w_down = np.asarray(inputs["L0_ffn_w_down"], dtype=np.float32)
    c_ = np.asarray(inputs["L0_c"], dtype=np.float32)
    wmod = np.asarray(inputs["L0_wmod"], dtype=np.float32)

    wgT = np.ascontiguousarray(w_gate.T)
    wuT = np.ascontiguousarray(w_up.T)
    wdT = np.ascontiguousarray(w_down.T)
    gnt = np.ascontiguousarray(np.tile(g_norm, 4).reshape(8, 128).T)
    in_maps = []
    for cidx in range(8):
        b = cidx // 2
        hp = cidx % 2
        xb = x[b]
        in_maps.append(dict(
            x_tm=np.ascontiguousarray(xb),
            x_own=np.ascontiguousarray(xb[hp * NOWN:(hp + 1) * NOWN]),
            xT=np.ascontiguousarray(xb.T),
            wq=np.ascontiguousarray(Wq_[:, hp * 512:(hp + 1) * 512]),
            wk=np.ascontiguousarray(Wk_[:, hp * 512:(hp + 1) * 512]),
            wv=np.ascontiguousarray(Wv_[:, hp * 512:(hp + 1) * 512]),
            wb=np.ascontiguousarray(Wbeta[:, hp * 2:hp * 2 + 2]),
            wo=Wo_, wgT=wgT, wuT=wuT, wdT=wdT, wmod=wmod,
            cvec=np.ascontiguousarray(c_[b].reshape(8, 128).T),
            gnorm=gnt,
            tok_base=np.array([[hp * NOWN]], np.int32),
        ))
    return in_maps


import os
def kernel(**inputs):
    nc = get_nc(int(os.environ.get("KVARIANT", "4")))
    in_maps = make_in_maps(inputs)
    res = run_bass_kernel_spmd(nc, in_maps, core_ids=list(range(8)))
    out = np.empty((B, N, D), np.float32)
    for cidx in range(8):
        b, hp = cidx // 2, cidx % 2
        out[b, hp * NOWN:(hp + 1) * NOWN] = res.results[cidx]["out"]
    return out


# version 10
# speedup vs baseline: 1.0307x; 1.0194x over previous
import numpy as np
import ml_dtypes
import traceback
from contextlib import ExitStack

_DBG_ORIGIN = True

import concourse.bass as bass
import concourse.mybir as mybir
from concourse.bass_utils import run_bass_kernel_spmd

F32 = mybir.dt.float32
BF16 = mybir.dt.bfloat16
FP8 = mybir.dt.float8e4
I32 = mybir.dt.int32
AF = mybir.ActivationFunctionType
ALU = mybir.AluOpType
AX = mybir.AxisListType.X

B, N, D = 4, 8192, 1024
C = 128
TW = 512
NT = N // TW
CPT = TW // C
NOWN = N // 2
NOT = NOWN // TW
IF = 4096
C32 = 12582912.0
C64 = 6755399441055744.0
EPS = 1e-5

ENGS = ("t", "v", "s", "g", "y")
QUEUES = ("yd", "sd", "gd")


class Prog:

    def __init__(self, nc):
        self.nc = nc
        self.items = {e: [] for e in ENGS}
        self.cnt = {k: 0 for k in ENGS}
        self.wm = {e: {} for e in ENGS}
        self.sem = {}
        self.waited = {e: set() for e in ENGS}

    def make_sems(self, st):
        self._semstack = st
        for k in ENGS:
            self.sem[k] = st.enter_context(self.nc.semaphore("sem_" + k))

    def _stream_sem(self, stream):
        if stream not in self.sem:
            self.sem[stream] = self._semstack.enter_context(
                self.nc.semaphore("sem_" + stream))
            self.cnt[stream] = 0
        return self.sem[stream]

    def _need(self, eng, waits):
        wm = self.wm[eng]
        best = {}
        for tok in waits:
            if tok is None:
                continue
            s, v = tok
            if v is None:
                continue
            if v > best.get(s, -1):
                best[s] = v
        need = []
        for s, v in best.items():
            if wm.get(s, 0) >= v:
                continue
            wm[s] = v
            need.append((s, v))
            if s in ENGS:
                self.waited[s].add(v)
        return need

    def _add(self, eng, need, val, fn):
        self.items[eng].append((need, val, fn))

    def do(self, eng, op, *args, waits=(), inc=True, **kw):
        need = self._need(eng, waits)
        self.cnt[eng] += 1
        val = self.cnt[eng]

        def fn(e, op=op, args=args, kw=kw):
            return getattr(e, op)(*args, **kw)
        self._add(eng, need, val, fn)
        return (eng, val)

    def raw(self, eng, fn, waits=(), inc=True):
        need = self._need(eng, waits)
        self.cnt[eng] += 1
        val = self.cnt[eng]
        self._add(eng, need, val, fn)
        return (eng, val)

    def dma(self, queue, stream, out, in_, waits=()):
        eng = {"yd": "y", "sd": "s", "gd": "g"}[queue]
        self._stream_sem(stream)
        need = self._need(eng, waits)
        self.cnt[stream] += 16
        val = self.cnt[stream]
        sems = self.sem

        def fn(e, out=out, in_=in_, stream=stream):
            h = e.dma_start(out=out, in_=in_)
            h.then_inc(sems[stream], 16)
            return None
        self._add(eng, need, None, fn)
        return (stream, val)

    def dma_reg(self, queue, stream, fn0, waits=()):
        eng = {"yd": "y", "sd": "s", "gd": "g"}[queue]
        self._stream_sem(stream)
        need = self._need(eng, waits)
        self.cnt[stream] += 16
        sems = self.sem

        def fn(e, fn0=fn0, stream=stream):
            fn0(e).then_inc(sems[stream], 16)
            return None
        self._add(eng, need, None, fn)
        return (stream, self.cnt[stream])

    def fence(self, eng, waits):
        need = self._need(eng, waits)
        if not need:
            return
        self._add(eng, need, None, None)

    def emit(self, block):
        rank = {}
        for e in ENGS:
            rank[e] = {v: i + 1 for i, v in enumerate(sorted(self.waited[e]))}
        sems = self.sem

        def mkbody(name, items):
            r_all = rank
            waited = self.waited[name]
            rk = rank[name]

            def body(e):
                for need, val, fn in items:
                    for s, v in need:
                        if s in ENGS:
                            e.wait_ge(sems[s], r_all[s][v])
                        else:
                            e.wait_ge(sems[s], v)
                    if fn is None:
                        continue
                    h = fn(e)
                    if h is not None and val is not None and val in waited:
                        h.then_inc(sems[name], 1)
            return body
        for name, attr in (("t", "tensor"), ("v", "vector"), ("s", "scalar"),
                           ("g", "gpsimd"), ("y", "sync")):
            getattr(block, attr)(mkbody(name, self.items[name]))


def W(*toks):
    return [t for t in toks if t is not None]


def build_program(variant=4):
    nc = bass.Bass("TRN2", target_bir_lowering=False)

    def din(name, shape, dt=F32):
        return nc.dram_tensor(name, shape, dt, kind="ExternalInput").ap()

    x_tm = din("x_tm", [N, D])
    x_own = din("x_own", [NOWN, D])
    xT = din("xT", [D, N])
    wq_d = din("wq", [D, 512])
    wk_d = din("wk", [D, 512])
    wv_d = din("wv", [D, 512])
    wb_d = din("wb", [D, 2])
    wo_d = din("wo", [D, D])
    wgT_d = din("wgT", [D, IF])
    wuT_d = din("wuT", [D, IF])
    wdT_d = din("wdT", [IF, D])
    wmod_d = din("wmod", [D, 6 * D])
    cvec_d = din("cvec", [128, 8])
    gnorm_d = din("gnorm", [128, 8])
    tokbase_d = din("tok_base", [1, 1], I32)
    out_d = nc.dram_tensor("out", [NOWN, D], F32, kind="ExternalOutput").ap()
    coll_in = nc.dram_tensor("coll_in", [512, N], BF16, kind="Internal",
                             addr_space="Local").ap()
    coll_out = nc.dram_tensor("coll_out", [2, 512, N], BF16, kind="Internal",
                              addr_space="Local").ap()
    wg8_d = nc.dram_tensor("wg8_d", [128, 8, IF], FP8, kind="Internal",
                           addr_space="Local").ap()
    wu8_d = nc.dram_tensor("wu8_d", [128, 8, IF], FP8, kind="Internal",
                           addr_space="Local").ap()

    p = Prog(nc)
    with ExitStack() as stA:
        p.make_sems(stA)

        def sbA(name, shape, dt=F32):
            return stA.enter_context(nc.sbuf_tensor(name, shape, dt))

        Wq = sbA("Wq", [128, 8, 512], BF16)
        Wk = sbA("Wk", [128, 8, 512], BF16)
        Wv = sbA("Wv", [128, 8, 512], BF16)
        Wbt = sbA("Wbt", [128, 8, 2], BF16)
        Wo = sbA("Wo", [128, 8, 1024], BF16)
        Wd8 = sbA("Wd8", [128, 32, 1024], FP8)
        qaug = sbA("qaug", [1, 1024], BF16)
        kaug = sbA("kaug", [1, 1024], BF16)
        vaug = sbA("vaug", [1, 1024], BF16)
        baug = sbA("baug", [1, 4], BF16)
        eyeb = sbA("eyeb", [128, 128], BF16)
        eyef = sbA("eyef", [128, 128], F32)
        mask_sl = sbA("mask_sl", [128, 128], BF16)
        mask_ule = sbA("mask_ule", [128, 128], BF16)
        ones_c = sbA("ones_c", [128, 1], BF16)
        ones_cf = sbA("ones_cf", [128, 1], F32)
        nones_c = sbA("nones_c", [128, 1], BF16)
        ones_rf = sbA("ones_rf", [1, 128], F32)
        gdisc = sbA("gdisc", [128, 4], F32)
        g2bc = sbA("g2bc", [128, D], F32)
        s2bc = sbA("s2bc", [128, D], F32)
        gbbc = sbA("gbbc", [128, D], F32)
        gncol = sbA("gncol", [128, 8], F32)
        sagcol = sbA("sagcol", [128, 8], BF16)
        g1col = sbA("g1col", [128, 8], F32)
        wsc = sbA("wsc", [128, 8], F32)
        nthr = sbA("nthr", [128, 4], F32)
        tb_s = sbA("tb_s", [1, 1], I32)
        cv = sbA("cv", [128, 8], F32)
        silc = sbA("silc", [128, 8], BF16)
        eps_c = sbA("eps_c", [128, 1], F32)
        gatescr = sbA("gatescr", [1, 1], F32)

        pb = [stA.enter_context(nc.psum_tensor(f"pb{i}", [128, 512], F32))
              for i in range(8)]

        def MM(bank, sl, lhsT, rhs, start, stop, waits=(), inc=True):
            return p.do("t", "matmul", pb[bank][:, sl], lhsT, rhs, waits=waits,
                        start=start, stop=stop, inc=inc)

        def MMr(bank, psl, sl, lhsT, rhs, start, stop, waits=(), inc=True):
            return p.do("t", "matmul", pb[bank][psl, sl], lhsT, rhs, waits=waits,
                        start=start, stop=stop, inc=inc)

        def TRb(bank, bsl, in_, waits=(), inc=True):
            return p.do("t", "transpose", pb[bank].bitcast(BF16)[:, bsl], in_,
                        eyeb[:], waits=waits, inc=inc)

        stI = ExitStack()
        ri = stI.enter_context(nc.sbuf_tensor("ri", [128, 128], I32))
        ci = stI.enter_context(nc.sbuf_tensor("ci", [128, 128], I32))
        tio = p.do("g", "iota", ri[:], [[0, 128]], channel_multiplier=1)
        tio = p.do("g", "iota", ci[:], [[1, 128]], channel_multiplier=0)
        EYEB = p.do("v", "tensor_tensor", eyeb[:], ri[:], ci[:], ALU.is_equal,
                    waits=[tio])
        EYEF = p.do("v", "tensor_tensor", eyef[:], ri[:], ci[:], ALU.is_equal)
        MSL = p.do("v", "tensor_tensor", mask_sl[:], ri[:], ci[:], ALU.is_gt)
        MULE = p.do("v", "tensor_tensor", mask_ule[:], ri[:], ci[:], ALU.is_le)
        ONESC = p.do("v", "memset", ones_c[:], 1.0)
        ONESCF = p.do("v", "memset", ones_cf[:], 1.0)
        NONESC = p.do("v", "memset", nones_c[:], -1.0)
        ONESRF = p.do("v", "memset", ones_rf[:], 1.0)
        tg = p.do("v", "memset", gdisc[:, 0:1], 0.5)
        tg = p.do("v", "tensor_scalar", gdisc[:, 0:1], gdisc[:, 0:1], C32, -C32, ALU.add,
                  ALU.add, waits=[tg])
        GDISC = p.do("v", "tensor_scalar", gdisc[:, 0:1], gdisc[:, 0:1], -2.0, 1.0, ALU.mult,
                     ALU.add, waits=[tg])
        GDISC = p.do("v", "tensor_scalar", gdisc[:, 1:2], gdisc[:, 0:1], -1.0,
                     1.0, ALU.mult, ALU.add, waits=[GDISC])
        GDISC = p.do("v", "tensor_scalar", gdisc[:, 2:3], gdisc[:, 0:1], -C32,
                     None, ALU.mult, waits=[GDISC])
        GDISC = p.do("v", "tensor_scalar", gdisc[:, 3:4], gdisc[:, 1:2], -C64,
                     None, ALU.mult, waits=[GDISC])
        GDISC = p.do("v", "tensor_tensor", gdisc[:, 2:3], gdisc[:, 2:3],
                     gdisc[:, 3:4], ALU.add, waits=[GDISC])
        stI.close()
        EPSC = p.do("v", "memset", eps_c[:], EPS)
        d_tb = p.dma("yd", "tbq", tb_s[:], tokbase_d)
        d_cv = p.dma("yd", "cvq", cv[:], cvec_d)
        t_sc = p.do("s", "activation", cv[:], cv[:], AF.Silu, waits=[d_cv])
        t_scb = p.do("v", "tensor_copy", silc[:], cv[:], waits=[t_sc])
        gn = p.dma("yd", "gnq", gncol[:], gnorm_d)

        b7busy = [None]

        with ExitStack() as stP:
            def sbP(name, shape, dt=F32):
                return stP.enter_context(nc.sbuf_tensor(name, shape, dt))

            modrow = sbP("modrow", [1, 6 * D], F32)
            garow = sbP("garow", [1, D], F32)
            gbrow = sbP("gbrow", [1, D], F32)
            magrow = sbP("magrow", [1, D], F32)
            g1row = sbP("g1row", [1, D], F32)
            g2row = sbP("g2row", [1, D], F32)
            sagrow = sbP("sagrow", [1, D], F32)
            sagf = sbP("sagf", [128, 8], F32)
            gabc = sbP("gabc", [128, D], F32)
            wmsb = [sbP(f"wmsb{i}", [128, 8, 512], BF16) for i in range(2)]
            wstb = [sbP(f"wstb{i}", [128, 8, 512], BF16) for i in range(2)]
            wotf = [sbP(f"wotf{i}", [128, 1024], F32) for i in range(2)]
            wbstb = sbP("wbstb", [128, 8, 2], BF16)

            wm_free = [None, None]
            last_ev = None
            for nb in range(12):
                pr = nb % 2
                src = wmod_d.rearrange("(c p) m -> p c m", p=128)[:, :, nb * 512:(nb + 1) * 512]
                d = p.dma("gd", f"wmod{pr}", wmsb[pr][:], src, waits=W(wm_free[pr]))
                m = None
                for kc in range(8):
                    m = MMr(7, slice(0, 1), slice(0, 512), silc[:, kc:kc + 1],
                            wmsb[pr][:, kc, :], start=(kc == 0), stop=(kc == 7),
                            waits=W(t_scb, d, b7busy[0]) if kc == 0 else ())
                last_ev = p.do("s", "copy", modrow[0:1, nb * 512:(nb + 1) * 512],
                               pb[7][0:1, 0:512], waits=[m])
                b7busy[0] = last_ev
                wm_free[pr] = m
            MOD = last_ev

            def mrow(k):
                return modrow[0:1, k * D:(k + 1) * D]

            def clamp_gate(k, dst, after):
                t1 = p.do("v", "tensor_scalar", dst, mrow(k), 0.0, None, ALU.is_ge,
                          waits=W(after))
                t1 = p.do("v", "tensor_scalar", dst, dst, 2.0, -1.0, ALU.mult,
                          ALU.add, waits=[t1])
                t2 = p.do("v", "scalar_tensor_tensor", magrow[:], mrow(k), -1.0,
                          mrow(k), ALU.mult, ALU.max, waits=W(after))
                t2 = p.do("v", "tensor_scalar", magrow[:], magrow[:], 0.25, None,
                          ALU.max, waits=[t2])
                return p.do("v", "tensor_tensor", dst, dst, magrow[:], ALU.mult,
                            waits=[t1, t2])

            t_ga = clamp_gate(2, garow[:], MOD)
            t_gb = clamp_gate(5, gbrow[:], t_ga)
            t_g1r = p.do("v", "tensor_scalar", g1row[:], mrow(1), 1.0, None,
                         ALU.add, waits=[MOD])
            t_g2r = p.do("v", "tensor_scalar", g2row[:], mrow(4), 1.0, None,
                         ALU.add, waits=[MOD])

            def row_to_cols(row_ap, dst, after):
                m = None
                for j in range(8):
                    m = p.do("t", "transpose", pb[7][:, j:j + 1],
                             row_ap[0:1, j * 128:(j + 1) * 128], eyef[0:1, 0:1],
                             waits=W(after, EYEF, b7busy[0]) if j == 0 else ())
                ev = p.do("s", "copy", dst, pb[7][:, 0:8], waits=[m])
                b7busy[0] = ev
                return ev

            t_g1c = row_to_cols(g1row, g1col[:, :], t_g1r)
            t_r = p.do("v", "reciprocal", sagrow[:], g1row[:], waits=[t_g1r])
            t_r = p.do("v", "tensor_tensor", sagrow[:], sagrow[:], mrow(0),
                       ALU.mult, waits=[t_r])
            t_sagc = row_to_cols(sagrow, sagf[:, :], t_r)
            t_sagb = p.do("v", "tensor_copy", sagcol[:], sagf[:], waits=[t_sagc])

            def bcast_tile(row_ap, dst, after):
                ev = None
                for h in range(2):
                    m = p.do("t", "matmul", pb[7][:, 0:512], ones_rf[:],
                             row_ap[0:1, h * 512:(h + 1) * 512], start=True,
                             stop=True, waits=W(after, ONESRF, b7busy[0]))
                    ev = p.do("s", "copy", dst[:, h * 512:(h + 1) * 512],
                              pb[7][:, 0:512], waits=[m])
                    b7busy[0] = ev
                return ev

            t_g2bc = bcast_tile(g2row, g2bc, t_g2r)
            t_s2bc = bcast_tile(modrow[0:1, 3 * D:4 * D], s2bc, MOD)
            t_gbbc = bcast_tile(gbrow, gbbc, t_gb)
            t_gabc = bcast_tile(garow, gabc, t_ga)

            wfree = [None, None]

            def fold_qkv(widx, dram, Wt, augt):
                pr = widx % 2
                d = p.dma("gd", f"wst{pr}", wstb[pr][:],
                          dram.rearrange("(c p) m -> p c m", p=128),
                          waits=W(wfree[pr]))
                tt = None
                for kc in range(8):
                    tt = p.do("v", "tensor_scalar", Wt[:, kc, :], wstb[pr][:, kc, :],
                              g1col[:, kc:kc + 1], None, ALU.mult,
                              waits=W(d, t_g1c) if kc == 0 else ())
                wfree[pr] = tt
                m = None
                for kc in range(8):
                    m = MMr(7, slice(0, 1), slice(0, 512), ones_c[:], Wt[:, kc, :],
                            start=(kc == 0), stop=(kc == 7),
                            waits=W(tt, ONESC, b7busy[0]) if kc == 0 else ())
                e1 = p.do("v", "tensor_copy", augt[0:1, 0:512], pb[7][0:1, 0:512],
                          waits=[m])
                b7busy[0] = e1
                m = None
                for kc in range(8):
                    m = MMr(7, slice(0, 1), slice(0, 512), sagcol[:, kc:kc + 1],
                            Wt[:, kc, :], start=(kc == 0), stop=(kc == 7),
                            waits=W(t_sagb, b7busy[0]) if kc == 0 else ())
                e2 = p.do("v", "tensor_copy", augt[0:1, 512:1024],
                          pb[7][0:1, 0:512], waits=[m])
                b7busy[0] = e2
                return e2

            t_wq = fold_qkv(0, wq_d, Wq, qaug)
            t_wk = fold_qkv(1, wk_d, Wk, kaug)
            t_wv = fold_qkv(2, wv_d, Wv, vaug)

            d = p.dma("gd", "wst0", wbstb[:], wb_d.rearrange("(c p) m -> p c m", p=128), waits=W(wfree[0]))
            tt = None
            for kc in range(8):
                tt = p.do("v", "tensor_scalar", Wbt[:, kc, :], wbstb[:, kc, :],
                          g1col[:, kc:kc + 1], None, ALU.mult,
                          waits=W(d, t_g1c) if kc == 0 else ())
            m = None
            for kc in range(8):
                m = MMr(7, slice(0, 1), slice(0, 2), ones_c[:], Wbt[:, kc, :],
                        start=(kc == 0), stop=(kc == 7),
                        waits=W(tt, b7busy[0]) if kc == 0 else ())
            e1 = p.do("v", "tensor_copy", baug[0:1, 0:2], pb[7][0:1, 0:2], waits=[m])
            b7busy[0] = e1
            m = None
            for kc in range(8):
                m = MMr(7, slice(0, 1), slice(0, 2), sagcol[:, kc:kc + 1],
                        Wbt[:, kc, :], start=(kc == 0), stop=(kc == 7),
                        waits=W(t_sagb, b7busy[0]) if kc == 0 else ())
            t_wb = p.do("v", "tensor_copy", baug[0:1, 2:4], pb[7][0:1, 0:2], waits=[m])
            b7busy[0] = t_wb

            wofree = [None, None]
            t_wo = None
            for kc in range(8):
                pr = kc % 2
                d = p.dma("yd", f"wo{pr}", wotf[pr][:], wo_d[kc * 128:(kc + 1) * 128, :],
                          waits=W(wofree[pr]))
                u = p.do("v", "tensor_scalar", wotf[pr][:], wotf[pr][:],
                         gncol[:, kc:kc + 1], None, ALU.mult, waits=[d, gn])
                t_wo = p.do("v", "tensor_tensor", Wo[:, kc, :], wotf[pr][:],
                            gabc[:], ALU.mult, waits=[u, t_gabc])
                wofree[pr] = t_wo

            PRO_GATE = [t_wq, t_wk, t_wv, t_wb, t_wo, t_g2bc, t_s2bc, t_gbbc,
                        t_sagb, MOD, t_gabc, t_sagc]

        PROJ_RDY = [t_wq, t_wk, t_wv, t_wb]
        GATEP = p.do("v", "memset", gatescr[:], 0.0, waits=PRO_GATE)

        with ExitStack() as stB:
            def sbB(name, shape, dt=F32):
                return stB.enter_context(nc.sbuf_tensor(name, shape, dt))

            xTb = [sbB(f"xTb{i}", [128, 8, TW], BF16) for i in range(2)]
            xtmt = sbB("xtmt", [128, 2, D], F32)
            qT = [sbB(f"qT{i}", [128, 4, TW], BF16) for i in range(2)]
            kT = [sbB(f"kT{i}", [128, 4, TW], BF16) for i in range(2)]
            vt = [sbB(f"vt{i}", [128, 4, TW], BF16) for i in range(2)]
            sqt = sbB("sqt", [128, TW], BF16)
            osq = sqt
            augT_ = sbB("augT_", [1, 2 * TW], BF16)
            augT = [augT_, augT_]
            rows_ = sbB("rows_", [1, 8 * TW], F32)
            eqrow = sbB("eqrow", [1, 2, 2 * TW], BF16)
            rowsb = sbB("rowsb", [1, 4 * TW], F32)
            bnsS = sbB("bnsS", [128, 12], F32)
            bnaS = sbB("bnaS", [128, 8], F32)
            ost_ = sbB("ost_", [128, 4, TW], BF16)
            ost = [ost_, ost_]
            Pt = sbB("Pt", [128, 2, 128], BF16)
            Zt = sbB("Zt", [128, 2, 128], BF16)
            Rt = [sbB(f"Rt{j_}", [128, 2, 128], BF16) for j_ in range(2)]
            Mtt = [sbB(f"Mtt{j_}", [128, 2, 128], BF16) for j_ in range(2)]
            Ktm = [sbB(f"Ktm{j_}", [128, 2, 256], BF16) for j_ in range(2)]
            Kkap = sbB("Kkap", [128, 2, 256], BF16)
            Wpt = [sbB(f"Wpt{j_}", [128, 4, 128], BF16) for j_ in range(2)]
            Ut = sbB("Ut", [128, 256], BF16)
            Sst = [sbB(f"Sst{h}", [128, 512], BF16) for h in range(2)]
            Sf32 = [sbB(f"Sf32{h}", [128, 512], F32) for h in range(2)]
            statc2 = [sbB(f"statc{j_}", [128, 4, 4], F32) for j_ in range(2)]
            rowsc = sbB("rowsc", [1, 128], F32)
            rbc = sbB("rbc", [128, 128], BF16)
            ffst_ = sbB("ffst_", [128, IF // 2], BF16)
            ffst = [ffst_, ffst_]
            tpos = sbB("tpos", [128, IF], BF16)
            absdmp = tpos
            w8tmp = sbB("w8tmp", [128, IF], FP8)
            tneg = w8tmp
            absacc = sbB("absacc", [128, 35], F32)

            RM, RVAR, RSIG, RINV = 0, 1, 2, 3
            RB0, RNN0, REQ0 = 4, 6, 8

            def R(pr, r, n=1, cs=None):
                if cs is None:
                    return rows_[0:1, r * TW:(r + n) * TW]
                return rows_[0:1, r * TW + cs.start:r * TW + cs.stop]

            def Rb(r, n=1, cs=None):
                if cs is None:
                    return rowsb[0:1, r * TW:(r + n) * TW]
                return rowsb[0:1, r * TW + cs.start:r * TW + cs.stop]

            bg = []
            ffree = [None, None]
            abs_done = {}
            abs_last = {}
            dmp_last = [None]
            tern_last = [None]
            w8_free = [None]
            GATE_BG = [None]

            def mk_abs(mi, dram, nch, chw):
                def mkgo(j, hh):
                    def go():
                        pr = 0
                        c2 = chw // 2
                        dd = p.dma("gd", "ffst", ffst[pr][:, 0:c2],
                                   dram[j * 128:(j + 1) * 128,
                                        hh * c2:(hh + 1) * c2],
                                   waits=W(ffree[pr]))
                        col = j if hh == 0 else 32
                        a = p.do("s", "activation", absdmp[:, 0:c2],
                                 ffst[pr][:, 0:c2], AF.Abs,
                                 accum_out=absacc[:, col:col + 1],
                                 waits=W(dd, dmp_last[0],
                                         abs_done.get(mi - 1) if mi > 0 else None,
                                         abs_last.get(mi)))
                        dmp_last[0] = a
                        ffree[pr] = a
                        if hh == 1:
                            a = p.do("v", "tensor_tensor", absacc[:, j:j + 1],
                                     absacc[:, j:j + 1], absacc[:, 32:33],
                                     ALU.add, waits=[a, abs_last.get(mi)])
                        abs_last[mi] = a
                    return go
                for j in range(nch):
                    for hh in range(2):
                        bg.append(mkgo(j, hh))

                def fin():
                    r = p.do("v", "tensor_reduce", absacc[:, 33:34],
                             absacc[:, 0:nch], AX, ALU.add,
                             waits=[abs_last[mi]])
                    mm_ = p.do("t", "matmul", pb[7][0:1, 0:1], ones_cf[:],
                               absacc[:, 33:34], start=True, stop=True,
                               waits=W(r, ONESCF, b7busy[0]))
                    nelem = float(D * IF)
                    s1 = p.do("v", "tensor_scalar", absacc[0:1, 34:35],
                              pb[7][0:1, 0:1], 1.0 / nelem, 1e-5, ALU.mult,
                              ALU.max, waits=[mm_])
                    b7busy[0] = s1
                    mm2 = p.do("t", "matmul", pb[7][:, 4:5], ones_rf[:],
                               absacc[0:1, 34:35], start=True, stop=True,
                               waits=W(s1, b7busy[0]))
                    e1 = p.do("v", "tensor_copy", wsc[:, 3 + mi:4 + mi],
                              pb[7][:, 4:5], waits=[mm2])
                    b7busy[0] = e1
                    w1 = p.do("v", "tensor_scalar", wsc[:, mi:mi + 1],
                              wsc[:, 3 + mi:4 + mi], 0.5, None, ALU.mult,
                              waits=[e1])
                    w2 = p.do("v", "tensor_scalar", nthr[:, mi:mi + 1],
                              wsc[:, mi:mi + 1], -1.0, None, ALU.mult, waits=[w1])
                    abs_done[mi] = w2
                bg.append(fin)

            def mk_tern(mi, dram, dst_sb, dst_dram, nch, chw):
                def mkgo(j, hh):
                    def go():
                        pr = 0
                        c2 = chw // 2
                        dd = p.dma("gd", "ffst", ffst[pr][:, 0:c2],
                                   dram[j * 128:(j + 1) * 128,
                                        hh * c2:(hh + 1) * c2],
                                   waits=W(ffree[pr]))
                        a1 = p.do("v", "tensor_scalar", tpos[:, 0:c2],
                                  ffst[pr][:, 0:c2], wsc[:, mi:mi + 1], None,
                                  ALU.is_gt, waits=W(dd, abs_done[mi], tern_last[0]))
                        a2 = p.do("v", "tensor_scalar", w8tmp[:, 0:c2],
                                  ffst[pr][:, 0:c2], nthr[:, mi:mi + 1], None,
                                  ALU.is_lt, waits=W(dd, w8_free[0]))
                        if dst_sb is not None:
                            a3 = p.do("v", "tensor_tensor",
                                      dst_sb[:, j, hh * c2:(hh + 1) * c2],
                                      tpos[:, 0:c2], w8tmp[:, 0:c2],
                                      ALU.subtract, waits=[a1, a2])
                            w8_free[0] = a3
                        else:
                            a3 = p.do("v", "tensor_tensor",
                                      w8tmp[:, 0:c2], tpos[:, 0:c2],
                                      w8tmp[:, 0:c2], ALU.subtract,
                                      waits=[a1, a2])
                            a4 = p.dma("gd", "w8out",
                                       dst_dram[:, j, hh * c2:(hh + 1) * c2],
                                       w8tmp[:, 0:c2], waits=[a3])
                            w8_free[0] = a4
                        ffree[pr] = a3
                        tern_last[0] = a3
                    return go
                for j in range(nch):
                    for hh in range(2):
                        bg.append(mkgo(j, hh))

            mk_abs(0, wgT_d, 8, IF)
            mk_tern(0, wgT_d, None, wg8_d, 8, IF)
            mk_abs(1, wuT_d, 8, IF)
            mk_tern(1, wuT_d, None, wu8_d, 8, IF)
            mk_abs(2, wdT_d, 32, 1024)
            mk_tern(2, wdT_d, Wd8, None, 32, 1024)

            def fin_muxd():
                t1 = p.do("v", "tensor_tensor", wsc[:, 6:7], wsc[:, 4:5],
                          wsc[:, 5:6], ALU.mult, waits=[abs_done[1], abs_done[2]])
                t1 = p.do("v", "tensor_scalar", wsc[:, 6:7], wsc[:, 6:7],
                          1.0 / 127.0, None, ALU.mult, waits=[t1])
                abs_done[3] = t1
            bg.append(fin_muxd)
            bgi = [0]

            def run_bg(k):
                n = 0
                while n < k and bgi[0] < len(bg):
                    bg[bgi[0]]()
                    bgi[0] += 1
                    n += 1

            GATE0 = GATEP
            ffree[0] = GATE0
            ffree[1] = GATE0
            dmp_last[0] = GATE0
            tern_last[0] = GATE0
            w8_free[0] = GATE0
            xT_free = [GATE0, GATE0]
            xtm_free = [GATE0]
            q_free = [GATE0, GATE0]
            k_free = [GATE0, GATE0]
            v_free = [GATE0, GATE0]
            aug_free = [GATE0]
            rows_free = [GATE0]
            ost_free = [GATE0]
            bn_last = [None]
            bna_free = [None]
            big_free = [None, None]
            ch_free = [(), ()]
            b6_free = [None]
            b6u_free = [None]
            pend_chain = [None]
            sq_last = [None]
            osq_free = [None]
            Pt_free = [GATE0, GATE0]
            Zt_free = [GATE0, GATE0]
            Rt_free = [[GATE0, GATE0], [GATE0, GATE0]]
            Mt_free = [[GATE0, GATE0], [GATE0, GATE0]]
            Ktm_free = [[GATE0, GATE0], [GATE0, GATE0]]
            Kkap_free = [GATE0, GATE0]
            Wpt_free = [[GATE0, GATE0], [GATE0, GATE0]]
            Ut_free = [GATE0]
            rowsc_free = [GATE0]
            rbc_free = [GATE0]
            statc_free = [GATE0, GATE0]
            chain_tok = [None, None]
            state_guard = [None, None]

            prep_q = []
            prep_st = {}

            def run_prep(k=1):
                for _ in range(k):
                    if prep_q:
                        prep_q.pop(0)()

            def queue_prep(i):
                pr = i % 2
                t0 = i * TW
                st = {}
                prep_st[i] = st
                def _step0():
                    st['dx'] = p.dma("gd", f"xTb{pr}", xTb[pr][:],
                               xT.rearrange("(c p) t -> p c t", p=128)[:, :, t0:t0 + TW],
                               waits=W(xT_free[pr]))
                    st['bt'] = None
                    bn_prev = [xtm_free[0], xtm_free[0]]
                    for c_ in range(CPT):
                        sl2 = c_ % 2
                        dxm = p.dma("yd", f"xtm{sl2}", xtmt[:, sl2, :],
                                    x_tm[t0 + c_ * 128:t0 + (c_ + 1) * 128, :],
                                    waits=W(bn_prev[sl2]))
                        b1 = p.do("v", "bn_stats", bnsS[:, 0:6], xtmt[:, sl2, 0:512],
                                  waits=W(dxm, bn_last[0]))
                        b1 = p.do("v", "bn_stats", bnsS[:, 6:12], xtmt[:, sl2, 512:1024],
                                  waits=[b1])
                        b2 = p.do("v", "bn_aggr", bnaS[:, 2 * c_:2 * c_ + 2], bnsS[:],
                                  waits=W(b1, bna_free[0] if c_ == 0 else None))
                        bn_last[0] = b2
                        bn_prev[sl2] = b1
                        st['bt'] = b2
                    xtm_free[0] = st['bt']
                prep_q.append(_step0)
                def _step1():
                    m = None
                    for c_ in range(CPT):
                        m = p.do("t", "transpose", pb[7][0:1, c_ * 128:(c_ + 1) * 128],
                                 bnaS[:, 2 * c_:2 * c_ + 1], eyef[:],
                                 waits=W(st['bt'], EYEF, b7busy[0]) if c_ == 0 else ())
                    ev = p.do("s", "copy", R(pr, RM), pb[7][0:1, 0:512],
                              waits=W(m, rows_free[0]))
                    b7busy[0] = ev
                    rows_free[0] = None
                    m = None
                    for c_ in range(CPT):
                        m = p.do("t", "transpose", pb[7][0:1, c_ * 128:(c_ + 1) * 128],
                                 bnaS[:, 2 * c_ + 1:2 * c_ + 2], eyef[:],
                                 waits=W(b7busy[0]) if c_ == 0 else ())
                    bna_free[0] = m
                    ev = p.do("s", "copy", R(pr, RVAR), pb[7][0:1, 0:512], waits=[m])
                    b7busy[0] = ev
                    t_sig = p.do("s", "activation", R(pr, RSIG), R(pr, RVAR), AF.Sqrt,
                                 bias=eps_c[0:1, 0:1], waits=W(ev, EPSC))
                    st['t_inv'] = p.do("v", "reciprocal", R(pr, RINV), R(pr, RSIG),
                                 waits=[t_sig])
                    a1 = p.do("v", "tensor_scalar", augT[pr][0:1, 0:TW],
                              R(pr, RM), -1.0, None, ALU.mult,
                              waits=W(ev, aug_free[0]))
                    a2 = p.do("v", "tensor_copy", augT[pr][0:1, TW:2 * TW],
                              R(pr, RSIG), waits=[t_sig])
                    st['AUG'] = [a1, a2]
                prep_q.append(_step1)
                def _step2():
                    def proj_fm(Wt, waug, dstT, dst_free_tok, sign_ones, row0, sgn_tok):
                        last_tok = None
                        for oc in range(4):
                            bk = oc % 2
                            m = None
                            for kc in range(8):
                                m = MM(bk, slice(0, 512),
                                       Wt[:, kc, oc * 128:(oc + 1) * 128],
                                       xTb[pr][:, kc, :], start=(kc == 0), stop=False,
                                       waits=W(st['dx'], *PROJ_RDY, big_free[bk], *st['AUG'])
                                       if kc == 0 else ())
                            m = MM(bk, slice(0, 512),
                                   waug[0:1, oc * 128:(oc + 1) * 128],
                                   augT[pr][0:1, 0:TW], start=False, stop=False)
                            m = MM(bk, slice(0, 512),
                                   waug[0:1, 512 + oc * 128:512 + (oc + 1) * 128],
                                   augT[pr][0:1, TW:2 * TW], start=False, stop=True)
                            e = p.do("s", "copy", dstT[:, oc, :], pb[bk][:, 0:512],
                                     waits=W(m, dst_free_tok if oc == 0 else None))
                            sq = p.do("s", "activation", sqt[:], pb[bk][:, 0:512],
                                      AF.Square, waits=W(sq_last[0]))
                            mm2 = MMr(7, slice(0, 1), slice(0, 512), sign_ones, sqt[:],
                                      start=(oc % 2 == 0), stop=(oc % 2 == 1),
                                      waits=W(sq, b7busy[0], sgn_tok)
                                      if oc % 2 == 0 else [sq])
                            big_free[bk] = sq
                            sq_last[0] = mm2
                            if oc % 2 == 1:
                                if row0 is None:
                                    e2 = p.do("v", "tensor_scalar",
                                              eqrow[0:1, pr, (oc // 2) * TW:
                                                    (oc // 2 + 1) * TW],
                                              pb[7][0:1, 0:512], EPS, None,
                                              ALU.mult, waits=[mm2])
                                else:
                                    e2 = p.do("v", "tensor_copy",
                                              R(pr, row0 + oc // 2),
                                              pb[7][0:1, 0:512], waits=[mm2])
                                b7busy[0] = e2
                                last_tok = e2
                        return last_tok
                    st['proj_fm'] = proj_fm
                prep_q.append(_step2)
                def _step3():
                    st['qsq'] = st['proj_fm'](Wq, qaug, qT[pr], q_free[pr], ones_c[:],
                                        None, ONESC)
                    q_free[pr] = None
                prep_q.append(_step3)
                def _step4():
                    st['ksq'] = st['proj_fm'](Wk, kaug, kT[pr], k_free[pr], nones_c[:], RNN0,
                                  NONESC)
                    k_free[pr] = None
                prep_q.append(_step4)
                def _step5():
                    st['tb1'] = None
                    for hh in range(2):
                        m = None
                        for kc in range(8):
                            m = MMr(7, slice(0, 1), slice(0, 512),
                                    Wbt[:, kc, hh:hh + 1],
                                    xTb[pr][:, kc, :], start=(kc == 0), stop=False,
                                    waits=W(b7busy[0]) if kc == 0 else ())
                        m = MMr(7, slice(0, 1), slice(0, 512), baug[0:1, hh:hh + 1],
                                augT[pr][0:1, 0:TW], start=False, stop=False)
                        m = MMr(7, slice(0, 1), slice(0, 512), baug[0:1, 2 + hh:3 + hh],
                                augT[pr][0:1, TW:2 * TW], start=False, stop=True)
                        st['tb1'] = p.do("v", "tensor_tensor", R(pr, RB0 + hh),
                                   pb[7][0:1, 0:512], R(pr, RINV), ALU.mult,
                                   waits=[m, st['t_inv']])
                        b7busy[0] = st['tb1']
                prep_q.append(_step5)
                def _step6():
                    t_bsig = p.do("s", "activation", R(pr, RB0, 2), R(pr, RB0, 2),
                                  AF.Sigmoid, waits=[st['tb1']])
                    t_sn = p.do("s", "activation", R(pr, RNN0, 2), R(pr, RNN0, 2),
                                AF.Sqrt, scale=-1.0, waits=[st['ksq']])
                    t_dl = p.do("v", "reciprocal", R(pr, RNN0, 2), R(pr, RNN0, 2),
                                waits=[t_sn])
                    t_bd = p.do("v", "tensor_tensor", R(pr, RB0, 2), R(pr, RB0, 2),
                                R(pr, RNN0, 2), ALU.mult, waits=[t_dl, t_bsig])
                    st['t_ng'] = p.do("v", "scalar_tensor_tensor", Rb(0, 2), R(pr, RB0, 2),
                                -1.0, R(pr, RNN0, 2), ALU.mult, ALU.mult,
                                waits=[t_bd])
                    st['t_gv0'] = p.do("v", "tensor_tensor", Rb(2), R(pr, RB0),
                                 R(pr, RINV), ALU.mult, waits=[t_bd])
                    st['t_gv1'] = p.do("v", "tensor_tensor", Rb(3), R(pr, RB0 + 1),
                                 R(pr, RINV), ALU.mult, waits=[t_bd])
                prep_q.append(_step6)
                def _step7():
                    m = None
                    first = True
                    for c_ in range(CPT):
                        for st_ in range(4):
                            m = p.do("t", "transpose",
                                     pb[7][:, c_ * 4 + st_:c_ * 4 + st_ + 1],
                                     Rb(st_, cs=slice(c_ * 128, (c_ + 1) * 128)),
                                     ones_cf[0:1, 0:1],
                                     waits=W(st['t_ng'], st['t_gv0'], st['t_gv1'], b7busy[0],
                                             statc_free[pr]) if first else ())
                            first = False
                    st['STATC'] = p.do("s", "copy", statc2[pr][:].rearrange("p c m -> p (c m)"),
                                 pb[7][:, 0:16], waits=[m])
                    b7busy[0] = st['STATC']
                    statc_free[pr] = None
                prep_q.append(_step7)
                def _step8():
                    st['vlast'] = None
                    for tc in range(CPT):
                        bk = tc % 2
                        m = None
                        for kc in range(8):
                            m = MM(bk, slice(0, 512),
                                   xTb[pr][:, kc, tc * 128:(tc + 1) * 128],
                                   Wv[:, kc, :], start=(kc == 0), stop=False,
                                   waits=W(big_free[bk]) if kc == 0 else ())
                        m = MM(bk, slice(0, 512),
                               augT[pr][0:1, tc * 128:(tc + 1) * 128],
                               vaug[0:1, 0:512], start=False, stop=False)
                        m = MM(bk, slice(0, 512),
                               augT[pr][0:1, TW + tc * 128:TW + (tc + 1) * 128],
                               vaug[0:1, 512:1024], start=False, stop=True)
                        e0 = p.do("v", "tensor_scalar", vt[pr][:, tc, 0:256],
                                  pb[bk][:, 0:256], statc2[pr][:, tc, 2:3], None, ALU.mult,
                                  waits=W(m, st['STATC'], v_free[pr] if tc == 0 else None))
                        e1 = p.do("v", "tensor_scalar", vt[pr][:, tc, 256:512],
                                  pb[bk][:, 256:512], statc2[pr][:, tc, 3:4], None,
                                  ALU.mult, waits=[m])
                        big_free[bk] = e1
                        st['vlast'] = e1
                    v_free[pr] = None
                    aug_free[0] = m
                    xT_free[pr] = m
                    rows_free[0] = st['t_gv1']
                    run_bg(4)
                prep_q.append(_step8)

            for i in (range(NT) if variant >= 1 else []):
                pr = i % 2
                t0 = i * TW
                if i == 0:
                    queue_prep(0)
                    run_prep(99)
                if i + 1 < NT:
                    queue_prep(i + 1)
                st = prep_st[i]
                for c_ in (range(CPT) if variant >= 2 else []):
                    sub = variant if variant < 10 else variant - 20
                    g = i * CPT + c_
                    cs = slice(c_ * 128, (c_ + 1) * 128)
                    bp = c_ % 2
                    pp = 2 + (g % 2)
                    ppo = 2 + ((g + 1) % 2)
                    solved = [None, None]
                    for h in range(2):
                        m = None
                        for jj in range(2):
                            m = MM(pp, slice(h * 128, (h + 1) * 128),
                                   kT[pr][:, 2 * h + jj, cs],
                                   kT[pr][:, 2 * h + jj, cs],
                                   start=(jj == 0), stop=(jj == 1),
                                   waits=W(*ch_free[pp - 2], st['ksq']) if jj == 0 else ())
                        mA = m
                        m = None
                        for jj in range(2):
                            m = MM(pp, slice(256 + h * 128, 256 + (h + 1) * 128),
                                   kT[pr][:, 2 * h + jj, cs],
                                   qT[pr][:, 2 * h + jj, cs],
                                   start=(jj == 0), stop=(jj == 1),
                                   waits=W(st['qsq']) if jj == 0 else ())
                        mM = m
                        eP = p.do("v", "scalar_tensor_tensor", Pt[:, h, :],
                                  pb[pp][:, h * 128:(h + 1) * 128],
                                  statc2[pr][:, c_, h:h + 1], mask_sl[:], ALU.mult,
                                  ALU.mult, waits=W(mA, st['STATC'], MSL, Pt_free[h]))
                        Pt_free[h] = None
                        eM = p.do("v", "tensor_tensor", Mtt[bp][:, h, :],
                                  pb[pp][:, 256 + h * 128:256 + (h + 1) * 128],
                                  mask_ule[:], ALU.mult,
                                  waits=W(mM, MULE, Mt_free[bp][h]))
                        Mt_free[bp][h] = None
                        ch_free[pp - 2] = (eM,)
                        if variant > 10 and sub < 2:
                            solved[h] = eM
                            continue
                        mz = TRb(ppo, slice(512 * h, 512 * h + 128), Pt[:, h, :],
                                 waits=W(eP, EYEB, *ch_free[ppo - 2]))
                        mk2 = TRb(ppo, slice(512 * h + 128, 512 * h + 256),
                                  kT[pr][:, 2 * h, cs], waits=())
                        mk2 = TRb(ppo, slice(512 * h + 256, 512 * h + 384),
                                  kT[pr][:, 2 * h + 1, cs], waits=())
                        eZ = p.do("s", "copy", Zt[:, h, :],
                                  pb[ppo].bitcast(BF16)[:, 512 * h:512 * h + 128],
                                  waits=W(mz, Zt_free[h]))
                        Zt_free[h] = None
                        eK = p.do("s", "copy", Ktm[bp][:, h, :],
                                  pb[ppo].bitcast(BF16)[:, 512 * h + 128:512 * h + 384],
                                  waits=W(mk2, Ktm_free[bp][h]))
                        Ktm_free[bp][h] = None
                        ch_free[ppo - 2] = (eK,)
                        eKk = p.do("v", "tensor_scalar", Kkap[:, h, :],
                                   Ktm[bp][:, h, :], statc2[pr][:, c_, h:h + 1], None,
                                   ALU.mult, waits=W(eK, Kkap_free[h]))
                        Kkap_free[h] = None
                        if variant > 10 and sub < 3:
                            solved[h] = eKk
                            continue
                        r1 = MM(6, slice(h * 128, (h + 1) * 128), eyeb[:], eyeb[:],
                                start=True, stop=False, waits=W(EYEB, b6_free[0]))
                        r2 = MM(6, slice(h * 128, (h + 1) * 128), Pt[:, h, :],
                                eyeb[:], start=False, stop=True, waits=[eP])
                        lr = p.do("s", "copy", Rt[bp][:, h, :],
                                  pb[6][:, h * 128:(h + 1) * 128],
                                  waits=W(r2, Rt_free[bp][h]))
                        Rt_free[bp][h] = None
                        b6_free[0] = lr
                        lp, lz = eP, eZ
                        for lvl in range(1, 7):
                            bkl = 2 + ((g + lvl) % 2)
                            m1 = MM(bkl, slice(h * 128, (h + 1) * 128), Zt[:, h, :],
                                    Pt[:, h, :], start=True, stop=True,
                                    waits=W(lp, lz, *ch_free[bkl - 2])
                                    if lvl == 1 else ())
                            m2 = MM(bkl, slice(256 + h * 128, 256 + (h + 1) * 128),
                                    Pt[:, h, :], Zt[:, h, :], start=True, stop=True)
                            lz = p.do("s", "copy", Zt[:, h, :],
                                      pb[bkl][:, 256 + h * 128:256 + (h + 1) * 128],
                                      waits=[m2])
                            lp = p.do("v", "tensor_copy", Pt[:, h, :],
                                      pb[bkl][:, h * 128:(h + 1) * 128],
                                      waits=[lz])
                            ch_free[bkl - 2] = (lp, lz)
                            rm = MM(6, slice(h * 128, (h + 1) * 128), eyeb[:],
                                    Rt[bp][:, h, :], start=True, stop=False,
                                    waits=[lp])
                            rm = MM(6, slice(h * 128, (h + 1) * 128), Pt[:, h, :],
                                    Rt[bp][:, h, :], start=False, stop=True)
                            lr = p.do("s", "copy", Rt[bp][:, h, :],
                                      pb[6][:, h * 128:(h + 1) * 128], waits=[rm])
                            b6_free[0] = lr
                            run_prep(1)
                        if variant > 10 and sub < 4:
                            solved[h] = lr
                            continue
                        m = MM(pp, slice(0, 128), Kkap[:, h, 0:128], Rt[bp][:, h, :],
                               start=True, stop=True,
                               waits=W(eKk, lr, *ch_free[pp - 2]))
                        m2 = MM(pp, slice(128, 256), Kkap[:, h, 128:256],
                                Rt[bp][:, h, :], start=True, stop=True)
                        eW = p.do("s", "copy",
                                  Wpt[bp][:].rearrange("p c m -> p (c m)")[:, h * 256:(h + 1) * 256],
                                  pb[pp][:, 0:256], waits=W(m2, Wpt_free[bp][h]))
                        Wpt_free[bp][h] = None
                        ch_free[pp - 2] = (eW,)
                        Kkap_free[h] = eW
                        solved[h] = eW
                        Pt_free[h] = eW
                        Zt_free[h] = lz
                        run_prep(1)

                    def mk_chain(c_=c_, g=g, cs=cs, pp=pp, bp=bp,
                                 solved=tuple(solved), vlast=st['vlast'], pr=pr,
                                 sub=sub, qsq=st['qsq']):
                      for h in (range(2) if (variant < 10 or sub >= 4) else []):
                        mU = MM(6, slice(256, 512), Rt[bp][:, h, :],
                                vt[pr][:, c_, h * 256:(h + 1) * 256],
                                start=True, stop=(g == 0),
                                waits=W(solved[0], solved[1], vlast, b6u_free[0],
                                        chain_tok[h]))
                        if g > 0:
                            for jj in range(2):
                                mU = MM(6, slice(256, 512),
                                        Wpt[bp][:, 2 * h + jj, :],
                                        Sst[h][:, jj * 256:(jj + 1) * 256],
                                        start=False, stop=(jj == 1))
                        eU = p.do("s", "copy", Ut[:], pb[6][:, 256:512],
                                  waits=W(mU, Ut_free[0]))
                        b6u_free[0] = eU
                        mo = None
                        for dd in range(2):
                            if g > 0:
                                for jj in range(2):
                                    mo = MM(6, slice(256 + dd * 128, 256 + (dd + 1) * 128),
                                            Sst[h][:, jj * 256 + dd * 128:jj * 256 + (dd + 1) * 128],
                                            qT[pr][:, 2 * h + jj, cs],
                                            start=(jj == 0), stop=False,
                                            waits=W(eU, b6u_free[0])
                                            if (dd == 0 and jj == 0) else ())
                                mo = MM(6, slice(256 + dd * 128, 256 + (dd + 1) * 128),
                                        Ut[:, dd * 128:(dd + 1) * 128], Mtt[bp][:, h, :],
                                        start=False, stop=True)
                            else:
                                mo = MM(6, slice(256 + dd * 128, 256 + (dd + 1) * 128),
                                        Ut[:, dd * 128:(dd + 1) * 128], Mtt[bp][:, h, :],
                                        start=True, stop=True,
                                        waits=W(eU, b6u_free[0]) if dd == 0 else ())
                        sqo = p.do("s", "activation", osq[:, 0:256],
                                   pb[6][:, 256:512],
                                   AF.Square, waits=W(mo, osq_free[0], sq_last[0]))
                        sq_last[0] = sqo
                        mo2 = MMr(7, slice(0, 1), slice(0, 128), ones_c[:],
                                  osq[:, 0:128], start=True, stop=False,
                                  waits=W(sqo, b7busy[0]))
                        mo2 = MMr(7, slice(0, 1), slice(0, 128), ones_c[:],
                                  osq[:, 128:256], start=False, stop=True)
                        t1 = p.do("v", "tensor_scalar", rowsc[0:1, :],
                                  pb[7][0:1, 0:128], 1.0 / 256.0, None, ALU.mult,
                                  waits=W(mo2, rowsc_free[0]))
                        b7busy[0] = t1
                        t1 = p.do("v", "tensor_tensor", rowsc[0:1, :],
                                  rowsc[0:1, :],
                                  eqrow[0:1, pr, h * TW + cs.start:h * TW + cs.stop],
                                  ALU.add, waits=[t1, qsq])
                        t1 = p.do("s", "activation", rowsc[0:1, :],
                                  rowsc[0:1, :], AF.Sqrt, waits=[t1])
                        t1 = p.do("v", "reciprocal", rowsc[0:1, :],
                                  rowsc[0:1, :], waits=[t1])
                        mb = p.do("t", "matmul", pb[7][:, 128:256], ones_rf[:],
                                  rowsc[0:1, :], start=True, stop=True,
                                  waits=W(t1, b7busy[0]))
                        eb = p.do("s", "copy", rbc[:], pb[7][:, 128:256],
                                  waits=W(mb, rbc_free[0]))
                        b7busy[0] = eb
                        oe = None
                        for dd in range(2):
                            oe = p.do("v", "tensor_tensor",
                                      ost[pr][:, 2 * h + dd, cs],
                                      pb[6][:, 256 + dd * 128:256 + (dd + 1) * 128],
                                      rbc[:], ALU.mult,
                                      waits=W(eb, ost_free[0]
                                              if (c_ == 0 and h == 0 and dd == 0)
                                              else None))
                        if c_ == 0 and h == 0:
                            ost_free[0] = None
                        rbc_free[0] = oe
                        rowsc_free[0] = oe
                        b6u_free[0] = oe
                        osq_free[0] = mo2
                        sq_last[0] = mo2
                        ms = None
                        for jj in range(2):
                            ms = MM(4 + h, slice(jj * 256, (jj + 1) * 256),
                                    Ktm[bp][:, h, jj * 128:(jj + 1) * 128], Ut[:],
                                    start=(jj == 0), stop=(jj == 1),
                                    waits=W(eU, state_guard[h]) if jj == 0 else ())
                        if g == 0:
                            es = p.do("v", "tensor_copy", Sf32[h][:], pb[4 + h][:],
                                      waits=[ms])
                        else:
                            es = p.do("v", "tensor_tensor", Sf32[h][:], Sf32[h][:],
                                      pb[4 + h][:], ALU.add, waits=[ms])
                        es = p.do("s", "copy", Sst[h][:], Sf32[h][:], waits=[es])
                        state_guard[h] = es
                        Ut_free[0] = ms
                        chain_tok[h] = es
                        Ktm_free[bp][h] = ms
                        Rt_free[bp][h] = mU
                        Mt_free[bp][h] = mo
                        Wpt_free[bp][h] = mU
                    if pend_chain[0] is not None:
                        pend_chain[0]()
                    pend_chain[0] = mk_chain

                if pend_chain[0] is not None:
                    pend_chain[0]()
                    pend_chain[0] = None
                run_prep(99)
                if variant >= 2 and (variant < 10 or (variant - 20) >= 4):
                    do_ = p.dma("gd", f"ocoll{pr}",
                                coll_in.rearrange("(c p) t -> p c t", p=128)[:, :, t0:t0 + TW],
                                ost[pr][:], waits=W(chain_tok[0], chain_tok[1]))
                    ost_free[0] = do_
                q_free[pr] = chain_tok[1]
                k_free[pr] = chain_tok[1]
                v_free[pr] = chain_tok[1]
                statc_free[pr] = chain_tok[1]
                run_bg(4)

            run_bg(len(bg))
            FFN_RDY = W(tern_last[0], w8_free[0], abs_done.get(0),
                        abs_done.get(1), abs_done.get(2), abs_done.get(3))
            DUMMY_OUT = ost[0]
            PH23 = [("t", p.cnt["t"]), ("v", p.cnt["v"]), ("s", p.cnt["s"]),
                    ("g", p.cnt["g"])]

        GATE5 = p.do("v", "memset", gatescr[:], 0.0, waits=PH23 + FFN_RDY)
        if variant >= 3:
            COLL = p.do("g", "collective_compute", "AllGather", ALU.bypass,
                        replica_groups=[[0, 1], [2, 3], [4, 5], [6, 7]],
                        ins=[coll_in], outs=[coll_out],
                        waits=[("ocoll0", p.cnt.get("ocoll0", 0)),
                               ("ocoll1", p.cnt.get("ocoll1", 0))])
        else:
            COLL = GATE5

        with ExitStack() as stC:
            def sbC(name, shape, dt=F32):
                return stC.enter_context(nc.sbuf_tensor(name, shape, dt))

            sc8k = sbC("sc8k", [128, 4096], BF16)
            oat = sc8k.rearrange("p (c t) -> p c t", c=8)
            guiT = sc8k.rearrange("p (c t) -> p c t", c=32)
            xot = sbC("xot", [128, 4, D], F32)
            xmid = sbC("xmid", [128, 4, D], F32)
            h2t = sbC("h2t", [128, D], F32)
            h2i = sbC("h2i", [128, D], BF16)
            h2iT = sbC("h2iT", [128, 4, 8, 128], BF16)
            gub = [sbC(f"gub{t}", [128, IF], BF16) for t in range(4)]
            zt_ = sbC("zt_", [128, 512], F32)
            silg = zt_
            r32t = sbC("r32t", [128, 512], F32)
            r64t = sbC("r64t", [128, 512], F32)
            outt = sbC("outt", [128, 1024], F32)
            bns2 = sbC("bns2", [128, 12], F32)
            cols2 = sbC("cols2", [128, 4, 16], F32)
            wgst = [sbC(f"wgst{t}", [128, 8, 512], FP8) for t in range(2)]
            wust_ = sbC("wust_", [128, 8, 512], FP8)
            wust = [wust_, wust_]

            r0 = stC.enter_context(nc.sync.register("r0"))
            r1 = stC.enter_context(nc.sync.register("r1"))
            p.raw("y", lambda e: e.reg_load(r0, tb_s[:1, :1]),
                  waits=[d_tb], inc=False)

            G5 = PH23[-1]
            oat_free = [None]
            xot_free = [None]
            xmid_free = [None]
            big5 = [None, None]
            ch5 = [None, None]
            h2_free = [None]
            h2i_free = [None]
            h2iT_free = [None]
            gu_free = [[None] * 4]
            guiT_free = [None]
            z_free = [None]
            r32_free = [None]
            r64_free = [None]
            cols_free = [[None] * 4]
            bn2_last = [None]
            outt_free = [None]
            wg_free = [None, None]
            wu_free = [None, None]

            for j in (range(NOT) if variant >= 4 else []):
                fvar = 99 if variant == 4 else (variant - 40 if variant > 40 else 99)
                to = j * TW

                def load_oat(e, to=to):
                    e.reg_mov(r1, to)
                    e.reg_add(r1, r1, r0)
                    in_ap = bass.AP(coll_out.tensor, r1,
                                    [[N, 128], [512 * N, 2], [128 * N, 4], [1, TW]])
                    return e.dma_start(
                        out=oat.rearrange("p s2 t -> p s2 t") if False else sc8k[:].rearrange("p (s f t) -> p s f t", s=2, f=4),
                        in_=in_ap)
                dol = p.dma_reg("yd", "oat", load_oat,
                                waits=W(COLL, GATE5, oat_free[0]))
                dxo = p.dma("yd", "xot", xot[:],
                            x_own[to:to + TW, :].rearrange("(c p) m -> p c m", p=128),
                            waits=W(COLL, GATE5, xot_free[0]))

                xm_toks = []
                e = None
                m = None
                for tc in range(CPT):
                    for nn in range(2):
                        bk = nn
                        m = None
                        for kc in range(8):
                            m = MM(bk, slice(0, 512),
                                   oat[:, kc, tc * 128:(tc + 1) * 128],
                                   Wo[:, kc, nn * 512:(nn + 1) * 512],
                                   start=(kc == 0), stop=(kc == 7),
                                   waits=W(dol, dxo, t_wo, big5[bk], *PH23)
                                   if kc == 0 else ())
                        e = p.do("v", "tensor_tensor",
                                 xmid[:, tc, nn * 512:(nn + 1) * 512],
                                 pb[bk][:, 0:512],
                                 xot[:, tc, nn * 512:(nn + 1) * 512], ALU.add,
                                 waits=W(m, xmid_free[0]
                                         if (tc == 0 and nn == 0) else None))
                        big5[bk] = e
                    xm_toks.append(e)
                xmid_free[0] = None
                oat_free[0] = m
                guiT_free[0] = m
                xot_free[0] = e
                if fvar < 2:
                    continue

                h2it_toks = []
                for tc in range(CPT):
                    cl = cols2[:, tc, :]
                    b1 = p.do("v", "bn_stats", bns2[:, 0:6], xmid[:, tc, 0:512],
                              waits=W(xm_toks[tc], bn2_last[0]))
                    b1 = p.do("v", "bn_stats", bns2[:, 6:12], xmid[:, tc, 512:1024],
                              waits=[b1])
                    b2 = p.do("v", "bn_aggr", cl[:, 0:2], bns2[:],
                              waits=W(b1, cols_free[0][tc]))
                    cols_free[0][tc] = None
                    bn2_last[0] = b2
                    s1 = p.do("s", "activation", cl[:, 2:3], cl[:, 1:2], AF.Sqrt,
                              bias=eps_c[:, 0:1], waits=W(b2, EPSC))
                    v1 = p.do("v", "reciprocal", cl[:, 3:4], cl[:, 2:3], waits=[s1])
                    v2 = p.do("v", "scalar_tensor_tensor", cl[:, 4:5], cl[:, 0:1],
                              -1.0, cl[:, 3:4], ALU.mult, ALU.mult,
                              waits=[v1, b2])
                    xh = p.do("v", "tensor_scalar", h2t[:], xmid[:, tc, :],
                              cl[:, 3:4], cl[:, 4:5], ALU.mult, ALU.add,
                              waits=W(v2, h2_free[0]))
                    h2a = p.do("v", "tensor_tensor", h2t[:], h2t[:], g2bc[:],
                               ALU.mult, waits=W(xh, t_g2bc))
                    h2a = p.do("v", "tensor_tensor", h2t[:], h2t[:], s2bc[:],
                               ALU.add, waits=W(h2a, t_s2bc))
                    am = p.do("v", "tensor_reduce", cl[:, 5:6], h2t[:], AX,
                              ALU.max, apply_absolute_value=True, waits=[h2a])
                    am = p.do("v", "tensor_scalar", cl[:, 5:6], cl[:, 5:6], 1e-5,
                              None, ALU.max, waits=[am])
                    sc = p.do("v", "reciprocal", cl[:, 6:7], cl[:, 5:6],
                              waits=[am])
                    sc = p.do("v", "tensor_scalar", cl[:, 6:7], cl[:, 6:7], 127.0,
                              None, ALU.mult, waits=[sc])
                    hi = None
                    for s2_ in range(2):
                        ss = slice(s2_ * 512, (s2_ + 1) * 512)
                        ra = p.do("v", "tensor_scalar", r32t[:], h2t[:, ss],
                                  cl[:, 6:7], C32, ALU.mult, ALU.add,
                                  waits=W(sc, r32_free[0]))
                        rb = p.do("v", "tensor_scalar", r64t[:], h2t[:, ss],
                                  cl[:, 6:7], C64, ALU.mult, ALU.add,
                                  waits=W(r64_free[0]))
                        td = p.do("v", "tensor_scalar", r64t[:], r64t[:],
                                  gdisc[:, 1:2], gdisc[:, 2:3], ALU.mult,
                                  ALU.add, waits=W(rb, GDISC))
                        hi = p.do("v", "scalar_tensor_tensor", h2i[:, ss],
                                  r32t[:], gdisc[:, 0:1], r64t[:], ALU.mult,
                                  ALU.add, waits=W(td, h2i_free[0]
                                                   if s2_ == 0 else None))
                        r32_free[0] = hi
                        r64_free[0] = hi
                    h2i_free[0] = None
                    h2_free[0] = hi
                    m = None
                    for kc in range(8):
                        m = TRb(2, slice(kc * 128, (kc + 1) * 128),
                                h2i[:, kc * 128:(kc + 1) * 128],
                                waits=W(hi, ch5[0]) if kc == 0 else ())
                    e = p.do("s", "copy",
                             h2iT[:, tc, :, :].rearrange("p c m -> p (c m)"),
                             pb[2].bitcast(BF16)[:, 0:1024],
                             waits=W(m, h2iT_free[0] if tc == 0 else None))
                    ch5[0] = e
                    h2i_free[0] = e
                    h2it_toks.append(e)
                h2iT_free[0] = None
                if fvar < 3:
                    continue

                dq_toks = []
                for tc in range(CPT):
                    cl = cols2[:, tc, :]
                    dg = p.do("v", "reciprocal", cl[:, 7:8], cl[:, 6:7],
                              waits=[h2it_toks[tc]])
                    dg = p.do("v", "tensor_tensor", cl[:, 8:9], cl[:, 7:8],
                              wsc[:, 3:4], ALU.mult, waits=W(dg, *FFN_RDY[:1]))
                    dq_toks.append(dg)

                gu_toks = [None] * 4
                mg = None
                mu_ = None
                for sl in range(8):
                    spr = sl % 2
                    dwg = p.dma("sd", f"wg{spr}", wgst[spr][:],
                                wg8_d[:, :, sl * 512:(sl + 1) * 512],
                                waits=W(wg_free[spr], *FFN_RDY))
                    dwu = p.dma("sd", "wuq", wust[spr][:],
                                wu8_d[:, :, sl * 512:(sl + 1) * 512],
                                waits=W(wu_free[0]))
                    for tc in range(CPT):
                        mg = None
                        for kc in range(8):
                            mg = MM(0, slice(0, 512), h2iT[:, tc, kc, :],
                                    wgst[spr][:, kc, :], start=(kc == 0),
                                    stop=(kc == 7),
                                    waits=W(h2it_toks[tc], dwg, big5[0])
                                    if kc == 0 else ())
                        sg_ = p.do("s", "activation", silg[:], pb[0][:, 0:512],
                                   AF.Silu, scale=cols2[:, tc, 8:9],
                                   waits=W(mg, dq_toks[tc], z_free[0]))
                        big5[0] = sg_
                        mu_ = None
                        for kc in range(8):
                            mu_ = MM(1, slice(0, 512), h2iT[:, tc, kc, :],
                                     wust[spr][:, kc, :], start=(kc == 0),
                                     stop=(kc == 7),
                                     waits=W(dwu, big5[1]) if kc == 0 else ())
                        gu_ = p.do("v", "tensor_tensor",
                                   gub[tc][:, sl * 512:(sl + 1) * 512], silg[:],
                                   pb[1][:, 0:512], ALU.mult,
                                   waits=W(mu_, sg_, gu_free[0][tc]
                                           if sl == 0 else None))
                        if sl == 0:
                            gu_free[0][tc] = None
                        big5[1] = gu_
                        z_free[0] = gu_
                        gu_toks[tc] = gu_
                    wg_free[spr] = mg
                    wu_free[0] = mu_
                if fvar < 4:
                    continue

                for tc in range(CPT):
                    cl = cols2[:, tc, :]
                    ag = p.do("v", "tensor_reduce", cl[:, 9:10], gub[tc][:], AX,
                              ALU.max, apply_absolute_value=True,
                              waits=W(*gu_toks))
                    th = p.do("v", "reciprocal", cl[:, 10:11], wsc[:, 4:5],
                              waits=W(abs_done.get(1)))
                    th = p.do("v", "tensor_tensor", cl[:, 10:11], cl[:, 10:11],
                              cl[:, 6:7], ALU.mult, waits=[th])
                    th = p.do("v", "tensor_scalar", cl[:, 10:11], cl[:, 10:11],
                              1e-5, None, ALU.mult, waits=[th])
                    me = p.do("v", "tensor_tensor", cl[:, 11:12], cl[:, 9:10],
                              cl[:, 10:11], ALU.max, waits=[ag, th])
                    s3 = p.do("v", "reciprocal", cl[:, 12:13], cl[:, 11:12],
                              waits=[me])
                    s3 = p.do("v", "tensor_scalar", cl[:, 12:13], cl[:, 12:13],
                              127.0, None, ALU.mult, waits=[s3])
                    dd_ = p.do("v", "tensor_tensor", cl[:, 13:14], cl[:, 11:12],
                               cl[:, 7:8], ALU.mult, waits=[me])
                    dd_ = p.do("v", "tensor_tensor", cl[:, 13:14], cl[:, 13:14],
                               wsc[:, 6:7], ALU.mult, waits=W(dd_))
                    cols_free[0][tc] = dd_
                    gl = None
                    for s2_ in range(8):
                        ss = slice(s2_ * 512, (s2_ + 1) * 512)
                        ra = p.do("v", "tensor_scalar", r32t[:], gub[tc][:, ss],
                                  cl[:, 12:13], C32, ALU.mult, ALU.add,
                                  waits=W(s3, r32_free[0]))
                        rb = p.do("v", "tensor_scalar", r64t[:], gub[tc][:, ss],
                                  cl[:, 12:13], C64, ALU.mult, ALU.add,
                                  waits=W(r64_free[0]))
                        td = p.do("v", "tensor_scalar", r64t[:], r64t[:],
                                  gdisc[:, 1:2], gdisc[:, 2:3], ALU.mult,
                                  ALU.add, waits=[rb])
                        gl = p.do("v", "scalar_tensor_tensor", gub[tc][:, ss],
                                  r32t[:], gdisc[:, 0:1], r64t[:], ALU.mult,
                                  ALU.add, waits=[td])
                        r32_free[0] = gl
                        r64_free[0] = gl
                    glast = None
                    for gq in range(4):
                        bq = 2 + (gq % 2)
                        m = None
                        for kc in range(8):
                            m = TRb(bq, slice(kc * 128, (kc + 1) * 128),
                                    gub[tc][:, (gq * 8 + kc) * 128:(gq * 8 + kc + 1) * 128],
                                    waits=W(gl, ch5[bq - 2]) if kc == 0 else ())
                        glast = p.do("s", "copy",
                                     guiT[:, gq * 8:(gq + 1) * 8, :].rearrange("p c m -> p (c m)"),
                                     pb[bq].bitcast(BF16)[:, 0:1024],
                                     waits=W(m, guiT_free[0] if gq == 0 else None))
                        ch5[bq - 2] = glast
                    guiT_free[0] = None
                    gu_free[0][tc] = glast
                    if fvar < 5:
                        oat_free[0] = glast
                        continue
                    o2 = None
                    mlast = None
                    for nn in range(2):
                        bk = nn
                        m = None
                        for kc in range(32):
                            m = MM(bk, slice(0, 512), guiT[:, kc, :],
                                   Wd8[:, kc, nn * 512:(nn + 1) * 512],
                                   start=(kc == 0), stop=(kc == 31),
                                   waits=W(glast, big5[bk], *FFN_RDY)
                                   if kc == 0 else ())
                        mlast = m
                        o1 = p.do("v", "scalar_tensor_tensor",
                                  outt[:, nn * 512:(nn + 1) * 512],
                                  pb[bk][:, 0:512], cl[:, 13:14],
                                  gbbc[:, nn * 512:(nn + 1) * 512], ALU.mult,
                                  ALU.mult,
                                  waits=W(m, dd_, t_gbbc, outt_free[0]
                                          if nn == 0 else None))
                        o2 = p.do("v", "tensor_tensor",
                                  outt[:, nn * 512:(nn + 1) * 512],
                                  outt[:, nn * 512:(nn + 1) * 512],
                                  xmid[:, tc, nn * 512:(nn + 1) * 512], ALU.add,
                                  waits=[o1])
                        big5[bk] = o2
                    guiT_free[0] = mlast
                    oat_free[0] = mlast
                    dout = p.dma("yd", "outq", out_d[to + tc * 128:to + (tc + 1) * 128, :],
                                 outt[:], waits=[o2])
                    outt_free[0] = dout
                xmid_free[0] = bn2_last[0]

            if variant < 4:
                p.dma("yd", "outq", out_d[0:128, :],
                      outt[:], waits=[GATE5])
            dstreams = [k for k in p.cnt if k not in ENGS]
            p.fence("y", [(k, p.cnt[k]) for k in dstreams])
            p.fence("y", [("t", p.cnt["t"]), ("v", p.cnt["v"]),
                          ("s", p.cnt["s"]), ("g", p.cnt["g"])])

            with nc.Block() as block:
                p.emit(block)
    return nc


_NC_CACHE = {}


def get_nc(variant=4):
    if variant not in _NC_CACHE:
        _NC_CACHE[variant] = build_program(variant)
    return _NC_CACHE[variant]


def make_in_maps(inputs):
    x = np.asarray(inputs["L0_attn_x"], dtype=np.float32)
    Wq_ = np.asarray(inputs["L0_attn_Wq"], dtype=np.float32)
    Wk_ = np.asarray(inputs["L0_attn_Wk"], dtype=np.float32)
    Wv_ = np.asarray(inputs["L0_attn_Wv"], dtype=np.float32)
    Wbeta = np.asarray(inputs["L0_attn_Wbeta"], dtype=np.float32)
    Wo_ = np.asarray(inputs["L0_attn_Wo"], dtype=np.float32)
    g_norm = np.asarray(inputs["L0_attn_g_norm"], dtype=np.float32)
    w_gate = np.asarray(inputs["L0_ffn_w_gate"], dtype=np.float32)
    w_up = np.asarray(inputs["L0_ffn_w_up"], dtype=np.float32)
    w_down = np.asarray(inputs["L0_ffn_w_down"], dtype=np.float32)
    c_ = np.asarray(inputs["L0_c"], dtype=np.float32)
    wmod = np.asarray(inputs["L0_wmod"], dtype=np.float32)

    wgT = np.ascontiguousarray(w_gate.T)
    wuT = np.ascontiguousarray(w_up.T)
    wdT = np.ascontiguousarray(w_down.T)
    gnt = np.ascontiguousarray(np.tile(g_norm, 4).reshape(8, 128).T)
    in_maps = []
    for cidx in range(8):
        b = cidx // 2
        hp = cidx % 2
        xb = x[b]
        in_maps.append(dict(
            x_tm=np.ascontiguousarray(xb),
            x_own=np.ascontiguousarray(xb[hp * NOWN:(hp + 1) * NOWN]),
            xT=np.ascontiguousarray(xb.T),
            wq=np.ascontiguousarray(Wq_[:, hp * 512:(hp + 1) * 512]),
            wk=np.ascontiguousarray(Wk_[:, hp * 512:(hp + 1) * 512]),
            wv=np.ascontiguousarray(Wv_[:, hp * 512:(hp + 1) * 512]),
            wb=np.ascontiguousarray(Wbeta[:, hp * 2:hp * 2 + 2]),
            wo=Wo_, wgT=wgT, wuT=wuT, wdT=wdT, wmod=wmod,
            cvec=np.ascontiguousarray(c_[b].reshape(8, 128).T),
            gnorm=gnt,
            tok_base=np.array([[hp * NOWN]], np.int32),
        ))
    return in_maps


import os
def kernel(**inputs):
    nc = get_nc(int(os.environ.get("KVARIANT", "4")))
    in_maps = make_in_maps(inputs)
    res = run_bass_kernel_spmd(nc, in_maps, core_ids=list(range(8)))
    out = np.empty((B, N, D), np.float32)
    for cidx in range(8):
        b, hp = cidx // 2, cidx % 2
        out[b, hp * NOWN:(hp + 1) * NOWN] = res.results[cidx]["out"]
    return out
